# version 07
# baseline (speedup 1.0000x reference)
import sys
import numpy as np

sys.path.insert(0, '/opt/trn_rl_repo')

import concourse.bass as bass
import concourse.mybir as mybir
from concourse.ap import AP
from concourse.bass_utils import run_bass_kernel_spmd
from contextlib import ExitStack

F32 = mybir.dt.float32
BF16 = mybir.dt.bfloat16
AF = mybir.ActivationFunctionType
ALU = mybir.AluOpType
AX = mybir.AxisListType

D = 2048
T = 2048
H = 32
HD = 64
DH = 8192
NCH = 16
NHC = DH // 128
NTC = T // 128
NTQ = T // 512
NEG = -1.0e9


def _perm_q():
    perm = np.zeros(2048, np.int64)
    for c in range(8):
        for j in range(4):
            h = 4 * (j + 4 * (1 if c >= 4 else 0)) + (c % 4)
            for i in range(32):
                perm[c * 128 + 32 * j + i] = h * 64 + 2 * i
                perm[(c + 8) * 128 + 32 * j + i] = h * 64 + 2 * i + 1
    return perm


def _perm_k():
    perm = np.zeros(512, np.int64)
    for c in range(2):
        for j in range(4):
            g = 4 * c + j
            for i in range(32):
                perm[c * 128 + 32 * j + i] = g * 64 + 2 * i
                perm[(c + 2) * 128 + 32 * j + i] = g * 64 + 2 * i + 1
    return perm


def _rope_tables():
    theta = 1.0 / (10000.0 ** (np.arange(0, HD, 2, dtype=np.float64) / HD))
    t = np.arange(T, dtype=np.float64)
    fr = theta[:, None] * t[None, :]
    cos = np.cos(fr).astype(np.float32)
    sin = np.sin(fr).astype(np.float32)
    return np.tile(cos, (4, 1)), np.tile(sin, (4, 1))



class _Dry:
    def then_inc(self, *a, **k):
        return self

    def __getattr__(self, name):
        return lambda *a, **k: self

_DRY = _Dry()


class B:

    def __init__(self, nc, es):
        self.nc = nc
        self.es = es
        self.sems = {}
        self.cnt = {}
        self.marks = {}
        self.want = []
        self.emitting = False

    def sem(self, name):
        self.sems[name] = self.es.enter_context(self.nc.semaphore(name))
        self.cnt[name] = 0

    def inc(self, instr, s, k=1, label=None):
        self.cnt[s] += k
        if self.emitting and instr is not _DRY and k > 0:
            instr.then_inc(self.sems[s], k)
        if label is not None:
            if self.emitting:
                assert self.marks[label] == (s, self.cnt[s]), (label, self.marks.get(label), (s, self.cnt[s]))
            else:
                assert label not in self.marks, label
                self.marks[label] = (s, self.cnt[s])
        return instr

    def mark(self, s, label):
        return self.inc(_DRY, s, 0, label)

    def wait(self, eng, label):
        if not self.emitting:
            self.want.append(label)
            return
        s, v = self.marks[label]
        eng.wait_ge(self.sems[s], v)

    def wait_maybe(self, eng, label):
        if not self.emitting:
            return
        if label in self.marks:
            s, v = self.marks[label]
            eng.wait_ge(self.sems[s], v)

    def phase(self, name, emitters):
        snap = dict(self.cnt)
        self.emitting = False
        self.want = []
        for en, fn in emitters:
            fn(_DRY)
        missing = [w for w in self.want if w not in self.marks]
        assert not missing, ('unresolved waits', missing[:5])
        self.cnt = snap
        self.emitting = True
        with self.nc.Block(name) as blk:
            for en, fn in emitters:
                getattr(blk, en)(fn)



def build_program(tap=None):
    nc = bass.Bass('TRN2', target_bir_lowering=False, detect_race_conditions=False)

    xf = nc.dram_tensor('xf', [D, T], F32, kind='ExternalInput')
    wq = nc.dram_tensor('wq', [D, D], F32, kind='ExternalInput')
    wk = nc.dram_tensor('wk', [D, 512], F32, kind='ExternalInput')
    wv = nc.dram_tensor('wv', [D, 512], F32, kind='ExternalInput')
    wo = nc.dram_tensor('wo', [D, D], F32, kind='ExternalInput')
    wg = nc.dram_tensor('wg', [D, DH], F32, kind='ExternalInput')
    wu = nc.dram_tensor('wu', [D, DH], F32, kind='ExternalInput')
    wd = nc.dram_tensor('wd', [DH, D], F32, kind='ExternalInput')
    cos_in = nc.dram_tensor('cos_in', [128, T], F32, kind='ExternalInput')
    sin_in = nc.dram_tensor('sin_in', [128, T], F32, kind='ExternalInput')
    ident_in = nc.dram_tensor('ident_in', [128, 128], F32, kind='ExternalInput')
    diag_in = nc.dram_tensor('diag_in', [128, 128], F32, kind='ExternalInput')
    onesc_in = nc.dram_tensor('onesc_in', [128, 1], F32, kind='ExternalInput')
    onesf_in = nc.dram_tensor('onesf_in', [128, 128], F32, kind='ExternalInput')

    x2h = nc.dram_tensor('x2h', [D, T], F32, kind='Internal')
    gth = nc.dram_tensor('gth', [DH, T], BF16, kind='Internal')

    def fm_pair(dram, sbuf_ap, nch, w):
        return (AP(dram, 0, [[w * 1, 128] if False else [dram.shape[1], 128],
                             [128 * dram.shape[1], nch], [1, w]]),
                sbuf_ap.rearrange('p (c t) -> p c t', c=nch))

    taps = {}
    out = None
    if tap is None:
        out = nc.dram_tensor('outf', [D, T], F32, kind='ExternalOutput')

    es = ExitStack()
    sb = lambda n, s, d: es.enter_context(nc.sbuf_tensor(n, s, d))

    BH = sb('BH', [128, NCH * T], BF16)
    BQ = sb('BQ', [128, NCH * T], BF16)
    BK = sb('BK', [128, 4 * T], BF16)
    BV = sb('BV', [128, NTC * 512], BF16)
    cosT = sb('cosT', [128, T], F32)
    sinT = sb('sinT', [128, T], F32)
    identb = sb('identb', [128, 128], BF16)
    diagm = sb('diagm', [128, 128], F32)
    onesc = sb('onesc', [128, 1], BF16)
    onesf = sb('onesf', [128, 128], F32)
    identf = sb('identf', [128, 128], F32)
    STT = sb('STT', [128, 160], F32)
    SCR = sb('SCR', [128, 5120], F32)
    SCRb = SCR.bitcast(BF16)
    BKf = BK.bitcast(F32)
    BVf = BV.bitcast(F32)
    cosTb = cosT.bitcast(BF16)
    sinTb = sinT.bitcast(BF16)
    mt = [sb('m%d' % i, [128, 1], F32) for i in range(2)]
    nmt = [sb('nm%d' % i, [128, 1], F32) for i in range(2)]
    lt = [sb('l%d' % i, [128, 1], F32) for i in range(2)]
    rt = [sb('r%d' % i, [128, 1], F32) for i in range(2)]

    RP = [4096, 4608]
    WSLOT = [0, 2048, 4096, 6144]
    PHB = [0, 2048]
    PTB = [0, 2048]
    GSL = [0, 2048]
    SIL = [4096, 4608]
    SQ1 = [0, 512]
    DGP = [8192, 8320]
    XS = [0, 2048]
    PBf = [0, 2048]

    psA = es.enter_context(nc.psum_tensor('psA', [128, 2048], F32))
    psB = es.enter_context(nc.psum_tensor('psB', [128, 2048], F32))
    psAb = psA.bitcast(BF16)
    psBb = psB.bitcast(BF16)

    b = B(nc, es)
    for s in ('sG', 'sS', 'sT', 'sV', 'sC'):
        b.sem(s)

    def R(i, lo=0, hi=16):
        return STT[:, i * 16 + lo:i * 16 + hi]

    def step(eng, sem, instr):
        b.inc(instr, sem, 1)
        if b.emitting:
            eng.wait_ge(b.sems[sem], b.cnt[sem])
        return instr

    def ln_finish_vec(v, ph):
        step(v, 'sV', v.tensor_scalar(R(2), R(0), 1.0 / D, None, ALU.mult))
        step(v, 'sV', v.tensor_scalar(R(3), R(1), 1.0 / D, None, ALU.mult))
        step(v, 'sV', v.tensor_tensor(R(4), R(2), R(2), ALU.mult))
        step(v, 'sV', v.tensor_tensor(R(4), R(3), R(4), ALU.subtract))
        step(v, 'sV', v.tensor_scalar(R(4), R(4), 1e-5, None, ALU.add))
        b.inc(v.reciprocal(R(5), R(4)), 'sV', 1, ph + '_rec')
        b.wait(v, ph + '_sqrt')
        step(v, 'sV', v.tensor_tensor(R(5), R(6), R(6), ALU.mult))
        step(v, 'sV', v.tensor_tensor(R(5), R(5), R(4), ALU.mult))
        step(v, 'sV', v.tensor_scalar(R(5), R(5), -1.0, 3.0, ALU.mult, ALU.add))
        step(v, 'sV', v.tensor_tensor(R(5), R(6), R(5), ALU.mult))
        step(v, 'sV', v.tensor_scalar(R(7), R(5), 0.5, None, ALU.mult))
        step(v, 'sV', v.tensor_tensor(R(8), R(2), R(7), ALU.mult))
        b.inc(v.tensor_scalar(R(8), R(8), -1.0, None, ALU.mult), 'sV', 1, ph + '_ab')
        if b.emitting:
            v.wait_ge(b.sems['sV'], b.cnt['sV'])

    def ln_finish_sca(s, ph):
        b.wait(s, ph + '_rec')
        b.inc(s.activation(R(6), R(5), AF.Sqrt), 'sC', 1, ph + '_sqrt')

    DGB = [RP[0], RP[0] + 128]

    def bcast_vec(v, ph):
        b.wait(v, 'c2_done')
        b.wait(v, ph + '_ab')
        for idx in range(32):
            tc, isb = idx // 2, idx % 2
            if idx >= 2:
                b.wait(v, (ph + '_bcm', idx - 2))
            col = R(8 if isb else 7, tc, tc + 1)
            b.inc(v.tensor_scalar(SCR[:, DGB[idx % 2]:DGB[idx % 2] + 128],
                                  identf[:, :], col, None, ALU.mult),
                  'sV', 1, (ph + '_dg', idx))

    def bcast_tensor(t, ph):
        for idx in range(32):
            tc, isb = idx // 2, idx % 2
            b.wait(t, (ph + '_dg', idx))
            ps = psB if isb else psA
            ins = t.matmul(ps[:, tc * 128:(tc + 1) * 128], onesf[:, :],
                           SCR[:, DGB[idx % 2]:DGB[idx % 2] + 128],
                           start=True, stop=True)
            b.inc(ins, 'sT', 1, (ph + '_bcm', idx))
        b.mark('sT', ph + '_bc')

    def stats_mms(t, ph, src_of, sq_of, sqwait):
        for wave in range(4):
            if wave >= 1:
                b.wait(t, (ph + '_evw', wave - 1))
            for c in range(NCH):
                b.wait(t, (ph + '_st_x', c) if ph == 'ln1' else ('x2bf', c))
                b.wait(t, sqwait(wave, c))
                for tbl in range(4):
                    tb = wave * 4 + tbl
                    t.matmul(psA[:, tbl * 512 + wave:tbl * 512 + wave + 1],
                             src_of(c)[:, tb * 128:(tb + 1) * 128],
                             onesc[:, :], start=(c == 0), stop=(c == NCH - 1))
                    ins = t.matmul(psB[:, tbl * 512 + wave:tbl * 512 + wave + 1],
                                   sq_of(c)[:, tbl * 128:(tbl + 1) * 128],
                                   onesc[:, :], start=(c == 0), stop=(c == NCH - 1))
                b.inc(ins, 'sT', 1, (ph + '_sw', wave, c))
            b.inc(_DRY, 'sT', 0, (ph + '_wave', wave))

    def stats_evac(v, ph):
        for wave in range(4):
            b.wait(v, (ph + '_wave', wave))
            b.inc(v.tensor_copy(R(0, wave * 4, wave * 4 + 4),
                                AP(psA, wave, [[2048, 128], [512, 4]])), 'sV', 1)
            b.inc(v.tensor_copy(R(1, wave * 4, wave * 4 + 4),
                                AP(psB, wave, [[2048, 128], [512, 4]])),
                  'sV', 1, (ph + '_evw', wave))
        if b.emitting:
            v.wait_ge(b.sems['sV'], b.cnt['sV'])
        b.mark('sV', ph + '_ev')

    def normalize(v, ph, src_off, dst, c):
        for tq in range(NTQ):
            sl = slice(tq * 512, (tq + 1) * 512)
            step(v, 'sV', v.tensor_tensor(SCR[:, RP[0]:RP[0] + 512],
                                          BKf[:, src_off + tq * 512:src_off + (tq + 1) * 512],
                                          psA[:, sl], ALU.mult))
            ins = v.tensor_tensor(dst[:, c * T + tq * 512:c * T + (tq + 1) * 512],
                                  SCR[:, RP[0]:RP[0] + 512], psB[:, sl], ALU.add)
            if tq < NTQ - 1:
                step(v, 'sV', ins)
            else:
                b.inc(ins, 'sV', 1)
                if b.emitting:
                    v.wait_ge(b.sems['sV'], b.cnt['sV'])
        return ins

    def p1_gpsimd(g):
        b.inc(g.dma_start(out=identb[:, :], in_=ident_in[:, :]), 'sG', 16)
        b.inc(g.dma_start(out=onesc[:, :], in_=onesc_in[:, :]), 'sG', 16, 'c_done')
        for c in range(NCH):
            b.inc(g.dma_start(out=BQ[:, c * T:(c + 1) * T],
                              in_=xf[c * 128:(c + 1) * 128, :]), 'sG', 16, ('xbf', c))

    def p1_sync(s):
        b.inc(s.dma_start(out=cosT[:, :], in_=cos_in[:, :]), 'sS', 16)
        b.inc(s.dma_start(out=sinT[:, :], in_=sin_in[:, :]), 'sS', 16)
        b.inc(s.dma_start(out=diagm[:, :], in_=diag_in[:, :]), 'sS', 16)
        b.inc(s.dma_start(out=onesf[:, :], in_=onesf_in[:, :]), 'sS', 16)
        b.inc(s.dma_start(out=identf[:, :], in_=ident_in[:, :]), 'sS', 16, 'c2_done')
        for c in range(NCH):
            if c >= 2:
                b.wait(s, ('norm1', c - 2))
            b.inc(s.dma_start(out=BKf[:, XS[c % 2]:XS[c % 2] + 2048],
                              in_=xf[c * 128:(c + 1) * 128, :]), 'sS', 16, ('xin', c))

    def p1_scalar(s):
        b.wait(s, 'c_done')
        for wave in range(4):
            for c in range(NCH):
                b.wait(s, ('xbf', c))
                idx = wave * NCH + c
                if idx >= 2:
                    pidx = idx - 2
                    b.wait(s, ('ln1_sw', pidx // NCH, pidx % NCH))
                b.inc(s.activation(SCRb[:, SQ1[c % 2]:SQ1[c % 2] + 512],
                                   BQ[:, c * T + wave * 512:c * T + wave * 512 + 512],
                                   AF.Square), 'sC', 1, ('sq1', wave, c))
        ln_finish_sca(s, 'ln1')

    def p1_tensor(t):
        for c in range(NCH):
            b.wait(t, ('xbf', c))
            b.mark('sT', ('ln1_st_x', c))
        stats_mms(t, 'ln1', lambda c: BQ[:, c * T:(c + 1) * T],
                  lambda c: SCRb[:, SQ1[c % 2]:SQ1[c % 2] + 512],
                  lambda wave, c: ('sq1', wave, c))
        bcast_tensor(t, 'ln1')

    def p1_vector(v):
        stats_evac(v, 'ln1')
        ln_finish_vec(v, 'ln1')
        bcast_vec(v, 'ln1')
        b.wait(v, 'ln1_bc')
        for c in range(NCH):
            b.wait(v, ('xin', c))
            normalize(v, 'ln1', XS[c % 2], BH, c)
            b.mark('sV', ('norm1', c))

    b.phase('P1', [('gpsimd', p1_gpsimd), ('sync', p1_sync), ('scalar', p1_scalar),
                   ('tensor', p1_tensor), ('vector', p1_vector)])

    if tap == 'h':
        taps['h'] = nc.dram_tensor('tap_h', [D, T], BF16, kind='ExternalOutput')
        taps['stt'] = nc.dram_tensor('tap_stt', [128, 160], F32, kind='ExternalOutput')

        def ts_(s):
            o_ap, i_ap = fm_pair(taps['h'], BH[:, :], NCH, T)
            b.inc(s.dma_start(out=o_ap, in_=i_ap), 'sS', 16, 'tap_st')
            b.inc(s.dma_start(out=taps['stt'][:, :], in_=STT[:, :]), 'sS', 16, 'tap_st2')
            b.wait(s, 'tap_st2')
        b.phase('TAP', [('sync', ts_)])
        es.close()
        return nc, taps

    def wblk_ap(w, ncols, cblk, blkw=128):
        rows = w.shape[0]
        return AP(w, cblk * blkw, [[ncols, 128], [128 * ncols, rows // 128], [1, blkw]])

    def wslot(i, n=2048):
        return SCRb[:, WSLOT[i]:WSLOT[i] + n]

    def wslot_r(i, k=16, n=2048):
        return SCRb[:, WSLOT[i]:WSLOT[i] + n].rearrange('p (k n) -> p k n', k=k)

    def p2_gpsimd(g):
        for p in range(8):
            if p >= 2:
                b.wait(g, ('qmm', p - 2, NTQ - 1))
            b.inc(g.dma_start(out=wslot_r(2 * (p % 2)), in_=wblk_ap(wq, D, p)),
                  'sG', 16, ('wqe', p))
            b.inc(g.dma_start(out=wslot_r(2 * (p % 2) + 1), in_=wblk_ap(wq, D, p + 8)),
                  'sG', 16, ('wqo', p))
        for c in range(4):
            b.inc(g.dma_start(out=BV[:, c * 2048:(c + 1) * 2048]
                              .rearrange('p (k n) -> p k n', k=16),
                              in_=wblk_ap(wk, 512, c)), 'sG', 16, ('wk', c))
        b.wait(g, ('qmm', 6, NTQ - 1))
        b.inc(g.dma_start(out=SCRb[:, WSLOT[0]:WSLOT[0] + 4096]
                          .rearrange('p (k n) -> p k n', k=16),
                          in_=wblk_ap(wv, 512, 0, 256)), 'sG', 16, ('wv', 0))
        b.wait(g, ('qmm', 7, NTQ - 1))
        b.inc(g.dma_start(out=SCRb[:, WSLOT[2]:WSLOT[2] + 4096]
                          .rearrange('p (k n) -> p k n', k=16),
                          in_=wblk_ap(wv, 512, 1, 256)), 'sG', 16, ('wv', 1))

    def p2_tensor(t):
        for p in range(8):
            b.wait(t, ('wqe', p))
            b.wait(t, ('wqo', p))
            for tq in range(NTQ):
                if p >= 1:
                    b.wait(t, ('rope_q', p - 1, tq))
                se, so = 2 * (p % 2), 2 * (p % 2) + 1
                for K in range(NCH):
                    t.matmul(psA[:, tq * 512:(tq + 1) * 512],
                             wslot(se)[:, K * 128:(K + 1) * 128],
                             BH[:, K * T + tq * 512:K * T + (tq + 1) * 512],
                             start=(K == 0), stop=(K == NCH - 1))
                for K in range(NCH):
                    ins = t.matmul(psB[:, tq * 512:(tq + 1) * 512],
                                   wslot(so)[:, K * 128:(K + 1) * 128],
                                   BH[:, K * T + tq * 512:K * T + (tq + 1) * 512],
                                   start=(K == 0), stop=(K == NCH - 1))
                b.inc(ins, 'sT', 1, ('qmm', p, tq))
        for p in range(2):
            b.wait(t, ('wk', p))
            b.wait(t, ('wk', p + 2))
            for tq in range(NTQ):
                b.wait(t, ('rope_q', 7, tq))
                if p == 1:
                    b.wait(t, ('rope_k', 0, tq))
                for K in range(NCH):
                    t.matmul(psA[:, tq * 512:(tq + 1) * 512],
                             BV[:, p * 2048 + K * 128:p * 2048 + (K + 1) * 128],
                             BH[:, K * T + tq * 512:K * T + (tq + 1) * 512],
                             start=(K == 0), stop=(K == NCH - 1))
                for K in range(NCH):
                    ins = t.matmul(psB[:, tq * 512:(tq + 1) * 512],
                                   BV[:, (p + 2) * 2048 + K * 128:(p + 2) * 2048 + (K + 1) * 128],
                                   BH[:, K * T + tq * 512:K * T + (tq + 1) * 512],
                                   start=(K == 0), stop=(K == NCH - 1))
                b.inc(ins, 'sT', 1, ('kmm', p, tq))
        for tc in range(NTC):
            for oh in range(2):
                b.wait(t, ('wv', oh))
                b.wait(t, ('rope_k', 1, NTQ - 1))
                if tc >= 2:
                    b.wait(t, ('vev', tc - 2, oh))
                slot = (2 * tc + oh) % 4
                woff = WSLOT[0] if oh == 0 else WSLOT[2]
                for K in range(NCH):
                    ins = t.matmul(psA[:, slot * 512:slot * 512 + 256],
                                   BH[:, K * T + tc * 128:K * T + (tc + 1) * 128],
                                   SCRb[:, woff + K * 256:woff + (K + 1) * 256],
                                   start=(K == 0), stop=(K == NCH - 1))
                b.inc(ins, 'sT', 1, ('vmm', tc, oh))

    def _rope(v, p, tq, is_q):
        sl = slice(tq * 512, (tq + 1) * 512)
        cs = cosT[:, tq * 512:(tq + 1) * 512]
        sn = sinT[:, tq * 512:(tq + 1) * 512]
        t1 = SCR[:, RP[0]:RP[0] + 512]
        t2 = SCR[:, RP[1]:RP[1] + 512]
        dst, off = (BQ, 8) if is_q else (BK, 2)
        de = dst[:, p * T + tq * 512:p * T + (tq + 1) * 512]
        do = dst[:, (p + off) * T + tq * 512:(p + off) * T + (tq + 1) * 512]
        b.inc(v.tensor_tensor(t1, psA[:, sl], cs, ALU.mult), 'sV', 1)
        step(v, 'sV', v.tensor_tensor(t2, psB[:, sl], sn, ALU.mult))
        step(v, 'sV', v.tensor_tensor(de, t1, t2, ALU.subtract))
        b.inc(v.tensor_tensor(t1, psA[:, sl], sn, ALU.mult), 'sV', 1)
        step(v, 'sV', v.tensor_tensor(t2, psB[:, sl], cs, ALU.mult))
        ins = v.tensor_tensor(do, t1, t2, ALU.add)
        b.inc(ins, 'sV', 1)
        if b.emitting:
            v.wait_ge(b.sems['sV'], b.cnt['sV'])
        return ins

    def p2_vector(v):
        for p in range(8):
            for tq in range(NTQ):
                b.wait(v, ('qmm', p, tq))
                _rope(v, p, tq, True)
                b.mark('sV', ('rope_q', p, tq))
        for p in range(2):
            for tq in range(NTQ):
                b.wait(v, ('kmm', p, tq))
                _rope(v, p, tq, False)
                b.mark('sV', ('rope_k', p, tq))
        for tc in range(NTC):
            for oh in range(2):
                b.wait(v, ('vmm', tc, oh))
                slot = (2 * tc + oh) % 4
                b.inc(v.tensor_copy(BV[:, tc * 512 + oh * 256:tc * 512 + oh * 256 + 256],
                                    psA[:, slot * 512:slot * 512 + 256]),
                      'sV', 1, ('vev', tc, oh))

    b.phase('P2', [('gpsimd', p2_gpsimd), ('tensor', p2_tensor), ('vector', p2_vector)])

    if tap == 'qkv':
        taps['q'] = nc.dram_tensor('tap_q', [D, T], BF16, kind='ExternalOutput')
        taps['k'] = nc.dram_tensor('tap_k', [512, T], BF16, kind='ExternalOutput')
        taps['v'] = nc.dram_tensor('tap_v', [T, 512], BF16, kind='ExternalOutput')

        def ts_(s):
            o_ap, i_ap = fm_pair(taps['q'], BQ[:, :], NCH, T)
            b.inc(s.dma_start(out=o_ap, in_=i_ap), 'sS', 16, 'tap_qst')
            o_ap, i_ap = fm_pair(taps['k'], BK[:, :], 4, T)
            b.inc(s.dma_start(out=o_ap, in_=i_ap), 'sS', 16, 'tap_kst')
            o_ap, i_ap = fm_pair(taps['v'], BV[:, :], NTC, 512)
            b.inc(s.dma_start(out=o_ap, in_=i_ap), 'sS', 16, 'tap_vst')
            b.wait(s, 'tap_vst')
        b.phase('TAP', [('sync', ts_)])
        es.close()
        return nc, taps

    LOGI = [(h, qt) for h in range(H) for qt in range(NTC)]
    SUBS = []
    LAST_SUB = {}
    for li, (h, qt) in enumerate(LOGI):
        S = (qt + 1) * 128
        nsub = (S + 1023) // 1024
        for c in range(nsub):
            W = min(1024, S - 1024 * c)
            SUBS.append((li, h, qt, c, W, c == 0, c == nsub - 1, nsub))
            LAST_SUB[li] = len(SUBS) - 1

    REGB = [(psA, 0), (psA, 1024), (psB, 0)]

    def reg(u, lo, hi):
        ps, off = REGB[u % 3]
        return ps[:, off + lo:off + hi]

    def pvt(li, h):
        off = 1024 + 128 * (li % 2)
        rows = slice(0, 64) if h % 2 == 0 else slice(64, 128)
        return psB[rows, off:off + 128]

    mt3 = [sb('mt3_%d' % i, [128, 1], F32) for i in range(3)]
    lt3 = [sb('lt3_%d' % i, [128, 1], F32) for i in range(3)]
    PHB3 = [0, 1024, 2048]
    PTB3 = [0, 1024, 2048]
    EVCUT = 512

    def p3_tensor(t):
        for u, (li, h, qt, c, W, first, last, nsub) in enumerate(SUBS):
            g = h // 4
            j = g % 4
            m = h % 4
            c_ev = m + 4 * (1 if g >= 4 else 0)
            if u >= 3:
                b.wait(t, ('ev', u - 3))
                b.wait(t, ('ev_v', u - 3))
            rows = slice(32 * j, 32 * j + 32)
            for st in range((W + 511) // 512):
                w = min(512, W - st * 512)
                so = (g // 4) * T + 1024 * c + st * 512
                t.matmul(reg(u, st * 512, st * 512 + w),
                         BQ[rows, c_ev * T + qt * 128:c_ev * T + (qt + 1) * 128],
                         BK[rows, so:so + w],
                         start=True, stop=False, tile_position=(32 * j, 0))
                ins = t.matmul(reg(u, st * 512, st * 512 + w),
                               BQ[rows, (c_ev + 8) * T + qt * 128:(c_ev + 8) * T + (qt + 1) * 128],
                               BK[rows, (g // 4 + 2) * T + 1024 * c + st * 512:
                                  (g // 4 + 2) * T + 1024 * c + st * 512 + w],
                               start=False, stop=True, tile_position=(32 * j, 0))
            b.inc(ins, 'sT', 1, ('sc', u))
            if u >= 1:
                emit_tail(t, u - 1)
        emit_tail(t, len(SUBS) - 1)

    def emit_tail(t, u):
        li, h, qt, c, W, first, last, nsub = SUBS[u]
        par = u % 3
        lp = li % 2
        b.wait(t, ('dg', li))
        for sj in range(W // 128):
            ins = t.matmul(reg(u, sj * 128, (sj + 1) * 128),
                           cosTb[:, PHB3[par] + sj * 128:PHB3[par] + (sj + 1) * 128],
                           SCRb[:, DGP[lp]:DGP[lp] + 128],
                           start=True, stop=True)
        b.inc(ins, 'sT', 1, ('tp', u))
        b.wait(t, ('ev', u))
        b.wait(t, ('ev_v', u))
        if first and li >= 1:
            b.wait(t, ('ye', li - 1))
        gg = h // 4
        loc = pvt(li, h)
        tpos = (0, 0) if h % 2 == 0 else (0, 64)
        for sj in range(W // 128):
            sg = c * 8 + sj
            ins = t.matmul(loc, BV[:, sg * 512 + gg * 64:sg * 512 + gg * 64 + 64],
                           sinTb[:, PTB3[par] + sj * 128:PTB3[par] + (sj + 1) * 128],
                           start=(first and sj == 0), stop=(last and sj == W // 128 - 1),
                           tile_position=tpos)
        b.inc(ins, 'sT', 1, ('pv', u))
        if last:
            b.mark('sT', ('pvl', li))

    def p3_vector(v):
        for u, (li, h, qt, c, W, first, last, nsub) in enumerate(SUBS):
            par = u % 3
            lp = li % 2
            if u >= 1:
                pli, ph_, pqt, pc, pW, pfirst, plast, pnsub = SUBS[u - 1]
                if plast:
                    b.wait(v, ('exp', u - 1))
                    if pnsub == 2:
                        b.wait(v, ('exp', u - 2))
                        step(v, 'sV', v.tensor_tensor(lt3[(u - 1) % 3][:, :], lt3[(u - 2) % 3][:, :],
                                                      lt3[(u - 1) % 3][:, :], ALU.add))
                    if pli >= 4:
                        b.wait(v, ('tp', LAST_SUB[pli - 2]))
                    b.inc(v.reciprocal(rt[pli % 2][:, :], lt3[(u - 1) % 3][:, :]), 'sV', 1, ('rc', pli))
                    b.wait(v, ('rc', pli))
                    b.inc(v.tensor_scalar(SCRb[:, DGP[pli % 2]:DGP[pli % 2] + 128],
                                          identb[:, :], rt[pli % 2][:, :], None, ALU.mult),
                          'sV', 1, ('dg', pli))
            if u >= 2:
                b.wait(v, ('tp', u - 2))
                if u >= 5:
                    b.wait(v, ('pv', u - 5))
                b.inc(v.tensor_copy(sinTb[:, PTB3[(u - 2) % 3]:PTB3[(u - 2) % 3] + EVCUT],
                                    reg(u - 2, 0, EVCUT)), 'sV', 1, ('ev_v', u - 2))
            b.wait(v, ('sc', u))
            if last:
                step(v, 'sV', v.tensor_tensor(reg(u, W - 128, W), reg(u, W - 128, W),
                                              diagm[:, :], ALU.add))
            b.inc(v.reduce_max(mt3[par][:, :], reg(u, 0, W), axis=AX.X), 'sV', 1, ('red', u))
            if last:
                b.wait(v, ('red', u))
                if nsub == 2:
                    b.wait(v, ('red', u - 1))
                    step(v, 'sV', v.tensor_tensor(mt3[par][:, :], mt3[(u - 1) % 3][:, :],
                                                  mt3[par][:, :], ALU.max))
                if li >= 2:
                    b.wait(v, ('exp', LAST_SUB[li - 2]))
                b.inc(v.tensor_scalar(nmt[lp][:, :], mt3[par][:, :], -0.125, None, ALU.mult),
                      'sV', 1, ('nm', li))
            if u >= 2 and SUBS[u - 2][6]:
                lix = SUBS[u - 2][0]
                hp, qp = LOGI[lix]
                b.wait(v, ('pvl', lix))
                rws = slice(0, 64) if hp % 2 == 0 else slice(64, 128)
                b.inc(v.tensor_copy(BH[rws, (hp // 2) * T + qp * 128:(hp // 2) * T + (qp + 1) * 128],
                                    pvt(lix, hp)), 'sV', 1, ('ye', lix))
        u = len(SUBS) - 1
        li, h, qt, c, W, first, last, nsub = SUBS[u]
        b.wait(v, ('exp', u))
        if nsub == 2:
            b.wait(v, ('exp', u - 1))
            step(v, 'sV', v.tensor_tensor(lt3[u % 3][:, :], lt3[(u - 1) % 3][:, :], lt3[u % 3][:, :], ALU.add))
        b.inc(v.reciprocal(rt[li % 2][:, :], lt3[u % 3][:, :]), 'sV', 1, ('rc', li))
        b.wait(v, ('rc', li))
        b.inc(v.tensor_scalar(SCRb[:, DGP[li % 2]:DGP[li % 2] + 128],
                              identb[:, :], rt[li % 2][:, :], None, ALU.mult),
              'sV', 1, ('dg', li))
        for x in (u - 1, u):
            b.wait(v, ('tp', x))
            b.inc(v.tensor_copy(sinTb[:, PTB3[x % 3]:PTB3[x % 3] + EVCUT],
                                reg(x, 0, EVCUT)), 'sV', 1, ('ev_v', x))
        lix = SUBS[-1][0]
        hp, qp = LOGI[lix]
        b.wait(v, ('pvl', lix))
        rws = slice(0, 64) if hp % 2 == 0 else slice(64, 128)
        b.inc(v.tensor_copy(BH[rws, (hp // 2) * T + qp * 128:(hp // 2) * T + (qp + 1) * 128],
                            pvt(lix, hp)), 'sV', 1, ('ye', lix))

    def p3_scalar(s):
        for u, (li, h, qt, c, W, first, last, nsub) in enumerate(SUBS):
            par = u % 3
            lp = li % 2
            b.wait(s, ('nm', li))
            if u >= 3:
                b.wait(s, ('tp', u - 3))
            b.inc(s.activation(cosTb[:, PHB3[par]:PHB3[par] + W], reg(u, 0, W), AF.Exp,
                               bias=nmt[lp][:, :], scale=0.125,
                               accum_out=lt3[par][:, :]), 'sC', 1, ('exp', u))
            if last:
                for x in range(u - nsub + 1, u + 1):
                    xW = SUBS[x][4]
                    if xW <= EVCUT:
                        b.mark('sC', ('ev', x))
                        continue
                    b.wait(s, ('tp', x))
                    if x >= 3:
                        b.wait(s, ('pv', x - 3))
                    b.inc(s.copy(sinTb[:, PTB3[x % 3] + EVCUT:PTB3[x % 3] + xW],
                                 reg(x, EVCUT, xW)), 'sC', 1, ('ev', x))

    b.phase('P3', [('tensor', p3_tensor), ('vector', p3_vector), ('scalar', p3_scalar)])

    if tap == 'y':
        taps['y'] = nc.dram_tensor('tap_y', [D, T], BF16, kind='ExternalOutput')

        def ts_(s):
            o_ap, i_ap = fm_pair(taps['y'], BH[:, :], NCH, T)
            b.inc(s.dma_start(out=o_ap, in_=i_ap), 'sS', 16, 'tap_st')
            b.wait(s, 'tap_st')
        b.phase('TAP', [('sync', ts_)])
        es.close()
        return nc, taps

    def p4_gpsimd(g):
        for ec in range(NCH):
            if ec >= 2:
                b.wait(g, ('wo_mm', ec - 2, NTQ - 1))
            b.inc(g.dma_start(out=wslot_r(ec % 2), in_=wblk_ap(wo, D, ec)),
                  'sG', 16, ('wo', ec))

    def p4_sync(s):
        for ec in range(NCH):
            if ec >= 2:
                b.wait(s, ('x2v', ec - 2, NTQ - 1))
            b.inc(s.dma_start(out=BKf[:, XS[ec % 2]:XS[ec % 2] + 2048],
                              in_=xf[ec * 128:(ec + 1) * 128, :]), 'sS', 16, ('x4', ec))
            b.wait(s, ('x2bf', ec))
            b.inc(s.dma_start(out=x2h[ec * 128:(ec + 1) * 128, :],
                              in_=BVf[:, PBf[ec % 2]:PBf[ec % 2] + 2048]), 'sS', 16, ('x2st', ec))
        if tap in ('x2', 'gated'):
            b.wait(s, ('x2bf', NCH - 1))

    def p4_tensor(t):
        for ec in range(NCH):
            b.wait(t, ('wo', ec))
            for tq in range(NTQ):
                if ec >= 1:
                    b.wait(t, ('x2v', ec - 1, tq))
                for K in range(NCH):
                    ins = t.matmul(psA[:, tq * 512:(tq + 1) * 512],
                                   wslot(ec % 2)[:, K * 128:(K + 1) * 128],
                                   BH[:, K * T + tq * 512:K * T + (tq + 1) * 512],
                                   start=(K == 0), stop=(K == NCH - 1))
                b.inc(ins, 'sT', 1, ('wo_mm', ec, tq))
        for tq in range(NTQ):
            b.wait(t, ('x2v', NCH - 1, tq))
        stats_mms(t, 'ln2', lambda c: BQ[:, c * T:(c + 1) * T],
                  lambda c: SCRb[:, SQ1[c % 2]:SQ1[c % 2] + 512],
                  lambda wave, c: ('sq2', wave, c))
        bcast_tensor(t, 'ln2')

    def p4_vector(v):
        for ec in range(NCH):
            for tq in range(NTQ):
                b.wait(v, ('wo_mm', ec, tq))
                if tq == 0:
                    b.wait(v, ('x4', ec))
                    if ec >= 2:
                        b.wait(v, ('x2st', ec - 2))
                        b.wait(v, ('x2bf', ec - 2))
                b.inc(v.tensor_tensor(BVf[:, PBf[ec % 2] + tq * 512:PBf[ec % 2] + (tq + 1) * 512],
                                      psA[:, tq * 512:(tq + 1) * 512],
                                      BKf[:, XS[ec % 2] + tq * 512:XS[ec % 2] + (tq + 1) * 512],
                                      ALU.add), 'sV', 1, ('x2v', ec, tq))
        stats_evac(v, 'ln2')
        ln_finish_vec(v, 'ln2')
        bcast_vec(v, 'ln2')

    def p4_scalar(s):
        for ec in range(NCH):
            b.wait(s, ('x2v', ec, NTQ - 1))
            b.inc(s.copy(BQ[:, ec * T:(ec + 1) * T],
                         BVf[:, PBf[ec % 2]:PBf[ec % 2] + 2048]), 'sC', 1, ('x2bf', ec))
        for wave in range(4):
            for ec in range(NCH):
                idx = wave * NCH + ec
                if idx >= 2:
                    pidx = idx - 2
                    b.wait(s, ('ln2_sw', pidx // NCH, pidx % NCH))
                b.wait(s, ('x2bf', ec))
                b.inc(s.activation(SCRb[:, SQ1[ec % 2]:SQ1[ec % 2] + 512],
                                   BQ[:, ec * T + wave * 512:ec * T + wave * 512 + 512],
                                   AF.Square), 'sC', 1, ('sq2', wave, ec))
        ln_finish_sca(s, 'ln2')

    b.phase('P4', [('gpsimd', p4_gpsimd), ('sync', p4_sync), ('tensor', p4_tensor),
                   ('vector', p4_vector), ('scalar', p4_scalar)])

    if tap == 'x2':
        taps['x2o'] = nc.dram_tensor('tap_x2o', [D, T], F32, kind='ExternalOutput')

        def ts_(s):
            for ec in range(NCH):
                b.inc(s.dma_start(out=taps['x2o'][ec * 128:(ec + 1) * 128, :],
                                  in_=x2h[ec * 128:(ec + 1) * 128, :]), 'sS', 16, ('x2o', ec))
            b.wait(s, ('x2o', NCH - 1))
        b.phase('TAP', [('sync', ts_)])
        es.close()
        return nc, taps

    def p5_sync(s):
        for c in range(NCH):
            if c >= 2:
                b.wait(s, ('h2', c - 2))
            b.inc(s.dma_start(out=BKf[:, XS[c % 2]:XS[c % 2] + 2048],
                              in_=x2h[c * 128:(c + 1) * 128, :]), 'sS', 16, ('x25', c))
        for hc in range(NHC):
            b.wait(s, ('gsl', hc))
            b.inc(s.dma_start(out=gth[hc * 128:(hc + 1) * 128, :],
                              in_=BV[:, GSL[hc % 2]:GSL[hc % 2] + 2048]),
                  'sS', 16, ('gst', hc))

    def p5_gpsimd(g):
        for hc in range(NHC):
            if hc >= 2:
                b.wait(g, ('u_mm', hc - 2, NTQ - 1))
            sl = (hc % 2) * 2
            b.inc(g.dma_start(out=wslot_r(sl), in_=wblk_ap(wg, DH, hc)), 'sG', 16, ('wg', hc))
            b.inc(g.dma_start(out=wslot_r(sl + 1), in_=wblk_ap(wu, DH, hc)), 'sG', 16, ('wu', hc))

    def p5_tensor(t):
        for hc in range(NHC):
            b.wait(t, ('wg', hc))
            b.wait(t, ('wu', hc))
            if hc == 0:
                b.wait(t, ('h2', NCH - 1))
            sl = (hc % 2) * 2
            for tq in range(NTQ):
                if hc >= 1:
                    b.wait(t, ('gu_ev', hc - 1, tq))
                for K in range(NCH):
                    ins = t.matmul(psA[:, tq * 512:(tq + 1) * 512],
                                   wslot(sl)[:, K * 128:(K + 1) * 128],
                                   BH[:, K * T + tq * 512:K * T + (tq + 1) * 512],
                                   start=(K == 0), stop=(K == NCH - 1))
                b.inc(ins, 'sT', 1, ('g_mm', hc, tq))
                for K in range(NCH):
                    ins = t.matmul(psB[:, tq * 512:(tq + 1) * 512],
                                   wslot(sl + 1)[:, K * 128:(K + 1) * 128],
                                   BH[:, K * T + tq * 512:K * T + (tq + 1) * 512],
                                   start=(K == 0), stop=(K == NCH - 1))
                b.inc(ins, 'sT', 1, ('u_mm', hc, tq))

    def p5_vector(v):
        b.wait(v, 'ln2_bc')
        for c in range(NCH):
            b.wait(v, ('x25', c))
            normalize(v, 'ln2', XS[c % 2], BH, c)
            b.mark('sV', ('h2', c))
        for hc in range(NHC):
            if hc >= 2:
                b.wait(v, ('gst', hc - 2))
            for tq in range(NTQ):
                b.wait(v, ('sil', hc, tq))
                b.wait(v, ('u_mm', hc, tq))
                b.inc(v.tensor_tensor(BV[:, GSL[hc % 2] + tq * 512:GSL[hc % 2] + (tq + 1) * 512],
                                      BV[:, SIL[tq % 2]:SIL[tq % 2] + 512],
                                      psB[:, tq * 512:(tq + 1) * 512], ALU.mult),
                      'sV', 1, ('gu_ev', hc, tq))
            b.mark('sV', ('gsl', hc))

    def p5_scalar(s):
        for hc in range(NHC):
            for tq in range(NTQ):
                b.wait(s, ('g_mm', hc, tq))
                prev = (hc, tq - 2) if tq >= 2 else ((hc - 1, tq + 2) if hc >= 1 else None)
                if prev is not None:
                    b.wait(s, ('gu_ev', prev[0], prev[1]))
                b.inc(s.activation(BV[:, SIL[tq % 2]:SIL[tq % 2] + 512],
                                   psA[:, tq * 512:(tq + 1) * 512], AF.Silu),
                      'sC', 1, ('sil', hc, tq))

    b.phase('P5', [('sync', p5_sync), ('gpsimd', p5_gpsimd), ('tensor', p5_tensor),
                   ('vector', p5_vector), ('scalar', p5_scalar)])

    if tap == 'gated':
        taps['g'] = nc.dram_tensor('tap_g', [DH, T], BF16, kind='ExternalOutput')
        taps['x2o'] = nc.dram_tensor('tap_x2o', [D, T], F32, kind='ExternalOutput')

        def ts2(s):
            for ec in range(NCH):
                b.inc(s.dma_start(out=taps['x2o'][ec * 128:(ec + 1) * 128, :],
                                  in_=x2h[ec * 128:(ec + 1) * 128, :]), 'sS', 16, ('x2o', ec))
            for hc in range(NHC):
                b.inc(s.dma_start(out=taps['g'][hc * 128:(hc + 1) * 128, :],
                                  in_=gth[hc * 128:(hc + 1) * 128, :]), 'sS', 16, ('gto', hc))
            b.wait(s, ('gto', NHC - 1))
        b.phase('TAP', [('sync', ts2)])
        es.close()
        return nc, taps

    XS6 = [0, 1024]
    PBf6 = [0, 1024]

    def p6_sync(s):
        for th in range(2):
            for half, buf in ((0, BH), (1, BQ)):
                in_ap = AP(gth, (half * 32 * 128) * T + th * 1024,
                           [[T, 128], [128 * T, 32], [1, 1024]])
                b.inc(s.dma_start(out=buf[:, 0:32 * 1024].rearrange('p (c n) -> p c n', c=32),
                                  in_=in_ap), 'sS', 16, ('gh', th, half))
            for dc in range(NCH):
                if dc >= 2:
                    b.wait(s, ('dnv', th, dc - 2, 1))
                b.inc(s.dma_start(out=BKf[:, XS6[dc % 2]:XS6[dc % 2] + 1024],
                                  in_=x2h[dc * 128:(dc + 1) * 128, th * 1024:(th + 1) * 1024]),
                      'sS', 16, ('x26', th, dc))
                b.wait(s, ('dnv', th, dc, 1))
                b.inc(s.dma_start(out=out[dc * 128:(dc + 1) * 128, th * 1024:(th + 1) * 1024],
                                  in_=BVf[:, PBf6[dc % 2]:PBf6[dc % 2] + 1024]),
                      'sS', 16, ('ost', th, dc))
        b.wait(s, ('ost', 1, NCH - 1))

    def p6_gpsimd(g):
        for th in range(2):
            for dc in range(NCH):
                idx = th * NCH + dc
                if idx >= 2:
                    pidx = idx - 2
                    b.wait(g, ('dnmm', pidx // NCH, pidx % NCH))
                for piece in range(4):
                    in_ap = AP(wd, dc * 128 + piece * 16 * 128 * D,
                               [[D, 128], [128 * D, 16], [1, 128]])
                    if dc % 2 == 0:
                        dst = SCRb[:, piece * 2048:(piece + 1) * 2048]
                    elif piece < 2:
                        dst = cosTb[:, piece * 2048:(piece + 1) * 2048]
                    else:
                        dst = sinTb[:, (piece - 2) * 2048:(piece - 1) * 2048]
                    b.inc(g.dma_start(out=dst.rearrange('p (c n) -> p c n', c=16),
                                      in_=in_ap), 'sG', 16,
                          ('wd', th, dc) if piece == 3 else None)

    def p6_tensor(t):
        for th in range(2):
            b.wait(t, ('gh', th, 0))
            b.wait(t, ('gh', th, 1))
            for dc in range(NCH):
                b.wait(t, ('wd', th, dc))
                for tq2 in range(2):
                    slot = (2 * dc + tq2) % 4
                    if 2 * dc + tq2 >= 4:
                        pr = 2 * dc + tq2 - 4
                        b.wait(t, ('dnv', th, pr // 2, pr % 2))
                    elif th == 1 and 2 * dc + tq2 < 4:
                        pr = 2 * NCH + 2 * dc + tq2 - 4
                        b.wait(t, ('dnv', 0, pr // 2, pr % 2))
                    for hc in range(NHC):
                        buf = BH if hc < 32 else BQ
                        if dc % 2 == 0:
                            wsl = SCRb[:, hc * 128:(hc + 1) * 128]
                        elif hc < 32:
                            wsl = cosTb[:, hc * 128:(hc + 1) * 128]
                        else:
                            wsl = sinTb[:, (hc - 32) * 128:(hc - 31) * 128]
                        ins = t.matmul(psA[:, slot * 512:(slot + 1) * 512], wsl,
                                       buf[:, (hc % 32) * 1024 + tq2 * 512:(hc % 32) * 1024 + (tq2 + 1) * 512],
                                       start=(hc == 0), stop=(hc == NHC - 1))
                    b.inc(ins, 'sT', 1, ('dn', th, dc, tq2))
                b.mark('sT', ('dnmm', th, dc))

    def p6_vector(v):
        for th in range(2):
            for dc in range(NCH):
                b.wait(v, ('x26', th, dc))
                for tq2 in range(2):
                    slot = (2 * dc + tq2) % 4
                    b.wait(v, ('dn', th, dc, tq2))
                    if dc >= 2:
                        b.wait(v, ('ost', th, dc - 2))
                    elif th == 1:
                        b.wait(v, ('ost', 0, NCH - 2 + dc))
                    b.inc(v.tensor_tensor(BVf[:, PBf6[dc % 2] + tq2 * 512:PBf6[dc % 2] + (tq2 + 1) * 512],
                                          psA[:, slot * 512:(slot + 1) * 512],
                                          BKf[:, XS6[dc % 2] + tq2 * 512:XS6[dc % 2] + (tq2 + 1) * 512],
                                          ALU.add), 'sV', 1, ('dnv', th, dc, tq2))

    b.phase('P6', [('sync', p6_sync), ('gpsimd', p6_gpsimd), ('tensor', p6_tensor),
                   ('vector', p6_vector)])
    es.close()
    return nc, {'out': out}



_NCACHE = {}


def _get_program(tap=None):
    if tap not in _NCACHE:
        _NCACHE[tap] = build_program(tap)
    return _NCACHE[tap]


def make_inputs(L0_attn_x, L0_attn_wq, L0_attn_wk, L0_attn_wv, L0_attn_wo,
                L0_ffn_w_gate, L0_ffn_w_up, L0_ffn_w_down):
    pq = _perm_q()
    pk = _perm_k()
    cos, sin = _rope_tables()
    wqT = np.ascontiguousarray(np.asarray(L0_attn_wq).T[:, pq], dtype=np.float32)
    wkT = np.ascontiguousarray(np.asarray(L0_attn_wk).T[:, pk], dtype=np.float32)
    wvT = np.ascontiguousarray(np.asarray(L0_attn_wv).T, dtype=np.float32)
    woT = np.ascontiguousarray(np.asarray(L0_attn_wo).T, dtype=np.float32)
    wgT = np.ascontiguousarray(np.asarray(L0_ffn_w_gate).T, dtype=np.float32)
    wuT = np.ascontiguousarray(np.asarray(L0_ffn_w_up).T, dtype=np.float32)
    wdT = np.ascontiguousarray(np.asarray(L0_ffn_w_down).T, dtype=np.float32)
    diag = np.where(np.arange(128)[None, :] > np.arange(128)[:, None],
                    np.float32(NEG), np.float32(0.0)).astype(np.float32)
    shared = dict(wq=wqT, wk=wkT, wv=wvT, wo=woT, wg=wgT, wu=wuT, wd=wdT,
                  cos_in=cos, sin_in=sin,
                  ident_in=np.eye(128, dtype=np.float32),
                  diag_in=diag,
                  onesc_in=np.ones((128, 1), np.float32),
                  onesf_in=np.ones((128, 128), np.float32))
    x = np.asarray(L0_attn_x)
    in_maps = []
    for core in range(8):
        im = dict(shared)
        im['xf'] = np.ascontiguousarray(x[core].T, dtype=np.float32)
        in_maps.append(im)
    return in_maps


def kernel(**inputs):
    nc, outs = _get_program(None)
    in_maps = make_inputs(**inputs)
    res = run_bass_kernel_spmd(nc, in_maps, core_ids=list(range(8)))
    return np.stack([r['outf'].T for r in res.results], axis=0).astype(np.float32)


# version 09
# speedup vs baseline: 1.2221x; 1.2221x over previous
import sys
import numpy as np

sys.path.insert(0, '/opt/trn_rl_repo')

import concourse.bass as bass
import concourse.mybir as mybir
from concourse.ap import AP
from concourse.bass_utils import run_bass_kernel_spmd
from contextlib import ExitStack

F32 = mybir.dt.float32
BF16 = mybir.dt.bfloat16
AF = mybir.ActivationFunctionType
ALU = mybir.AluOpType
AX = mybir.AxisListType

D = 2048
T = 2048
H = 32
HD = 64
DH = 8192
NCH = 16
NHC = DH // 128
NTC = T // 128
NTQ = T // 512
NEG = -1.0e9


def _perm_q():
    perm = np.zeros(2048, np.int64)
    for c in range(8):
        for j in range(4):
            h = 4 * (j + 4 * (1 if c >= 4 else 0)) + (c % 4)
            for i in range(32):
                perm[c * 128 + 32 * j + i] = h * 64 + 2 * i
                perm[(c + 8) * 128 + 32 * j + i] = h * 64 + 2 * i + 1
    return perm


def _perm_k():
    perm = np.zeros(512, np.int64)
    for c in range(2):
        for j in range(4):
            g = 4 * c + j
            for i in range(32):
                perm[c * 128 + 32 * j + i] = g * 64 + 2 * i
                perm[(c + 2) * 128 + 32 * j + i] = g * 64 + 2 * i + 1
    return perm


def _rope_tables():
    theta = 1.0 / (10000.0 ** (np.arange(0, HD, 2, dtype=np.float64) / HD))
    t = np.arange(T, dtype=np.float64)
    fr = theta[:, None] * t[None, :]
    cos = np.cos(fr).astype(np.float32)
    sin = np.sin(fr).astype(np.float32)
    return np.tile(cos, (4, 1)), np.tile(sin, (4, 1))



class _Dry:
    def then_inc(self, *a, **k):
        return self

    def __getattr__(self, name):
        return lambda *a, **k: self

_DRY = _Dry()


class B:

    def __init__(self, nc, es):
        self.nc = nc
        self.es = es
        self.sems = {}
        self.cnt = {}
        self.marks = {}
        self.want = []
        self.emitting = False

    def sem(self, name):
        self.sems[name] = self.es.enter_context(self.nc.semaphore(name))
        self.cnt[name] = 0

    def inc(self, instr, s, k=1, label=None):
        self.cnt[s] += k
        if self.emitting and instr is not _DRY and k > 0:
            instr.then_inc(self.sems[s], k)
        if label is not None:
            if self.emitting:
                assert self.marks[label] == (s, self.cnt[s]), (label, self.marks.get(label), (s, self.cnt[s]))
            else:
                assert label not in self.marks, label
                self.marks[label] = (s, self.cnt[s])
        return instr

    def mark(self, s, label):
        return self.inc(_DRY, s, 0, label)

    def wait(self, eng, label):
        if not self.emitting:
            self.want.append(label)
            return
        s, v = self.marks[label]
        eng.wait_ge(self.sems[s], v)

    def wait_maybe(self, eng, label):
        if not self.emitting:
            return
        if label in self.marks:
            s, v = self.marks[label]
            eng.wait_ge(self.sems[s], v)

    def phase(self, name, emitters):
        snap = dict(self.cnt)
        self.emitting = False
        self.want = []
        for en, fn in emitters:
            fn(_DRY)
        missing = [w for w in self.want if w not in self.marks]
        assert not missing, ('unresolved waits', missing[:5])
        self.cnt = snap
        self.emitting = True
        with self.nc.Block(name) as blk:
            for en, fn in emitters:
                getattr(blk, en)(fn)



def build_program(tap=None):
    nc = bass.Bass('TRN2', target_bir_lowering=False, detect_race_conditions=False)

    xf = nc.dram_tensor('xf', [D, T], F32, kind='ExternalInput')
    wq = nc.dram_tensor('wq', [D, D], F32, kind='ExternalInput')
    wk = nc.dram_tensor('wk', [D, 512], F32, kind='ExternalInput')
    wv = nc.dram_tensor('wv', [D, 512], F32, kind='ExternalInput')
    wo = nc.dram_tensor('wo', [D, D], F32, kind='ExternalInput')
    wg = nc.dram_tensor('wg', [D, DH], F32, kind='ExternalInput')
    wu = nc.dram_tensor('wu', [D, DH], F32, kind='ExternalInput')
    wd = nc.dram_tensor('wd', [DH, D], F32, kind='ExternalInput')
    cos_in = nc.dram_tensor('cos_in', [128, T], F32, kind='ExternalInput')
    sin_in = nc.dram_tensor('sin_in', [128, T], F32, kind='ExternalInput')
    ident_in = nc.dram_tensor('ident_in', [128, 128], F32, kind='ExternalInput')
    diag_in = nc.dram_tensor('diag_in', [128, 128], F32, kind='ExternalInput')
    onesc_in = nc.dram_tensor('onesc_in', [128, 1], F32, kind='ExternalInput')
    onesf_in = nc.dram_tensor('onesf_in', [128, 128], F32, kind='ExternalInput')

    x2h = nc.dram_tensor('x2h', [D, T], F32, kind='Internal')
    gth = nc.dram_tensor('gth', [DH, T], BF16, kind='Internal')

    def fm_pair(dram, sbuf_ap, nch, w):
        return (AP(dram, 0, [[w * 1, 128] if False else [dram.shape[1], 128],
                             [128 * dram.shape[1], nch], [1, w]]),
                sbuf_ap.rearrange('p (c t) -> p c t', c=nch))

    taps = {}
    out = None
    if tap is None:
        out = nc.dram_tensor('outf', [D, T], F32, kind='ExternalOutput')

    es = ExitStack()
    sb = lambda n, s, d: es.enter_context(nc.sbuf_tensor(n, s, d))

    BH = sb('BH', [128, NCH * T], BF16)
    BQ = sb('BQ', [128, NCH * T], BF16)
    BK = sb('BK', [128, 4 * T], BF16)
    BV = sb('BV', [128, NTC * 512], BF16)
    cosT = sb('cosT', [128, T], F32)
    sinT = sb('sinT', [128, T], F32)
    identb = sb('identb', [128, 128], BF16)
    diagm = sb('diagm', [128, 128], F32)
    onesc = sb('onesc', [128, 1], BF16)
    onesf = sb('onesf', [128, 128], F32)
    identf = sb('identf', [128, 128], F32)
    STT = sb('STT', [128, 160], F32)
    SCR = sb('SCR', [128, 5120], F32)
    SCRb = SCR.bitcast(BF16)
    BKf = BK.bitcast(F32)
    BVf = BV.bitcast(F32)
    cosTb = cosT.bitcast(BF16)
    sinTb = sinT.bitcast(BF16)
    mt = [sb('m%d' % i, [128, 1], F32) for i in range(2)]
    nmt = [sb('nm%d' % i, [128, 1], F32) for i in range(2)]
    lt = [sb('l%d' % i, [128, 1], F32) for i in range(2)]
    rt = [sb('r%d' % i, [128, 1], F32) for i in range(2)]

    RP = [4096, 4608]
    WSLOT = [0, 2048, 4096, 6144]
    PHB = [0, 2048]
    PTB = [0, 2048]
    GSL = [0, 2048]
    SIL = [4096, 4608]
    SQ1 = [0, 512]
    DGP = [8192, 8320]
    XS = [0, 2048]
    PBf = [0, 2048]

    psA = es.enter_context(nc.psum_tensor('psA', [128, 2048], F32))
    psB = es.enter_context(nc.psum_tensor('psB', [128, 2048], F32))
    psAb = psA.bitcast(BF16)
    psBb = psB.bitcast(BF16)

    b = B(nc, es)
    for s in ('sG', 'sS', 'sT', 'sV', 'sC'):
        b.sem(s)

    def R(i, lo=0, hi=16):
        return STT[:, i * 16 + lo:i * 16 + hi]

    def step(eng, sem, instr):
        b.inc(instr, sem, 1)
        if b.emitting:
            eng.wait_ge(b.sems[sem], b.cnt[sem])
        return instr

    def ln_finish_vec(v, ph):
        step(v, 'sV', v.tensor_scalar(R(2), R(0), 1.0 / D, None, ALU.mult))
        step(v, 'sV', v.tensor_scalar(R(3), R(1), 1.0 / D, None, ALU.mult))
        step(v, 'sV', v.tensor_tensor(R(4), R(2), R(2), ALU.mult))
        step(v, 'sV', v.tensor_tensor(R(4), R(3), R(4), ALU.subtract))
        step(v, 'sV', v.tensor_scalar(R(4), R(4), 1e-5, None, ALU.add))
        b.inc(v.reciprocal(R(5), R(4)), 'sV', 1, ph + '_rec')
        b.wait(v, ph + '_sqrt')
        step(v, 'sV', v.tensor_tensor(R(5), R(6), R(6), ALU.mult))
        step(v, 'sV', v.tensor_tensor(R(5), R(5), R(4), ALU.mult))
        step(v, 'sV', v.tensor_scalar(R(5), R(5), -1.0, 3.0, ALU.mult, ALU.add))
        step(v, 'sV', v.tensor_tensor(R(5), R(6), R(5), ALU.mult))
        step(v, 'sV', v.tensor_scalar(R(7), R(5), 0.5, None, ALU.mult))
        step(v, 'sV', v.tensor_tensor(R(8), R(2), R(7), ALU.mult))
        b.inc(v.tensor_scalar(R(8), R(8), -1.0, None, ALU.mult), 'sV', 1, ph + '_ab')
        if b.emitting:
            v.wait_ge(b.sems['sV'], b.cnt['sV'])

    def ln_finish_sca(s, ph):
        b.wait(s, ph + '_rec')
        b.inc(s.activation(R(6), R(5), AF.Sqrt), 'sC', 1, ph + '_sqrt')

    DGB = [RP[0], RP[0] + 128]

    def bcast_vec(v, ph):
        b.wait(v, 'c2_done')
        b.wait(v, ph + '_ab')
        for idx in range(32):
            tc, isb = idx // 2, idx % 2
            if idx >= 2:
                b.wait(v, (ph + '_bcm', idx - 2))
            col = R(8 if isb else 7, tc, tc + 1)
            b.inc(v.tensor_scalar(SCR[:, DGB[idx % 2]:DGB[idx % 2] + 128],
                                  identf[:, :], col, None, ALU.mult),
                  'sV', 1, (ph + '_dg', idx))

    def bcast_tensor(t, ph):
        for idx in range(32):
            tc, isb = idx // 2, idx % 2
            b.wait(t, (ph + '_dg', idx))
            ps = psB if isb else psA
            ins = t.matmul(ps[:, tc * 128:(tc + 1) * 128], onesf[:, :],
                           SCR[:, DGB[idx % 2]:DGB[idx % 2] + 128],
                           start=True, stop=True)
            b.inc(ins, 'sT', 1, (ph + '_bcm', idx))
        b.mark('sT', ph + '_bc')

    def stats_mms(t, ph, src_of, sq_of, sqwait):
        for wave in range(4):
            if wave >= 1:
                b.wait(t, (ph + '_evw', wave - 1))
            for c in range(NCH):
                b.wait(t, (ph + '_st_x', c) if ph == 'ln1' else ('x2bf', c))
                b.wait(t, sqwait(wave, c))
                for tbl in range(4):
                    tb = wave * 4 + tbl
                    t.matmul(psA[:, tbl * 512 + wave:tbl * 512 + wave + 1],
                             src_of(c)[:, tb * 128:(tb + 1) * 128],
                             onesc[:, :], start=(c == 0), stop=(c == NCH - 1))
                    ins = t.matmul(psB[:, tbl * 512 + wave:tbl * 512 + wave + 1],
                                   sq_of(c)[:, tbl * 128:(tbl + 1) * 128],
                                   onesc[:, :], start=(c == 0), stop=(c == NCH - 1))
                b.inc(ins, 'sT', 1, (ph + '_sw', wave, c))
            b.inc(_DRY, 'sT', 0, (ph + '_wave', wave))

    def stats_evac(v, ph):
        for wave in range(4):
            b.wait(v, (ph + '_wave', wave))
            b.inc(v.tensor_copy(R(0, wave * 4, wave * 4 + 4),
                                AP(psA, wave, [[2048, 128], [512, 4]])), 'sV', 1)
            b.inc(v.tensor_copy(R(1, wave * 4, wave * 4 + 4),
                                AP(psB, wave, [[2048, 128], [512, 4]])),
                  'sV', 1, (ph + '_evw', wave))
        if b.emitting:
            v.wait_ge(b.sems['sV'], b.cnt['sV'])
        b.mark('sV', ph + '_ev')

    def normalize(v, ph, src_off, dst, c):
        for tq in range(NTQ):
            sl = slice(tq * 512, (tq + 1) * 512)
            step(v, 'sV', v.tensor_tensor(SCR[:, RP[0]:RP[0] + 512],
                                          BKf[:, src_off + tq * 512:src_off + (tq + 1) * 512],
                                          psA[:, sl], ALU.mult))
            ins = v.tensor_tensor(dst[:, c * T + tq * 512:c * T + (tq + 1) * 512],
                                  SCR[:, RP[0]:RP[0] + 512], psB[:, sl], ALU.add)
            if tq < NTQ - 1:
                step(v, 'sV', ins)
            else:
                b.inc(ins, 'sV', 1)
                if b.emitting:
                    v.wait_ge(b.sems['sV'], b.cnt['sV'])
        return ins

    def p1_gpsimd(g):
        b.inc(g.dma_start(out=identb[:, :], in_=ident_in[:, :]), 'sG', 16)
        b.inc(g.dma_start(out=onesc[:, :], in_=onesc_in[:, :]), 'sG', 16, 'c_done')
        for c in range(NCH):
            b.inc(g.dma_start(out=BQ[:, c * T:(c + 1) * T],
                              in_=xf[c * 128:(c + 1) * 128, :]), 'sG', 16, ('xbf', c))

    def p1_sync(s):
        b.inc(s.dma_start(out=cosT[:, :], in_=cos_in[:, :]), 'sS', 16)
        b.inc(s.dma_start(out=sinT[:, :], in_=sin_in[:, :]), 'sS', 16)
        b.inc(s.dma_start(out=diagm[:, :], in_=diag_in[:, :]), 'sS', 16)
        b.inc(s.dma_start(out=onesf[:, :], in_=onesf_in[:, :]), 'sS', 16)
        b.inc(s.dma_start(out=identf[:, :], in_=ident_in[:, :]), 'sS', 16, 'c2_done')
        for c in range(NCH):
            if c >= 2:
                b.wait(s, ('norm1', c - 2))
            b.inc(s.dma_start(out=BKf[:, XS[c % 2]:XS[c % 2] + 2048],
                              in_=xf[c * 128:(c + 1) * 128, :]), 'sS', 16, ('xin', c))

    def p1_scalar(s):
        b.wait(s, 'c_done')
        for wave in range(4):
            for c in range(NCH):
                b.wait(s, ('xbf', c))
                idx = wave * NCH + c
                if idx >= 2:
                    pidx = idx - 2
                    b.wait(s, ('ln1_sw', pidx // NCH, pidx % NCH))
                b.inc(s.activation(SCRb[:, SQ1[c % 2]:SQ1[c % 2] + 512],
                                   BQ[:, c * T + wave * 512:c * T + wave * 512 + 512],
                                   AF.Square), 'sC', 1, ('sq1', wave, c))
        ln_finish_sca(s, 'ln1')

    def p1_tensor(t):
        for c in range(NCH):
            b.wait(t, ('xbf', c))
            b.mark('sT', ('ln1_st_x', c))
        stats_mms(t, 'ln1', lambda c: BQ[:, c * T:(c + 1) * T],
                  lambda c: SCRb[:, SQ1[c % 2]:SQ1[c % 2] + 512],
                  lambda wave, c: ('sq1', wave, c))
        bcast_tensor(t, 'ln1')

    def p1_vector(v):
        stats_evac(v, 'ln1')
        ln_finish_vec(v, 'ln1')
        bcast_vec(v, 'ln1')
        b.wait(v, 'ln1_bc')
        for c in range(NCH):
            b.wait(v, ('xin', c))
            normalize(v, 'ln1', XS[c % 2], BH, c)
            b.mark('sV', ('norm1', c))

    b.phase('P1', [('gpsimd', p1_gpsimd), ('sync', p1_sync), ('scalar', p1_scalar),
                   ('tensor', p1_tensor), ('vector', p1_vector)])

    if tap == 'h':
        taps['h'] = nc.dram_tensor('tap_h', [D, T], BF16, kind='ExternalOutput')
        taps['stt'] = nc.dram_tensor('tap_stt', [128, 160], F32, kind='ExternalOutput')

        def ts_(s):
            o_ap, i_ap = fm_pair(taps['h'], BH[:, :], NCH, T)
            b.inc(s.dma_start(out=o_ap, in_=i_ap), 'sS', 16, 'tap_st')
            b.inc(s.dma_start(out=taps['stt'][:, :], in_=STT[:, :]), 'sS', 16, 'tap_st2')
            b.wait(s, 'tap_st2')
        b.phase('TAP', [('sync', ts_)])
        es.close()
        return nc, taps

    def wblk_ap(w, ncols, cblk, blkw=128):
        rows = w.shape[0]
        return AP(w, cblk * blkw, [[ncols, 128], [128 * ncols, rows // 128], [1, blkw]])

    def wslot(i, n=2048):
        return SCRb[:, WSLOT[i]:WSLOT[i] + n]

    def wslot_r(i, k=16, n=2048):
        return SCRb[:, WSLOT[i]:WSLOT[i] + n].rearrange('p (k n) -> p k n', k=k)

    def p2_gpsimd(g):
        for p in range(8):
            if p >= 2:
                b.wait(g, ('qmm', p - 2, NTQ - 1))
            b.inc(g.dma_start(out=wslot_r(2 * (p % 2)), in_=wblk_ap(wq, D, p)),
                  'sG', 16, ('wqe', p))
            b.inc(g.dma_start(out=wslot_r(2 * (p % 2) + 1), in_=wblk_ap(wq, D, p + 8)),
                  'sG', 16, ('wqo', p))
        for c in range(4):
            b.inc(g.dma_start(out=BV[:, c * 2048:(c + 1) * 2048]
                              .rearrange('p (k n) -> p k n', k=16),
                              in_=wblk_ap(wk, 512, c)), 'sG', 16, ('wk', c))
        b.wait(g, ('qmm', 6, NTQ - 1))
        b.inc(g.dma_start(out=SCRb[:, WSLOT[0]:WSLOT[0] + 4096]
                          .rearrange('p (k n) -> p k n', k=16),
                          in_=wblk_ap(wv, 512, 0, 256)), 'sG', 16, ('wv', 0))
        b.wait(g, ('qmm', 7, NTQ - 1))
        b.inc(g.dma_start(out=SCRb[:, WSLOT[2]:WSLOT[2] + 4096]
                          .rearrange('p (k n) -> p k n', k=16),
                          in_=wblk_ap(wv, 512, 1, 256)), 'sG', 16, ('wv', 1))

    def p2_tensor(t):
        for p in range(8):
            b.wait(t, ('wqe', p))
            b.wait(t, ('wqo', p))
            for tq in range(NTQ):
                if p >= 1:
                    b.wait(t, ('rope_q', p - 1, tq))
                se, so = 2 * (p % 2), 2 * (p % 2) + 1
                for K in range(NCH):
                    t.matmul(psA[:, tq * 512:(tq + 1) * 512],
                             wslot(se)[:, K * 128:(K + 1) * 128],
                             BH[:, K * T + tq * 512:K * T + (tq + 1) * 512],
                             start=(K == 0), stop=(K == NCH - 1))
                for K in range(NCH):
                    ins = t.matmul(psB[:, tq * 512:(tq + 1) * 512],
                                   wslot(so)[:, K * 128:(K + 1) * 128],
                                   BH[:, K * T + tq * 512:K * T + (tq + 1) * 512],
                                   start=(K == 0), stop=(K == NCH - 1))
                b.inc(ins, 'sT', 1, ('qmm', p, tq))
        for p in range(2):
            b.wait(t, ('wk', p))
            b.wait(t, ('wk', p + 2))
            for tq in range(NTQ):
                b.wait(t, ('rope_q', 7, tq))
                if p == 1:
                    b.wait(t, ('rope_k', 0, tq))
                for K in range(NCH):
                    t.matmul(psA[:, tq * 512:(tq + 1) * 512],
                             BV[:, p * 2048 + K * 128:p * 2048 + (K + 1) * 128],
                             BH[:, K * T + tq * 512:K * T + (tq + 1) * 512],
                             start=(K == 0), stop=(K == NCH - 1))
                for K in range(NCH):
                    ins = t.matmul(psB[:, tq * 512:(tq + 1) * 512],
                                   BV[:, (p + 2) * 2048 + K * 128:(p + 2) * 2048 + (K + 1) * 128],
                                   BH[:, K * T + tq * 512:K * T + (tq + 1) * 512],
                                   start=(K == 0), stop=(K == NCH - 1))
                b.inc(ins, 'sT', 1, ('kmm', p, tq))
        for tc in range(NTC):
            for oh in range(2):
                b.wait(t, ('wv', oh))
                b.wait(t, ('rope_k', 1, NTQ - 1))
                if tc >= 2:
                    b.wait(t, ('vev', tc - 2, oh))
                slot = (2 * tc + oh) % 4
                woff = WSLOT[0] if oh == 0 else WSLOT[2]
                for K in range(NCH):
                    ins = t.matmul(psA[:, slot * 512:slot * 512 + 256],
                                   BH[:, K * T + tc * 128:K * T + (tc + 1) * 128],
                                   SCRb[:, woff + K * 256:woff + (K + 1) * 256],
                                   start=(K == 0), stop=(K == NCH - 1))
                b.inc(ins, 'sT', 1, ('vmm', tc, oh))

    def _rope(v, p, tq, is_q):
        sl = slice(tq * 512, (tq + 1) * 512)
        cs = cosT[:, tq * 512:(tq + 1) * 512]
        sn = sinT[:, tq * 512:(tq + 1) * 512]
        t1 = SCR[:, RP[0]:RP[0] + 512]
        t2 = SCR[:, RP[1]:RP[1] + 512]
        dst, off = (BQ, 8) if is_q else (BK, 2)
        de = dst[:, p * T + tq * 512:p * T + (tq + 1) * 512]
        do = dst[:, (p + off) * T + tq * 512:(p + off) * T + (tq + 1) * 512]
        b.inc(v.tensor_tensor(t1, psA[:, sl], cs, ALU.mult), 'sV', 1)
        step(v, 'sV', v.tensor_tensor(t2, psB[:, sl], sn, ALU.mult))
        step(v, 'sV', v.tensor_tensor(de, t1, t2, ALU.subtract))
        b.inc(v.tensor_tensor(t1, psA[:, sl], sn, ALU.mult), 'sV', 1)
        step(v, 'sV', v.tensor_tensor(t2, psB[:, sl], cs, ALU.mult))
        ins = v.tensor_tensor(do, t1, t2, ALU.add)
        b.inc(ins, 'sV', 1)
        if b.emitting:
            v.wait_ge(b.sems['sV'], b.cnt['sV'])
        return ins

    def p2_vector(v):
        for p in range(8):
            for tq in range(NTQ):
                b.wait(v, ('qmm', p, tq))
                _rope(v, p, tq, True)
                b.mark('sV', ('rope_q', p, tq))
        for p in range(2):
            for tq in range(NTQ):
                b.wait(v, ('kmm', p, tq))
                _rope(v, p, tq, False)
                b.mark('sV', ('rope_k', p, tq))
        for tc in range(NTC):
            for oh in range(2):
                b.wait(v, ('vmm', tc, oh))
                slot = (2 * tc + oh) % 4
                b.inc(v.tensor_copy(BV[:, tc * 512 + oh * 256:tc * 512 + oh * 256 + 256],
                                    psA[:, slot * 512:slot * 512 + 256]),
                      'sV', 1, ('vev', tc, oh))

    b.phase('P2', [('gpsimd', p2_gpsimd), ('tensor', p2_tensor), ('vector', p2_vector)])

    if tap == 'qkv':
        taps['q'] = nc.dram_tensor('tap_q', [D, T], BF16, kind='ExternalOutput')
        taps['k'] = nc.dram_tensor('tap_k', [512, T], BF16, kind='ExternalOutput')
        taps['v'] = nc.dram_tensor('tap_v', [T, 512], BF16, kind='ExternalOutput')

        def ts_(s):
            o_ap, i_ap = fm_pair(taps['q'], BQ[:, :], NCH, T)
            b.inc(s.dma_start(out=o_ap, in_=i_ap), 'sS', 16, 'tap_qst')
            o_ap, i_ap = fm_pair(taps['k'], BK[:, :], 4, T)
            b.inc(s.dma_start(out=o_ap, in_=i_ap), 'sS', 16, 'tap_kst')
            o_ap, i_ap = fm_pair(taps['v'], BV[:, :], NTC, 512)
            b.inc(s.dma_start(out=o_ap, in_=i_ap), 'sS', 16, 'tap_vst')
            b.wait(s, 'tap_vst')
        b.phase('TAP', [('sync', ts_)])
        es.close()
        return nc, taps

    ITERS = [(h, qt) for h in range(H) for qt in range(NTC)]

    def own_of(i):
        return (psA, psAb) if i % 2 == 0 else (psB, psBb)

    def pv_loc(i):
        h, qt = ITERS[i]
        S = (qt + 1) * 128
        ps = own_of(i)[0] if S <= 1920 else own_of(i + 1)[0]
        return ps[0:64, 1920:2048] if h % 2 == 0 else ps[64:128, 1920:2048]

    PT3 = [0, 2048, 4096]

    def phb3(i):
        k = i % 3
        if k < 2:
            return cosTb[:, k * 2048:k * 2048 + 2048]
        return SCRb[:, 6144:8192]

    def S_of(i):
        return (ITERS[i][1] + 1) * 128

    def vcut_of(i):
        return 512 if (S_of(i) > 512 and i % 2 == 0) else 0

    def pv_loc(i):
        h, qt = ITERS[i]
        ps = own_of(i + 1)[0] if S_of(i) <= 1920 else own_of(i)[0]
        return ps[0:64, 1920:2048] if h % 2 == 0 else ps[64:128, 1920:2048]

    def emit_tp(t, i):
        S = S_of(i)
        par = i % 2
        dst = own_of(i + 1)[0]
        b.wait(t, ('dg', i))
        if i + 1 < len(ITERS):
            b.wait(t, ('exp', i + 1))
        pb = phb3(i)
        for sj in range(S // 128):
            ins = t.matmul(dst[:, sj * 128:(sj + 1) * 128],
                           pb[:, sj * 128:(sj + 1) * 128],
                           SCRb[:, DGP[par]:DGP[par] + 128],
                           start=True, stop=True)
        b.inc(ins, 'sT', 1, ('tp', i))

    def emit_pv(t, i):
        h, qt = ITERS[i]
        S = S_of(i)
        g = h // 4
        b.wait(t, ('ev', i))
        b.wait(t, ('ev_v', i))
        loc = pv_loc(i)
        tpos = (0, 0) if h % 2 == 0 else (0, 64)
        for sj in range(S // 128):
            ins = t.matmul(loc, BV[:, sj * 512 + g * 64:sj * 512 + g * 64 + 64],
                           SCRb[:, PT3[i % 3] + sj * 128:PT3[i % 3] + (sj + 1) * 128],
                           start=(sj == 0), stop=(sj == S // 128 - 1), tile_position=tpos)
        b.inc(ins, 'sT', 1, ('pv', i))

    def p3_tensor(t):
        n = len(ITERS)
        for i in range(n + 3):
            if 0 <= i - 3 < n:
                emit_pv(t, i - 3)
            if i < n:
                h, qt = ITERS[i]
                S = (qt + 1) * 128
                g = h // 4
                j = g % 4
                m = h % 4
                c_ev = m + 4 * (1 if g >= 4 else 0)
                own = own_of(i)[0]
                if i >= 2:
                    b.wait(t, ('exp', i - 2))
                if i >= 3:
                    b.wait(t, ('ev', i - 3))
                    b.wait(t, ('ev_v', i - 3))
                    if S > 1536:
                        b.wait(t, ('ye', i - 3))
                rows = slice(32 * j, 32 * j + 32)
                for st in range((S + 511) // 512):
                    w = min(512, S - st * 512)
                    t.matmul(own[:, st * 512:st * 512 + w],
                             BQ[rows, c_ev * T + qt * 128:c_ev * T + (qt + 1) * 128],
                             BK[rows, (g // 4) * T + st * 512:(g // 4) * T + st * 512 + w],
                             start=True, stop=False, tile_position=(32 * j, 0))
                    ins = t.matmul(own[:, st * 512:st * 512 + w],
                                   BQ[rows, (c_ev + 8) * T + qt * 128:(c_ev + 8) * T + (qt + 1) * 128],
                                   BK[rows, (g // 4 + 2) * T + st * 512:(g // 4 + 2) * T + st * 512 + w],
                                   start=False, stop=True, tile_position=(32 * j, 0))
                b.inc(ins, 'sT', 1, ('sc', i))
            if 0 <= i - 2 < n:
                emit_tp(t, i - 2)

    def p3_vector(v):
        n = len(ITERS)
        for i in range(n + 3):
            if 0 <= i - 1 < n:
                j = i - 1
                pj = j % 2
                b.wait(v, ('exp', j))
                if j >= 2:
                    b.wait(v, ('tp', j - 2))
                b.inc(v.reciprocal(rt[pj][:, :], lt[pj][:, :]), 'sV', 1, ('rc', j))
                b.wait(v, ('rc', j))
                b.inc(v.tensor_scalar(SCRb[:, DGP[pj]:DGP[pj] + 128], identb[:, :],
                                      rt[pj][:, :], None, ALU.mult), 'sV', 1, ('dg', j))
            if 0 <= i - 3 < n:
                j = i - 3
                hp, qp = ITERS[j]
                b.wait(v, ('pv', j))
                rws = slice(0, 64) if hp % 2 == 0 else slice(64, 128)
                b.inc(v.tensor_copy(BH[rws, (hp // 2) * T + qp * 128:(hp // 2) * T + (qp + 1) * 128],
                                    pv_loc(j)), 'sV', 1, ('ye', j))
            if 0 <= i - 2 < n:
                j = i - 2
                vc = vcut_of(j)
                b.wait(v, ('tp', j))
                if vc:
                    if j >= 3:
                        b.wait(v, ('pv', j - 3))
                    b.inc(v.tensor_copy(SCRb[:, PT3[j % 3]:PT3[j % 3] + vc],
                                        own_of(j + 1)[0][:, 0:vc]), 'sV', 1, ('ev_v', j))
                else:
                    b.mark('sV', ('ev_v', j))
            if i < n:
                S = S_of(i)
                own = own_of(i)[0]
                par = i % 2
                b.wait(v, ('sc', i))
                step(v, 'sV', v.tensor_tensor(own[:, S - 128:S], own[:, S - 128:S],
                                              diagm[:, :], ALU.add))
                b.inc(v.reduce_max(mt[par][:, :], own[:, 0:S], axis=AX.X), 'sV', 1, ('red', i))
                b.wait(v, ('red', i))
                b.inc(v.tensor_scalar(nmt[par][:, :], mt[par][:, :], -0.125, None, ALU.mult),
                      'sV', 1, ('nm', i))

    def p3_scalar(s):
        n = len(ITERS)
        for i in range(n + 3):
            if 0 <= i - 2 < n:
                j = i - 2
                S = S_of(j)
                vc = vcut_of(j)
                b.wait(s, ('tp', j))
                if j >= 3:
                    b.wait(s, ('pv', j - 3))
                b.inc(s.copy(SCRb[:, PT3[j % 3] + vc:PT3[j % 3] + S],
                             own_of(j + 1)[0][:, vc:S]), 'sC', 1, ('ev', j))
            if i < n:
                S = S_of(i)
                par = i % 2
                b.wait(s, ('nm', i))
                if i >= 3:
                    b.wait(s, ('tp', i - 3))
                b.inc(s.activation(phb3(i)[:, 0:S], own_of(i)[0][:, 0:S],
                                   AF.Exp, bias=nmt[par][:, :], scale=0.125,
                                   accum_out=lt[par][:, :]), 'sC', 1, ('exp', i))

    b.phase('P3', [('tensor', p3_tensor), ('vector', p3_vector), ('scalar', p3_scalar)])

    if tap == 'y':
        taps['y'] = nc.dram_tensor('tap_y', [D, T], BF16, kind='ExternalOutput')

        def ts_(s):
            o_ap, i_ap = fm_pair(taps['y'], BH[:, :], NCH, T)
            b.inc(s.dma_start(out=o_ap, in_=i_ap), 'sS', 16, 'tap_st')
            b.wait(s, 'tap_st')
        b.phase('TAP', [('sync', ts_)])
        es.close()
        return nc, taps

    def p4_gpsimd(g):
        for ec in range(NCH):
            if ec >= 2:
                b.wait(g, ('wo_mm', ec - 2, NTQ - 1))
            b.inc(g.dma_start(out=wslot_r(ec % 2), in_=wblk_ap(wo, D, ec)),
                  'sG', 16, ('wo', ec))

    def p4_sync(s):
        for ec in range(NCH):
            if ec >= 2:
                b.wait(s, ('x2v', ec - 2, NTQ - 1))
            b.inc(s.dma_start(out=BKf[:, XS[ec % 2]:XS[ec % 2] + 2048],
                              in_=xf[ec * 128:(ec + 1) * 128, :]), 'sS', 16, ('x4', ec))
            b.wait(s, ('x2bf', ec))
            b.inc(s.dma_start(out=x2h[ec * 128:(ec + 1) * 128, :],
                              in_=BVf[:, PBf[ec % 2]:PBf[ec % 2] + 2048]), 'sS', 16, ('x2st', ec))
        if tap in ('x2', 'gated'):
            b.wait(s, ('x2bf', NCH - 1))

    def p4_tensor(t):
        for ec in range(NCH):
            b.wait(t, ('wo', ec))
            for tq in range(NTQ):
                if ec >= 1:
                    b.wait(t, ('x2v', ec - 1, tq))
                for K in range(NCH):
                    ins = t.matmul(psA[:, tq * 512:(tq + 1) * 512],
                                   wslot(ec % 2)[:, K * 128:(K + 1) * 128],
                                   BH[:, K * T + tq * 512:K * T + (tq + 1) * 512],
                                   start=(K == 0), stop=(K == NCH - 1))
                b.inc(ins, 'sT', 1, ('wo_mm', ec, tq))
        for tq in range(NTQ):
            b.wait(t, ('x2v', NCH - 1, tq))
        stats_mms(t, 'ln2', lambda c: BQ[:, c * T:(c + 1) * T],
                  lambda c: SCRb[:, SQ1[c % 2]:SQ1[c % 2] + 512],
                  lambda wave, c: ('sq2', wave, c))
        bcast_tensor(t, 'ln2')

    def p4_vector(v):
        for ec in range(NCH):
            for tq in range(NTQ):
                b.wait(v, ('wo_mm', ec, tq))
                if tq == 0:
                    b.wait(v, ('x4', ec))
                    if ec >= 2:
                        b.wait(v, ('x2st', ec - 2))
                        b.wait(v, ('x2bf', ec - 2))
                b.inc(v.tensor_tensor(BVf[:, PBf[ec % 2] + tq * 512:PBf[ec % 2] + (tq + 1) * 512],
                                      psA[:, tq * 512:(tq + 1) * 512],
                                      BKf[:, XS[ec % 2] + tq * 512:XS[ec % 2] + (tq + 1) * 512],
                                      ALU.add), 'sV', 1, ('x2v', ec, tq))
        stats_evac(v, 'ln2')
        ln_finish_vec(v, 'ln2')
        bcast_vec(v, 'ln2')

    def p4_scalar(s):
        for ec in range(NCH):
            b.wait(s, ('x2v', ec, NTQ - 1))
            b.inc(s.copy(BQ[:, ec * T:(ec + 1) * T],
                         BVf[:, PBf[ec % 2]:PBf[ec % 2] + 2048]), 'sC', 1, ('x2bf', ec))
        for wave in range(4):
            for ec in range(NCH):
                idx = wave * NCH + ec
                if idx >= 2:
                    pidx = idx - 2
                    b.wait(s, ('ln2_sw', pidx // NCH, pidx % NCH))
                b.wait(s, ('x2bf', ec))
                b.inc(s.activation(SCRb[:, SQ1[ec % 2]:SQ1[ec % 2] + 512],
                                   BQ[:, ec * T + wave * 512:ec * T + wave * 512 + 512],
                                   AF.Square), 'sC', 1, ('sq2', wave, ec))
        ln_finish_sca(s, 'ln2')

    b.phase('P4', [('gpsimd', p4_gpsimd), ('sync', p4_sync), ('tensor', p4_tensor),
                   ('vector', p4_vector), ('scalar', p4_scalar)])

    if tap == 'x2':
        taps['x2o'] = nc.dram_tensor('tap_x2o', [D, T], F32, kind='ExternalOutput')

        def ts_(s):
            for ec in range(NCH):
                b.inc(s.dma_start(out=taps['x2o'][ec * 128:(ec + 1) * 128, :],
                                  in_=x2h[ec * 128:(ec + 1) * 128, :]), 'sS', 16, ('x2o', ec))
            b.wait(s, ('x2o', NCH - 1))
        b.phase('TAP', [('sync', ts_)])
        es.close()
        return nc, taps

    def p5_sync(s):
        for c in range(NCH):
            if c >= 2:
                b.wait(s, ('h2', c - 2))
            b.inc(s.dma_start(out=BKf[:, XS[c % 2]:XS[c % 2] + 2048],
                              in_=x2h[c * 128:(c + 1) * 128, :]), 'sS', 16, ('x25', c))
        for hc in range(NHC):
            b.wait(s, ('gsl', hc))
            b.inc(s.dma_start(out=gth[hc * 128:(hc + 1) * 128, :],
                              in_=BV[:, GSL[hc % 2]:GSL[hc % 2] + 2048]),
                  'sS', 16, ('gst', hc))

    def p5_gpsimd(g):
        for hc in range(NHC):
            if hc >= 2:
                b.wait(g, ('u_mm', hc - 2, NTQ - 1))
            sl = (hc % 2) * 2
            b.inc(g.dma_start(out=wslot_r(sl), in_=wblk_ap(wg, DH, hc)), 'sG', 16, ('wg', hc))
            b.inc(g.dma_start(out=wslot_r(sl + 1), in_=wblk_ap(wu, DH, hc)), 'sG', 16, ('wu', hc))

    def p5_tensor(t):
        for hc in range(NHC):
            b.wait(t, ('wg', hc))
            b.wait(t, ('wu', hc))
            if hc == 0:
                b.wait(t, ('h2', NCH - 1))
            sl = (hc % 2) * 2
            for tq in range(NTQ):
                if hc >= 1:
                    b.wait(t, ('gu_ev', hc - 1, tq))
                for K in range(NCH):
                    ins = t.matmul(psA[:, tq * 512:(tq + 1) * 512],
                                   wslot(sl)[:, K * 128:(K + 1) * 128],
                                   BH[:, K * T + tq * 512:K * T + (tq + 1) * 512],
                                   start=(K == 0), stop=(K == NCH - 1))
                b.inc(ins, 'sT', 1, ('g_mm', hc, tq))
                for K in range(NCH):
                    ins = t.matmul(psB[:, tq * 512:(tq + 1) * 512],
                                   wslot(sl + 1)[:, K * 128:(K + 1) * 128],
                                   BH[:, K * T + tq * 512:K * T + (tq + 1) * 512],
                                   start=(K == 0), stop=(K == NCH - 1))
                b.inc(ins, 'sT', 1, ('u_mm', hc, tq))

    def p5_vector(v):
        b.wait(v, 'ln2_bc')
        for c in range(NCH):
            b.wait(v, ('x25', c))
            normalize(v, 'ln2', XS[c % 2], BH, c)
            b.mark('sV', ('h2', c))
        for hc in range(NHC):
            if hc >= 2:
                b.wait(v, ('gst', hc - 2))
            for tq in range(NTQ):
                b.wait(v, ('sil', hc, tq))
                b.wait(v, ('u_mm', hc, tq))
                b.inc(v.tensor_tensor(BV[:, GSL[hc % 2] + tq * 512:GSL[hc % 2] + (tq + 1) * 512],
                                      BV[:, SIL[tq % 2]:SIL[tq % 2] + 512],
                                      psB[:, tq * 512:(tq + 1) * 512], ALU.mult),
                      'sV', 1, ('gu_ev', hc, tq))
            b.mark('sV', ('gsl', hc))

    def p5_scalar(s):
        for hc in range(NHC):
            for tq in range(NTQ):
                b.wait(s, ('g_mm', hc, tq))
                prev = (hc, tq - 2) if tq >= 2 else ((hc - 1, tq + 2) if hc >= 1 else None)
                if prev is not None:
                    b.wait(s, ('gu_ev', prev[0], prev[1]))
                b.inc(s.activation(BV[:, SIL[tq % 2]:SIL[tq % 2] + 512],
                                   psA[:, tq * 512:(tq + 1) * 512], AF.Silu),
                      'sC', 1, ('sil', hc, tq))

    b.phase('P5', [('sync', p5_sync), ('gpsimd', p5_gpsimd), ('tensor', p5_tensor),
                   ('vector', p5_vector), ('scalar', p5_scalar)])

    if tap == 'gated':
        taps['g'] = nc.dram_tensor('tap_g', [DH, T], BF16, kind='ExternalOutput')
        taps['x2o'] = nc.dram_tensor('tap_x2o', [D, T], F32, kind='ExternalOutput')

        def ts2(s):
            for ec in range(NCH):
                b.inc(s.dma_start(out=taps['x2o'][ec * 128:(ec + 1) * 128, :],
                                  in_=x2h[ec * 128:(ec + 1) * 128, :]), 'sS', 16, ('x2o', ec))
            for hc in range(NHC):
                b.inc(s.dma_start(out=taps['g'][hc * 128:(hc + 1) * 128, :],
                                  in_=gth[hc * 128:(hc + 1) * 128, :]), 'sS', 16, ('gto', hc))
            b.wait(s, ('gto', NHC - 1))
        b.phase('TAP', [('sync', ts2)])
        es.close()
        return nc, taps

    XS6 = [0, 1024]
    PBf6 = [0, 1024]

    def p6_sync(s):
        for th in range(2):
            for half, buf in ((0, BH), (1, BQ)):
                in_ap = AP(gth, (half * 32 * 128) * T + th * 1024,
                           [[T, 128], [128 * T, 32], [1, 1024]])
                b.inc(s.dma_start(out=buf[:, 0:32 * 1024].rearrange('p (c n) -> p c n', c=32),
                                  in_=in_ap), 'sS', 16, ('gh', th, half))
            for dc in range(NCH):
                if dc >= 2:
                    b.wait(s, ('dnv', th, dc - 2, 1))
                b.inc(s.dma_start(out=BKf[:, XS6[dc % 2]:XS6[dc % 2] + 1024],
                                  in_=x2h[dc * 128:(dc + 1) * 128, th * 1024:(th + 1) * 1024]),
                      'sS', 16, ('x26', th, dc))
                b.wait(s, ('dnv', th, dc, 1))
                b.inc(s.dma_start(out=out[dc * 128:(dc + 1) * 128, th * 1024:(th + 1) * 1024],
                                  in_=BVf[:, PBf6[dc % 2]:PBf6[dc % 2] + 1024]),
                      'sS', 16, ('ost', th, dc))
        b.wait(s, ('ost', 1, NCH - 1))

    def p6_gpsimd(g):
        for th in range(2):
            for dc in range(NCH):
                idx = th * NCH + dc
                if idx >= 2:
                    pidx = idx - 2
                    b.wait(g, ('dnmm', pidx // NCH, pidx % NCH))
                for piece in range(4):
                    in_ap = AP(wd, dc * 128 + piece * 16 * 128 * D,
                               [[D, 128], [128 * D, 16], [1, 128]])
                    if dc % 2 == 0:
                        dst = SCRb[:, piece * 2048:(piece + 1) * 2048]
                    elif piece < 2:
                        dst = cosTb[:, piece * 2048:(piece + 1) * 2048]
                    else:
                        dst = sinTb[:, (piece - 2) * 2048:(piece - 1) * 2048]
                    b.inc(g.dma_start(out=dst.rearrange('p (c n) -> p c n', c=16),
                                      in_=in_ap), 'sG', 16,
                          ('wd', th, dc) if piece == 3 else None)

    def p6_tensor(t):
        for th in range(2):
            b.wait(t, ('gh', th, 0))
            b.wait(t, ('gh', th, 1))
            for dc in range(NCH):
                b.wait(t, ('wd', th, dc))
                for tq2 in range(2):
                    slot = (2 * dc + tq2) % 4
                    if 2 * dc + tq2 >= 4:
                        pr = 2 * dc + tq2 - 4
                        b.wait(t, ('dnv', th, pr // 2, pr % 2))
                    elif th == 1 and 2 * dc + tq2 < 4:
                        pr = 2 * NCH + 2 * dc + tq2 - 4
                        b.wait(t, ('dnv', 0, pr // 2, pr % 2))
                    for hc in range(NHC):
                        buf = BH if hc < 32 else BQ
                        if dc % 2 == 0:
                            wsl = SCRb[:, hc * 128:(hc + 1) * 128]
                        elif hc < 32:
                            wsl = cosTb[:, hc * 128:(hc + 1) * 128]
                        else:
                            wsl = sinTb[:, (hc - 32) * 128:(hc - 31) * 128]
                        ins = t.matmul(psA[:, slot * 512:(slot + 1) * 512], wsl,
                                       buf[:, (hc % 32) * 1024 + tq2 * 512:(hc % 32) * 1024 + (tq2 + 1) * 512],
                                       start=(hc == 0), stop=(hc == NHC - 1))
                    b.inc(ins, 'sT', 1, ('dn', th, dc, tq2))
                b.mark('sT', ('dnmm', th, dc))

    def p6_vector(v):
        for th in range(2):
            for dc in range(NCH):
                b.wait(v, ('x26', th, dc))
                for tq2 in range(2):
                    slot = (2 * dc + tq2) % 4
                    b.wait(v, ('dn', th, dc, tq2))
                    if dc >= 2:
                        b.wait(v, ('ost', th, dc - 2))
                    elif th == 1:
                        b.wait(v, ('ost', 0, NCH - 2 + dc))
                    b.inc(v.tensor_tensor(BVf[:, PBf6[dc % 2] + tq2 * 512:PBf6[dc % 2] + (tq2 + 1) * 512],
                                          psA[:, slot * 512:(slot + 1) * 512],
                                          BKf[:, XS6[dc % 2] + tq2 * 512:XS6[dc % 2] + (tq2 + 1) * 512],
                                          ALU.add), 'sV', 1, ('dnv', th, dc, tq2))

    b.phase('P6', [('sync', p6_sync), ('gpsimd', p6_gpsimd), ('tensor', p6_tensor),
                   ('vector', p6_vector)])
    es.close()
    return nc, {'out': out}



_NCACHE = {}


def _get_program(tap=None):
    if tap not in _NCACHE:
        _NCACHE[tap] = build_program(tap)
    return _NCACHE[tap]


def make_inputs(L0_attn_x, L0_attn_wq, L0_attn_wk, L0_attn_wv, L0_attn_wo,
                L0_ffn_w_gate, L0_ffn_w_up, L0_ffn_w_down):
    pq = _perm_q()
    pk = _perm_k()
    cos, sin = _rope_tables()
    wqT = np.ascontiguousarray(np.asarray(L0_attn_wq).T[:, pq], dtype=np.float32)
    wkT = np.ascontiguousarray(np.asarray(L0_attn_wk).T[:, pk], dtype=np.float32)
    wvT = np.ascontiguousarray(np.asarray(L0_attn_wv).T, dtype=np.float32)
    woT = np.ascontiguousarray(np.asarray(L0_attn_wo).T, dtype=np.float32)
    wgT = np.ascontiguousarray(np.asarray(L0_ffn_w_gate).T, dtype=np.float32)
    wuT = np.ascontiguousarray(np.asarray(L0_ffn_w_up).T, dtype=np.float32)
    wdT = np.ascontiguousarray(np.asarray(L0_ffn_w_down).T, dtype=np.float32)
    diag = np.where(np.arange(128)[None, :] > np.arange(128)[:, None],
                    np.float32(NEG), np.float32(0.0)).astype(np.float32)
    shared = dict(wq=wqT, wk=wkT, wv=wvT, wo=woT, wg=wgT, wu=wuT, wd=wdT,
                  cos_in=cos, sin_in=sin,
                  ident_in=np.eye(128, dtype=np.float32),
                  diag_in=diag,
                  onesc_in=np.ones((128, 1), np.float32),
                  onesf_in=np.ones((128, 128), np.float32))
    x = np.asarray(L0_attn_x)
    in_maps = []
    for core in range(8):
        im = dict(shared)
        im['xf'] = np.ascontiguousarray(x[core].T, dtype=np.float32)
        in_maps.append(im)
    return in_maps


def kernel(**inputs):
    nc, outs = _get_program(None)
    in_maps = make_inputs(**inputs)
    res = run_bass_kernel_spmd(nc, in_maps, core_ids=list(range(8)))
    return np.stack([r['outf'].T for r in res.results], axis=0).astype(np.float32)


# version 10
# speedup vs baseline: 1.2787x; 1.0463x over previous
import sys
import numpy as np

sys.path.insert(0, '/opt/trn_rl_repo')

import concourse.bass as bass
import concourse.mybir as mybir
from concourse.ap import AP
from concourse.bass_utils import run_bass_kernel_spmd
from contextlib import ExitStack

F32 = mybir.dt.float32
BF16 = mybir.dt.bfloat16
AF = mybir.ActivationFunctionType
ALU = mybir.AluOpType
AX = mybir.AxisListType

D = 2048
T = 2048
H = 32
HD = 64
DH = 8192
NCH = 16
NHC = DH // 128
NTC = T // 128
NTQ = T // 512
NEG = -1.0e9


def _perm_q():
    perm = np.zeros(2048, np.int64)
    for c in range(8):
        for j in range(4):
            h = 4 * (j + 4 * (1 if c >= 4 else 0)) + (c % 4)
            for i in range(32):
                perm[c * 128 + 32 * j + i] = h * 64 + 2 * i
                perm[(c + 8) * 128 + 32 * j + i] = h * 64 + 2 * i + 1
    return perm


def _perm_k():
    perm = np.zeros(512, np.int64)
    for c in range(2):
        for j in range(4):
            g = 4 * c + j
            for i in range(32):
                perm[c * 128 + 32 * j + i] = g * 64 + 2 * i
                perm[(c + 2) * 128 + 32 * j + i] = g * 64 + 2 * i + 1
    return perm


def _rope_tables():
    theta = 1.0 / (10000.0 ** (np.arange(0, HD, 2, dtype=np.float64) / HD))
    t = np.arange(T, dtype=np.float64)
    fr = theta[:, None] * t[None, :]
    cos = np.cos(fr).astype(np.float32)
    sin = np.sin(fr).astype(np.float32)
    return np.tile(cos, (4, 1)), np.tile(sin, (4, 1))



class _Dry:
    def then_inc(self, *a, **k):
        return self

    def __getattr__(self, name):
        return lambda *a, **k: self

_DRY = _Dry()


class B:

    def __init__(self, nc, es):
        self.nc = nc
        self.es = es
        self.sems = {}
        self.cnt = {}
        self.marks = {}
        self.want = []
        self.emitting = False

    def sem(self, name):
        self.sems[name] = self.es.enter_context(self.nc.semaphore(name))
        self.cnt[name] = 0

    def inc(self, instr, s, k=1, label=None):
        self.cnt[s] += k
        if self.emitting and instr is not _DRY and k > 0:
            instr.then_inc(self.sems[s], k)
        if label is not None:
            if self.emitting:
                assert self.marks[label] == (s, self.cnt[s]), (label, self.marks.get(label), (s, self.cnt[s]))
            else:
                assert label not in self.marks, label
                self.marks[label] = (s, self.cnt[s])
        return instr

    def mark(self, s, label):
        return self.inc(_DRY, s, 0, label)

    def wait(self, eng, label):
        if not self.emitting:
            self.want.append(label)
            return
        s, v = self.marks[label]
        eng.wait_ge(self.sems[s], v)

    def wait_maybe(self, eng, label):
        if not self.emitting:
            return
        if label in self.marks:
            s, v = self.marks[label]
            eng.wait_ge(self.sems[s], v)

    def phase(self, name, emitters):
        snap = dict(self.cnt)
        self.emitting = False
        self.want = []
        for en, fn in emitters:
            fn(_DRY)
        missing = [w for w in self.want if w not in self.marks]
        assert not missing, ('unresolved waits', missing[:5])
        self.cnt = snap
        self.emitting = True
        with self.nc.Block(name) as blk:
            for en, fn in emitters:
                getattr(blk, en)(fn)



def build_program(tap=None):
    nc = bass.Bass('TRN2', target_bir_lowering=False, detect_race_conditions=False)

    xf = nc.dram_tensor('xf', [D, T], F32, kind='ExternalInput')
    wq = nc.dram_tensor('wq', [D, D], F32, kind='ExternalInput')
    wk = nc.dram_tensor('wk', [D, 512], F32, kind='ExternalInput')
    wv = nc.dram_tensor('wv', [D, 512], F32, kind='ExternalInput')
    wo = nc.dram_tensor('wo', [D, D], F32, kind='ExternalInput')
    wg = nc.dram_tensor('wg', [D, DH], F32, kind='ExternalInput')
    wu = nc.dram_tensor('wu', [D, DH], F32, kind='ExternalInput')
    wd = nc.dram_tensor('wd', [DH, D], F32, kind='ExternalInput')
    cos_in = nc.dram_tensor('cos_in', [128, T], F32, kind='ExternalInput')
    sin_in = nc.dram_tensor('sin_in', [128, T], F32, kind='ExternalInput')
    ident_in = nc.dram_tensor('ident_in', [128, 128], F32, kind='ExternalInput')
    diag_in = nc.dram_tensor('diag_in', [128, 128], F32, kind='ExternalInput')
    onesc_in = nc.dram_tensor('onesc_in', [128, 1], F32, kind='ExternalInput')
    onesf_in = nc.dram_tensor('onesf_in', [128, 128], F32, kind='ExternalInput')

    x2h = nc.dram_tensor('x2h', [D, T], F32, kind='Internal')
    gth = nc.dram_tensor('gth', [DH, T], BF16, kind='Internal')

    def fm_pair(dram, sbuf_ap, nch, w):
        return (AP(dram, 0, [[w * 1, 128] if False else [dram.shape[1], 128],
                             [128 * dram.shape[1], nch], [1, w]]),
                sbuf_ap.rearrange('p (c t) -> p c t', c=nch))

    taps = {}
    out = None
    if tap is None:
        out = nc.dram_tensor('outf', [D, T], F32, kind='ExternalOutput')

    es = ExitStack()
    sb = lambda n, s, d: es.enter_context(nc.sbuf_tensor(n, s, d))

    BH = sb('BH', [128, NCH * T], BF16)
    BQ = sb('BQ', [128, NCH * T], BF16)
    BK = sb('BK', [128, 4 * T], BF16)
    BV = sb('BV', [128, NTC * 512], BF16)
    cosT = sb('cosT', [128, T], F32)
    sinT = sb('sinT', [128, T], F32)
    identb = sb('identb', [128, 128], BF16)
    diagm = sb('diagm', [128, 128], F32)
    onesc = sb('onesc', [128, 1], BF16)
    onesf = sb('onesf', [128, 128], F32)
    identf = sb('identf', [128, 128], F32)
    STT = sb('STT', [128, 160], F32)
    SCR = sb('SCR', [128, 5120], F32)
    SCRb = SCR.bitcast(BF16)
    BKf = BK.bitcast(F32)
    BVf = BV.bitcast(F32)
    cosTb = cosT.bitcast(BF16)
    sinTb = sinT.bitcast(BF16)
    mt = [sb('m%d' % i, [128, 1], F32) for i in range(2)]
    nmt = [sb('nm%d' % i, [128, 1], F32) for i in range(2)]
    lt = [sb('l%d' % i, [128, 1], F32) for i in range(2)]
    rt = [sb('r%d' % i, [128, 1], F32) for i in range(2)]

    RP = [4096, 4608]
    WSLOT = [0, 2048, 4096, 6144]
    PHB = [0, 2048]
    PTB = [0, 2048]
    GSL = [0, 2048]
    SIL = [4096, 4608]
    SQ1 = [0, 512]
    DGP = [8192, 8320]
    XS = [0, 2048]
    PBf = [0, 2048]

    psA = es.enter_context(nc.psum_tensor('psA', [128, 2048], F32))
    psB = es.enter_context(nc.psum_tensor('psB', [128, 2048], F32))
    psAb = psA.bitcast(BF16)
    psBb = psB.bitcast(BF16)

    b = B(nc, es)
    for s in ('sG', 'sS', 'sT', 'sV', 'sC'):
        b.sem(s)

    def R(i, lo=0, hi=16):
        return STT[:, i * 16 + lo:i * 16 + hi]

    def step(eng, sem, instr):
        b.inc(instr, sem, 1)
        if b.emitting:
            eng.wait_ge(b.sems[sem], b.cnt[sem])
        return instr

    def ln_finish_vec(v, ph):
        step(v, 'sV', v.tensor_scalar(R(2), R(0), 1.0 / D, None, ALU.mult))
        step(v, 'sV', v.tensor_scalar(R(3), R(1), 1.0 / D, None, ALU.mult))
        step(v, 'sV', v.tensor_tensor(R(4), R(2), R(2), ALU.mult))
        step(v, 'sV', v.tensor_tensor(R(4), R(3), R(4), ALU.subtract))
        step(v, 'sV', v.tensor_scalar(R(4), R(4), 1e-5, None, ALU.add))
        b.inc(v.reciprocal(R(5), R(4)), 'sV', 1, ph + '_rec')
        b.wait(v, ph + '_sqrt')
        step(v, 'sV', v.tensor_tensor(R(5), R(6), R(6), ALU.mult))
        step(v, 'sV', v.tensor_tensor(R(5), R(5), R(4), ALU.mult))
        step(v, 'sV', v.tensor_scalar(R(5), R(5), -1.0, 3.0, ALU.mult, ALU.add))
        step(v, 'sV', v.tensor_tensor(R(5), R(6), R(5), ALU.mult))
        step(v, 'sV', v.tensor_scalar(R(7), R(5), 0.5, None, ALU.mult))
        step(v, 'sV', v.tensor_tensor(R(8), R(2), R(7), ALU.mult))
        b.inc(v.tensor_scalar(R(8), R(8), -1.0, None, ALU.mult), 'sV', 1, ph + '_ab')
        if b.emitting:
            v.wait_ge(b.sems['sV'], b.cnt['sV'])

    def ln_finish_sca(s, ph):
        b.wait(s, ph + '_rec')
        b.inc(s.activation(R(6), R(5), AF.Sqrt), 'sC', 1, ph + '_sqrt')

    DGB = [RP[0], RP[0] + 128]

    def bcast_vec(v, ph):
        b.wait(v, 'c2_done')
        b.wait(v, ph + '_ab')
        for idx in range(32):
            tc, isb = idx // 2, idx % 2
            if idx >= 2:
                b.wait(v, (ph + '_bcm', idx - 2))
            col = R(8 if isb else 7, tc, tc + 1)
            b.inc(v.tensor_scalar(SCR[:, DGB[idx % 2]:DGB[idx % 2] + 128],
                                  identf[:, :], col, None, ALU.mult),
                  'sV', 1, (ph + '_dg', idx))

    def bcast_tensor(t, ph):
        for idx in range(32):
            tc, isb = idx // 2, idx % 2
            b.wait(t, (ph + '_dg', idx))
            ps = psB if isb else psA
            ins = t.matmul(ps[:, tc * 128:(tc + 1) * 128], onesf[:, :],
                           SCR[:, DGB[idx % 2]:DGB[idx % 2] + 128],
                           start=True, stop=True)
            b.inc(ins, 'sT', 1, (ph + '_bcm', idx))
        b.mark('sT', ph + '_bc')

    def stats_mms(t, ph, src_of, sq_of, sqwait):
        for wave in range(4):
            if wave >= 1:
                b.wait(t, (ph + '_evw', wave - 1))
            for c in range(NCH):
                b.wait(t, (ph + '_st_x', c) if ph == 'ln1' else ('x2bf', c))
                b.wait(t, sqwait(wave, c))
                for tbl in range(4):
                    tb = wave * 4 + tbl
                    t.matmul(psA[:, tbl * 512 + wave:tbl * 512 + wave + 1],
                             src_of(c)[:, tb * 128:(tb + 1) * 128],
                             onesc[:, :], start=(c == 0), stop=(c == NCH - 1))
                    ins = t.matmul(psB[:, tbl * 512 + wave:tbl * 512 + wave + 1],
                                   sq_of(c)[:, tbl * 128:(tbl + 1) * 128],
                                   onesc[:, :], start=(c == 0), stop=(c == NCH - 1))
                b.inc(ins, 'sT', 1, (ph + '_sw', wave, c))
            b.inc(_DRY, 'sT', 0, (ph + '_wave', wave))

    def stats_evac(v, ph):
        for wave in range(4):
            b.wait(v, (ph + '_wave', wave))
            b.inc(v.tensor_copy(R(0, wave * 4, wave * 4 + 4),
                                AP(psA, wave, [[2048, 128], [512, 4]])), 'sV', 1)
            b.inc(v.tensor_copy(R(1, wave * 4, wave * 4 + 4),
                                AP(psB, wave, [[2048, 128], [512, 4]])),
                  'sV', 1, (ph + '_evw', wave))
        if b.emitting:
            v.wait_ge(b.sems['sV'], b.cnt['sV'])
        b.mark('sV', ph + '_ev')

    def normalize(v, ph, src_off, dst, c):
        for tq in range(NTQ):
            sl = slice(tq * 512, (tq + 1) * 512)
            step(v, 'sV', v.tensor_tensor(SCR[:, RP[0]:RP[0] + 512],
                                          BKf[:, src_off + tq * 512:src_off + (tq + 1) * 512],
                                          psA[:, sl], ALU.mult))
            ins = v.tensor_tensor(dst[:, c * T + tq * 512:c * T + (tq + 1) * 512],
                                  SCR[:, RP[0]:RP[0] + 512], psB[:, sl], ALU.add)
            if tq < NTQ - 1:
                step(v, 'sV', ins)
            else:
                b.inc(ins, 'sV', 1)
                if b.emitting:
                    v.wait_ge(b.sems['sV'], b.cnt['sV'])
        return ins

    def p1_gpsimd(g):
        b.inc(g.dma_start(out=identb[:, :], in_=ident_in[:, :]), 'sG', 16)
        b.inc(g.dma_start(out=onesc[:, :], in_=onesc_in[:, :]), 'sG', 16, 'c_done')
        for c in range(NCH):
            b.inc(g.dma_start(out=BQ[:, c * T:(c + 1) * T],
                              in_=xf[c * 128:(c + 1) * 128, :]), 'sG', 16, ('xbf', c))

    def p1_sync(s):
        b.inc(s.dma_start(out=cosT[:, :], in_=cos_in[:, :]), 'sS', 16)
        b.inc(s.dma_start(out=sinT[:, :], in_=sin_in[:, :]), 'sS', 16)
        b.inc(s.dma_start(out=diagm[:, :], in_=diag_in[:, :]), 'sS', 16)
        b.inc(s.dma_start(out=onesf[:, :], in_=onesf_in[:, :]), 'sS', 16)
        b.inc(s.dma_start(out=identf[:, :], in_=ident_in[:, :]), 'sS', 16, 'c2_done')
        for c in range(NCH):
            if c >= 2:
                b.wait(s, ('norm1', c - 2))
            b.inc(s.dma_start(out=BKf[:, XS[c % 2]:XS[c % 2] + 2048],
                              in_=xf[c * 128:(c + 1) * 128, :]), 'sS', 16, ('xin', c))

    def p1_scalar(s):
        b.wait(s, 'c_done')
        for wave in range(4):
            for c in range(NCH):
                b.wait(s, ('xbf', c))
                idx = wave * NCH + c
                if idx >= 2:
                    pidx = idx - 2
                    b.wait(s, ('ln1_sw', pidx // NCH, pidx % NCH))
                b.inc(s.activation(SCRb[:, SQ1[c % 2]:SQ1[c % 2] + 512],
                                   BQ[:, c * T + wave * 512:c * T + wave * 512 + 512],
                                   AF.Square), 'sC', 1, ('sq1', wave, c))
        ln_finish_sca(s, 'ln1')

    def p1_tensor(t):
        for c in range(NCH):
            b.wait(t, ('xbf', c))
            b.mark('sT', ('ln1_st_x', c))
        stats_mms(t, 'ln1', lambda c: BQ[:, c * T:(c + 1) * T],
                  lambda c: SCRb[:, SQ1[c % 2]:SQ1[c % 2] + 512],
                  lambda wave, c: ('sq1', wave, c))
        bcast_tensor(t, 'ln1')

    def p1_vector(v):
        stats_evac(v, 'ln1')
        ln_finish_vec(v, 'ln1')
        bcast_vec(v, 'ln1')
        b.wait(v, 'ln1_bc')
        for c in range(NCH):
            b.wait(v, ('xin', c))
            normalize(v, 'ln1', XS[c % 2], BH, c)
            b.mark('sV', ('norm1', c))

    b.phase('P1', [('gpsimd', p1_gpsimd), ('sync', p1_sync), ('scalar', p1_scalar),
                   ('tensor', p1_tensor), ('vector', p1_vector)])

    if tap == 'h':
        taps['h'] = nc.dram_tensor('tap_h', [D, T], BF16, kind='ExternalOutput')
        taps['stt'] = nc.dram_tensor('tap_stt', [128, 160], F32, kind='ExternalOutput')

        def ts_(s):
            o_ap, i_ap = fm_pair(taps['h'], BH[:, :], NCH, T)
            b.inc(s.dma_start(out=o_ap, in_=i_ap), 'sS', 16, 'tap_st')
            b.inc(s.dma_start(out=taps['stt'][:, :], in_=STT[:, :]), 'sS', 16, 'tap_st2')
            b.wait(s, 'tap_st2')
        b.phase('TAP', [('sync', ts_)])
        es.close()
        return nc, taps

    def wblk_ap(w, ncols, cblk, blkw=128):
        rows = w.shape[0]
        return AP(w, cblk * blkw, [[ncols, 128], [128 * ncols, rows // 128], [1, blkw]])

    def wslot(i, n=2048):
        return SCRb[:, WSLOT[i]:WSLOT[i] + n]

    def wslot_r(i, k=16, n=2048):
        return SCRb[:, WSLOT[i]:WSLOT[i] + n].rearrange('p (k n) -> p k n', k=k)

    def p2_gpsimd(g):
        for p in range(8):
            if p >= 2:
                b.wait(g, ('qmm', p - 2, NTQ - 1))
            b.inc(g.dma_start(out=wslot_r(2 * (p % 2)), in_=wblk_ap(wq, D, p)),
                  'sG', 16, ('wqe', p))
            b.inc(g.dma_start(out=wslot_r(2 * (p % 2) + 1), in_=wblk_ap(wq, D, p + 8)),
                  'sG', 16, ('wqo', p))
        for c in range(4):
            b.inc(g.dma_start(out=BV[:, c * 2048:(c + 1) * 2048]
                              .rearrange('p (k n) -> p k n', k=16),
                              in_=wblk_ap(wk, 512, c)), 'sG', 16, ('wk', c))
        b.wait(g, ('qmm', 6, NTQ - 1))
        b.inc(g.dma_start(out=SCRb[:, WSLOT[0]:WSLOT[0] + 4096]
                          .rearrange('p (k n) -> p k n', k=16),
                          in_=wblk_ap(wv, 512, 0, 256)), 'sG', 16, ('wv', 0))
        b.wait(g, ('qmm', 7, NTQ - 1))
        b.inc(g.dma_start(out=SCRb[:, WSLOT[2]:WSLOT[2] + 4096]
                          .rearrange('p (k n) -> p k n', k=16),
                          in_=wblk_ap(wv, 512, 1, 256)), 'sG', 16, ('wv', 1))

    def p2_tensor(t):
        for p in range(8):
            b.wait(t, ('wqe', p))
            b.wait(t, ('wqo', p))
            for tq in range(NTQ):
                if p >= 1:
                    b.wait(t, ('rope_q', p - 1, tq))
                se, so = 2 * (p % 2), 2 * (p % 2) + 1
                for K in range(NCH):
                    t.matmul(psA[:, tq * 512:(tq + 1) * 512],
                             wslot(se)[:, K * 128:(K + 1) * 128],
                             BH[:, K * T + tq * 512:K * T + (tq + 1) * 512],
                             start=(K == 0), stop=(K == NCH - 1))
                for K in range(NCH):
                    ins = t.matmul(psB[:, tq * 512:(tq + 1) * 512],
                                   wslot(so)[:, K * 128:(K + 1) * 128],
                                   BH[:, K * T + tq * 512:K * T + (tq + 1) * 512],
                                   start=(K == 0), stop=(K == NCH - 1))
                b.inc(ins, 'sT', 1, ('qmm', p, tq))
        for p in range(2):
            b.wait(t, ('wk', p))
            b.wait(t, ('wk', p + 2))
            for tq in range(NTQ):
                b.wait(t, ('rope_q', 7, tq))
                if p == 1:
                    b.wait(t, ('rope_k', 0, tq))
                for K in range(NCH):
                    t.matmul(psA[:, tq * 512:(tq + 1) * 512],
                             BV[:, p * 2048 + K * 128:p * 2048 + (K + 1) * 128],
                             BH[:, K * T + tq * 512:K * T + (tq + 1) * 512],
                             start=(K == 0), stop=(K == NCH - 1))
                for K in range(NCH):
                    ins = t.matmul(psB[:, tq * 512:(tq + 1) * 512],
                                   BV[:, (p + 2) * 2048 + K * 128:(p + 2) * 2048 + (K + 1) * 128],
                                   BH[:, K * T + tq * 512:K * T + (tq + 1) * 512],
                                   start=(K == 0), stop=(K == NCH - 1))
                b.inc(ins, 'sT', 1, ('kmm', p, tq))
        for tc in range(NTC):
            for oh in range(2):
                b.wait(t, ('wv', oh))
                b.wait(t, ('rope_k', 1, NTQ - 1))
                if tc >= 2:
                    b.wait(t, ('vev', tc - 2, oh))
                slot = (2 * tc + oh) % 4
                woff = WSLOT[0] if oh == 0 else WSLOT[2]
                for K in range(NCH):
                    ins = t.matmul(psA[:, slot * 512:slot * 512 + 256],
                                   BH[:, K * T + tc * 128:K * T + (tc + 1) * 128],
                                   SCRb[:, woff + K * 256:woff + (K + 1) * 256],
                                   start=(K == 0), stop=(K == NCH - 1))
                b.inc(ins, 'sT', 1, ('vmm', tc, oh))

    def _rope(v, p, tq, is_q):
        sl = slice(tq * 512, (tq + 1) * 512)
        cs = cosT[:, tq * 512:(tq + 1) * 512]
        sn = sinT[:, tq * 512:(tq + 1) * 512]
        t1 = SCR[:, RP[0]:RP[0] + 512]
        t2 = SCR[:, RP[1]:RP[1] + 512]
        dst, off = (BQ, 8) if is_q else (BK, 2)
        de = dst[:, p * T + tq * 512:p * T + (tq + 1) * 512]
        do = dst[:, (p + off) * T + tq * 512:(p + off) * T + (tq + 1) * 512]
        b.inc(v.tensor_tensor(t1, psA[:, sl], cs, ALU.mult), 'sV', 1)
        step(v, 'sV', v.tensor_tensor(t2, psB[:, sl], sn, ALU.mult))
        step(v, 'sV', v.tensor_tensor(de, t1, t2, ALU.subtract))
        b.inc(v.tensor_tensor(t1, psA[:, sl], sn, ALU.mult), 'sV', 1)
        step(v, 'sV', v.tensor_tensor(t2, psB[:, sl], cs, ALU.mult))
        ins = v.tensor_tensor(do, t1, t2, ALU.add)
        b.inc(ins, 'sV', 1)
        if b.emitting:
            v.wait_ge(b.sems['sV'], b.cnt['sV'])
        return ins

    def p2_vector(v):
        for p in range(8):
            for tq in range(NTQ):
                b.wait(v, ('qmm', p, tq))
                _rope(v, p, tq, True)
                b.mark('sV', ('rope_q', p, tq))
        for p in range(2):
            for tq in range(NTQ):
                b.wait(v, ('kmm', p, tq))
                _rope(v, p, tq, False)
                b.mark('sV', ('rope_k', p, tq))
        for tc in range(NTC):
            for oh in range(2):
                b.wait(v, ('vmm', tc, oh))
                slot = (2 * tc + oh) % 4
                b.inc(v.tensor_copy(BV[:, tc * 512 + oh * 256:tc * 512 + oh * 256 + 256],
                                    psA[:, slot * 512:slot * 512 + 256]),
                      'sV', 1, ('vev', tc, oh))

    b.phase('P2', [('gpsimd', p2_gpsimd), ('tensor', p2_tensor), ('vector', p2_vector)])

    if tap == 'qkv':
        taps['q'] = nc.dram_tensor('tap_q', [D, T], BF16, kind='ExternalOutput')
        taps['k'] = nc.dram_tensor('tap_k', [512, T], BF16, kind='ExternalOutput')
        taps['v'] = nc.dram_tensor('tap_v', [T, 512], BF16, kind='ExternalOutput')

        def ts_(s):
            o_ap, i_ap = fm_pair(taps['q'], BQ[:, :], NCH, T)
            b.inc(s.dma_start(out=o_ap, in_=i_ap), 'sS', 16, 'tap_qst')
            o_ap, i_ap = fm_pair(taps['k'], BK[:, :], 4, T)
            b.inc(s.dma_start(out=o_ap, in_=i_ap), 'sS', 16, 'tap_kst')
            o_ap, i_ap = fm_pair(taps['v'], BV[:, :], NTC, 512)
            b.inc(s.dma_start(out=o_ap, in_=i_ap), 'sS', 16, 'tap_vst')
            b.wait(s, 'tap_vst')
        b.phase('TAP', [('sync', ts_)])
        es.close()
        return nc, taps

    ITERS = [(h, qt) for h in range(H) for qt in range(NTC)]

    def own_of(i):
        return (psA, psAb) if i % 2 == 0 else (psB, psBb)

    def pv_loc(i):
        h, qt = ITERS[i]
        S = (qt + 1) * 128
        ps = own_of(i)[0] if S <= 1920 else own_of(i + 1)[0]
        return ps[0:64, 1920:2048] if h % 2 == 0 else ps[64:128, 1920:2048]

    PT3 = [0, 2048, 4096]

    def phb3(i):
        k = i % 3
        if k < 2:
            return cosTb[:, k * 2048:k * 2048 + 2048]
        return SCRb[:, 6144:8192]

    def S_of(i):
        return (ITERS[i][1] + 1) * 128

    def vcut_of(i):
        return 512 if (S_of(i) > 512 and i % 2 == 0) else 0

    def pv_loc(i):
        h, qt = ITERS[i]
        ps = own_of(i + 1)[0] if S_of(i) <= 1920 else own_of(i)[0]
        return ps[0:64, 1920:2048] if h % 2 == 0 else ps[64:128, 1920:2048]

    def emit_tp(t, i):
        S = S_of(i)
        par = i % 2
        dst = own_of(i + 1)[0]
        b.wait(t, ('dg', i))
        if i + 1 < len(ITERS):
            b.wait(t, ('exp', i + 1))
        pb = phb3(i)
        for sj in range(S // 128):
            ins = t.matmul(dst[:, sj * 128:(sj + 1) * 128],
                           pb[:, sj * 128:(sj + 1) * 128],
                           SCRb[:, DGP[par]:DGP[par] + 128],
                           start=True, stop=True)
        b.inc(ins, 'sT', 1, ('tp', i))

    def emit_pv(t, i):
        h, qt = ITERS[i]
        S = S_of(i)
        g = h // 4
        b.wait(t, ('ev', i))
        b.wait(t, ('ev_v', i))
        loc = pv_loc(i)
        tpos = (0, 0) if h % 2 == 0 else (0, 64)
        for sj in range(S // 128):
            ins = t.matmul(loc, BV[:, sj * 512 + g * 64:sj * 512 + g * 64 + 64],
                           SCRb[:, PT3[i % 3] + sj * 128:PT3[i % 3] + (sj + 1) * 128],
                           start=(sj == 0), stop=(sj == S // 128 - 1), tile_position=tpos)
        b.inc(ins, 'sT', 1, ('pv', i))

    def p3_tensor(t):
        n = len(ITERS)
        for wmu in range(30):
            t.matmul(psA[:, 0:512], identb[:, :], BQ[:, 0:512],
                     start=True, stop=True)
        for i in range(n + 3):
            if 0 <= i - 3 < n:
                emit_pv(t, i - 3)
            if i < n:
                h, qt = ITERS[i]
                S = (qt + 1) * 128
                g = h // 4
                j = g % 4
                m = h % 4
                c_ev = m + 4 * (1 if g >= 4 else 0)
                own = own_of(i)[0]
                if i >= 2:
                    b.wait(t, ('exp', i - 2))
                if i >= 3:
                    b.wait(t, ('ev', i - 3))
                    b.wait(t, ('ev_v', i - 3))
                    if S > 1536:
                        b.wait(t, ('ye', i - 3))
                rows = slice(32 * j, 32 * j + 32)
                for st in range((S + 511) // 512):
                    w = min(512, S - st * 512)
                    t.matmul(own[:, st * 512:st * 512 + w],
                             BQ[rows, c_ev * T + qt * 128:c_ev * T + (qt + 1) * 128],
                             BK[rows, (g // 4) * T + st * 512:(g // 4) * T + st * 512 + w],
                             start=True, stop=False, tile_position=(32 * j, 0))
                    ins = t.matmul(own[:, st * 512:st * 512 + w],
                                   BQ[rows, (c_ev + 8) * T + qt * 128:(c_ev + 8) * T + (qt + 1) * 128],
                                   BK[rows, (g // 4 + 2) * T + st * 512:(g // 4 + 2) * T + st * 512 + w],
                                   start=False, stop=True, tile_position=(32 * j, 0))
                b.inc(ins, 'sT', 1, ('sc', i))
            if 0 <= i - 2 < n:
                emit_tp(t, i - 2)

    def p3_vector(v):
        n = len(ITERS)
        for i in range(n + 3):
            if 0 <= i - 1 < n:
                j = i - 1
                pj = j % 2
                b.wait(v, ('exp', j))
                if j >= 2:
                    b.wait(v, ('tp', j - 2))
                b.inc(v.reciprocal(rt[pj][:, :], lt[pj][:, :]), 'sV', 1, ('rc', j))
                b.wait(v, ('rc', j))
                b.inc(v.tensor_scalar(SCRb[:, DGP[pj]:DGP[pj] + 128], identb[:, :],
                                      rt[pj][:, :], None, ALU.mult), 'sV', 1, ('dg', j))
            if 0 <= i - 3 < n:
                j = i - 3
                hp, qp = ITERS[j]
                b.wait(v, ('pv', j))
                rws = slice(0, 64) if hp % 2 == 0 else slice(64, 128)
                b.inc(v.tensor_copy(BH[rws, (hp // 2) * T + qp * 128:(hp // 2) * T + (qp + 1) * 128],
                                    pv_loc(j)), 'sV', 1, ('ye', j))
            if 0 <= i - 2 < n:
                j = i - 2
                vc = vcut_of(j)
                b.wait(v, ('tp', j))
                if vc:
                    if j >= 3:
                        b.wait(v, ('pv', j - 3))
                    b.inc(v.tensor_copy(SCRb[:, PT3[j % 3]:PT3[j % 3] + vc],
                                        own_of(j + 1)[0][:, 0:vc]), 'sV', 1, ('ev_v', j))
                else:
                    b.mark('sV', ('ev_v', j))
            if i < n:
                S = S_of(i)
                own = own_of(i)[0]
                par = i % 2
                b.wait(v, ('sc', i))
                step(v, 'sV', v.tensor_tensor(own[:, S - 128:S], own[:, S - 128:S],
                                              diagm[:, :], ALU.add))
                b.inc(v.reduce_max(mt[par][:, :], own[:, 0:S], axis=AX.X), 'sV', 1, ('red', i))
                b.wait(v, ('red', i))
                b.inc(v.tensor_scalar(nmt[par][:, :], mt[par][:, :], -0.125, None, ALU.mult),
                      'sV', 1, ('nm', i))

    def p3_scalar(s):
        n = len(ITERS)
        for i in range(n + 3):
            if 0 <= i - 2 < n:
                j = i - 2
                S = S_of(j)
                vc = vcut_of(j)
                b.wait(s, ('tp', j))
                if j >= 3:
                    b.wait(s, ('pv', j - 3))
                b.inc(s.copy(SCRb[:, PT3[j % 3] + vc:PT3[j % 3] + S],
                             own_of(j + 1)[0][:, vc:S]), 'sC', 1, ('ev', j))
            if i < n:
                S = S_of(i)
                par = i % 2
                b.wait(s, ('nm', i))
                if i >= 3:
                    b.wait(s, ('tp', i - 3))
                b.inc(s.activation(phb3(i)[:, 0:S], own_of(i)[0][:, 0:S],
                                   AF.Exp, bias=nmt[par][:, :], scale=0.125,
                                   accum_out=lt[par][:, :]), 'sC', 1, ('exp', i))

    b.phase('P3', [('tensor', p3_tensor), ('vector', p3_vector), ('scalar', p3_scalar)])

    if tap == 'y':
        taps['y'] = nc.dram_tensor('tap_y', [D, T], BF16, kind='ExternalOutput')

        def ts_(s):
            o_ap, i_ap = fm_pair(taps['y'], BH[:, :], NCH, T)
            b.inc(s.dma_start(out=o_ap, in_=i_ap), 'sS', 16, 'tap_st')
            b.wait(s, 'tap_st')
        b.phase('TAP', [('sync', ts_)])
        es.close()
        return nc, taps

    def p4_gpsimd(g):
        for ec in range(NCH):
            if ec >= 2:
                b.wait(g, ('wo_mm', ec - 2, NTQ - 1))
            b.inc(g.dma_start(out=wslot_r(ec % 2), in_=wblk_ap(wo, D, ec)),
                  'sG', 16, ('wo', ec))

    def p4_sync(s):
        for ec in range(NCH):
            if ec >= 2:
                b.wait(s, ('x2v', ec - 2, NTQ - 1))
            b.inc(s.dma_start(out=BKf[:, XS[ec % 2]:XS[ec % 2] + 2048],
                              in_=xf[ec * 128:(ec + 1) * 128, :]), 'sS', 16, ('x4', ec))
            b.wait(s, ('x2bf', ec))
            b.inc(s.dma_start(out=x2h[ec * 128:(ec + 1) * 128, :],
                              in_=BVf[:, PBf[ec % 2]:PBf[ec % 2] + 2048]), 'sS', 16, ('x2st', ec))
        if tap in ('x2', 'gated'):
            b.wait(s, ('x2bf', NCH - 1))

    def p4_tensor(t):
        for ec in range(NCH):
            b.wait(t, ('wo', ec))
            for tq in range(NTQ):
                if ec >= 1:
                    b.wait(t, ('x2v', ec - 1, tq))
                for K in range(NCH):
                    ins = t.matmul(psA[:, tq * 512:(tq + 1) * 512],
                                   wslot(ec % 2)[:, K * 128:(K + 1) * 128],
                                   BH[:, K * T + tq * 512:K * T + (tq + 1) * 512],
                                   start=(K == 0), stop=(K == NCH - 1))
                b.inc(ins, 'sT', 1, ('wo_mm', ec, tq))
        for tq in range(NTQ):
            b.wait(t, ('x2v', NCH - 1, tq))
        stats_mms(t, 'ln2', lambda c: BQ[:, c * T:(c + 1) * T],
                  lambda c: SCRb[:, SQ1[c % 2]:SQ1[c % 2] + 512],
                  lambda wave, c: ('sq2', wave, c))
        bcast_tensor(t, 'ln2')

    def p4_vector(v):
        for ec in range(NCH):
            for tq in range(NTQ):
                b.wait(v, ('wo_mm', ec, tq))
                if tq == 0:
                    b.wait(v, ('x4', ec))
                    if ec >= 2:
                        b.wait(v, ('x2st', ec - 2))
                        b.wait(v, ('x2bf', ec - 2))
                b.inc(v.tensor_tensor(BVf[:, PBf[ec % 2] + tq * 512:PBf[ec % 2] + (tq + 1) * 512],
                                      psA[:, tq * 512:(tq + 1) * 512],
                                      BKf[:, XS[ec % 2] + tq * 512:XS[ec % 2] + (tq + 1) * 512],
                                      ALU.add), 'sV', 1, ('x2v', ec, tq))
        stats_evac(v, 'ln2')
        ln_finish_vec(v, 'ln2')
        bcast_vec(v, 'ln2')

    def p4_scalar(s):
        for ec in range(NCH):
            b.wait(s, ('x2v', ec, NTQ - 1))
            b.inc(s.copy(BQ[:, ec * T:(ec + 1) * T],
                         BVf[:, PBf[ec % 2]:PBf[ec % 2] + 2048]), 'sC', 1, ('x2bf', ec))
        for wave in range(4):
            for ec in range(NCH):
                idx = wave * NCH + ec
                if idx >= 2:
                    pidx = idx - 2
                    b.wait(s, ('ln2_sw', pidx // NCH, pidx % NCH))
                b.wait(s, ('x2bf', ec))
                b.inc(s.activation(SCRb[:, SQ1[ec % 2]:SQ1[ec % 2] + 512],
                                   BQ[:, ec * T + wave * 512:ec * T + wave * 512 + 512],
                                   AF.Square), 'sC', 1, ('sq2', wave, ec))
        ln_finish_sca(s, 'ln2')

    b.phase('P4', [('gpsimd', p4_gpsimd), ('sync', p4_sync), ('tensor', p4_tensor),
                   ('vector', p4_vector), ('scalar', p4_scalar)])

    if tap == 'x2':
        taps['x2o'] = nc.dram_tensor('tap_x2o', [D, T], F32, kind='ExternalOutput')

        def ts_(s):
            for ec in range(NCH):
                b.inc(s.dma_start(out=taps['x2o'][ec * 128:(ec + 1) * 128, :],
                                  in_=x2h[ec * 128:(ec + 1) * 128, :]), 'sS', 16, ('x2o', ec))
            b.wait(s, ('x2o', NCH - 1))
        b.phase('TAP', [('sync', ts_)])
        es.close()
        return nc, taps

    def p5_sync(s):
        for c in range(NCH):
            if c >= 2:
                b.wait(s, ('h2', c - 2))
            b.inc(s.dma_start(out=BKf[:, XS[c % 2]:XS[c % 2] + 2048],
                              in_=x2h[c * 128:(c + 1) * 128, :]), 'sS', 16, ('x25', c))
        for hc in range(NHC):
            b.wait(s, ('gsl', hc))
            b.inc(s.dma_start(out=gth[hc * 128:(hc + 1) * 128, :],
                              in_=BV[:, GSL[hc % 2]:GSL[hc % 2] + 2048]),
                  'sS', 16, ('gst', hc))

    def p5_gpsimd(g):
        for hc in range(NHC):
            if hc >= 2:
                b.wait(g, ('u_mm', hc - 2, NTQ - 1))
            sl = (hc % 2) * 2
            b.inc(g.dma_start(out=wslot_r(sl), in_=wblk_ap(wg, DH, hc)), 'sG', 16, ('wg', hc))
            b.inc(g.dma_start(out=wslot_r(sl + 1), in_=wblk_ap(wu, DH, hc)), 'sG', 16, ('wu', hc))

    def p5_tensor(t):
        for hc in range(NHC):
            b.wait(t, ('wg', hc))
            b.wait(t, ('wu', hc))
            if hc == 0:
                b.wait(t, ('h2', NCH - 1))
            sl = (hc % 2) * 2
            for tq in range(NTQ):
                if hc >= 1:
                    b.wait(t, ('gu_ev', hc - 1, tq))
                for K in range(NCH):
                    ins = t.matmul(psA[:, tq * 512:(tq + 1) * 512],
                                   wslot(sl)[:, K * 128:(K + 1) * 128],
                                   BH[:, K * T + tq * 512:K * T + (tq + 1) * 512],
                                   start=(K == 0), stop=(K == NCH - 1))
                b.inc(ins, 'sT', 1, ('g_mm', hc, tq))
                for K in range(NCH):
                    ins = t.matmul(psB[:, tq * 512:(tq + 1) * 512],
                                   wslot(sl + 1)[:, K * 128:(K + 1) * 128],
                                   BH[:, K * T + tq * 512:K * T + (tq + 1) * 512],
                                   start=(K == 0), stop=(K == NCH - 1))
                b.inc(ins, 'sT', 1, ('u_mm', hc, tq))

    def p5_vector(v):
        b.wait(v, 'ln2_bc')
        for c in range(NCH):
            b.wait(v, ('x25', c))
            normalize(v, 'ln2', XS[c % 2], BH, c)
            b.mark('sV', ('h2', c))
        for hc in range(NHC):
            if hc >= 2:
                b.wait(v, ('gst', hc - 2))
            for tq in range(NTQ):
                b.wait(v, ('sil', hc, tq))
                b.wait(v, ('u_mm', hc, tq))
                b.inc(v.tensor_tensor(BV[:, GSL[hc % 2] + tq * 512:GSL[hc % 2] + (tq + 1) * 512],
                                      BV[:, SIL[tq % 2]:SIL[tq % 2] + 512],
                                      psB[:, tq * 512:(tq + 1) * 512], ALU.mult),
                      'sV', 1, ('gu_ev', hc, tq))
            b.mark('sV', ('gsl', hc))

    def p5_scalar(s):
        for hc in range(NHC):
            for tq in range(NTQ):
                b.wait(s, ('g_mm', hc, tq))
                prev = (hc, tq - 2) if tq >= 2 else ((hc - 1, tq + 2) if hc >= 1 else None)
                if prev is not None:
                    b.wait(s, ('gu_ev', prev[0], prev[1]))
                b.inc(s.activation(BV[:, SIL[tq % 2]:SIL[tq % 2] + 512],
                                   psA[:, tq * 512:(tq + 1) * 512], AF.Silu),
                      'sC', 1, ('sil', hc, tq))

    b.phase('P5', [('sync', p5_sync), ('gpsimd', p5_gpsimd), ('tensor', p5_tensor),
                   ('vector', p5_vector), ('scalar', p5_scalar)])

    if tap == 'gated':
        taps['g'] = nc.dram_tensor('tap_g', [DH, T], BF16, kind='ExternalOutput')
        taps['x2o'] = nc.dram_tensor('tap_x2o', [D, T], F32, kind='ExternalOutput')

        def ts2(s):
            for ec in range(NCH):
                b.inc(s.dma_start(out=taps['x2o'][ec * 128:(ec + 1) * 128, :],
                                  in_=x2h[ec * 128:(ec + 1) * 128, :]), 'sS', 16, ('x2o', ec))
            for hc in range(NHC):
                b.inc(s.dma_start(out=taps['g'][hc * 128:(hc + 1) * 128, :],
                                  in_=gth[hc * 128:(hc + 1) * 128, :]), 'sS', 16, ('gto', hc))
            b.wait(s, ('gto', NHC - 1))
        b.phase('TAP', [('sync', ts2)])
        es.close()
        return nc, taps

    XS6 = [0, 1024]
    PBf6 = [0, 1024]

    def p6_sync(s):
        for th in range(2):
            for half, buf in ((0, BH), (1, BQ)):
                in_ap = AP(gth, (half * 32 * 128) * T + th * 1024,
                           [[T, 128], [128 * T, 32], [1, 1024]])
                b.inc(s.dma_start(out=buf[:, 0:32 * 1024].rearrange('p (c n) -> p c n', c=32),
                                  in_=in_ap), 'sS', 16, ('gh', th, half))
            for dc in range(NCH):
                if dc >= 2:
                    b.wait(s, ('dnv', th, dc - 2, 1))
                b.inc(s.dma_start(out=BKf[:, XS6[dc % 2]:XS6[dc % 2] + 1024],
                                  in_=x2h[dc * 128:(dc + 1) * 128, th * 1024:(th + 1) * 1024]),
                      'sS', 16, ('x26', th, dc))
                b.wait(s, ('dnv', th, dc, 1))
                b.inc(s.dma_start(out=out[dc * 128:(dc + 1) * 128, th * 1024:(th + 1) * 1024],
                                  in_=BVf[:, PBf6[dc % 2]:PBf6[dc % 2] + 1024]),
                      'sS', 16, ('ost', th, dc))
        b.wait(s, ('ost', 1, NCH - 1))

    def p6_gpsimd(g):
        for th in range(2):
            for dc in range(NCH):
                idx = th * NCH + dc
                if idx >= 2:
                    pidx = idx - 2
                    b.wait(g, ('dnmm', pidx // NCH, pidx % NCH))
                for piece in range(4):
                    in_ap = AP(wd, dc * 128 + piece * 16 * 128 * D,
                               [[D, 128], [128 * D, 16], [1, 128]])
                    if dc % 2 == 0:
                        dst = SCRb[:, piece * 2048:(piece + 1) * 2048]
                    elif piece < 2:
                        dst = cosTb[:, piece * 2048:(piece + 1) * 2048]
                    else:
                        dst = sinTb[:, (piece - 2) * 2048:(piece - 1) * 2048]
                    b.inc(g.dma_start(out=dst.rearrange('p (c n) -> p c n', c=16),
                                      in_=in_ap), 'sG', 16,
                          ('wd', th, dc) if piece == 3 else None)

    def p6_tensor(t):
        for th in range(2):
            b.wait(t, ('gh', th, 0))
            b.wait(t, ('gh', th, 1))
            for dc in range(NCH):
                b.wait(t, ('wd', th, dc))
                for tq2 in range(2):
                    slot = (2 * dc + tq2) % 4
                    if 2 * dc + tq2 >= 4:
                        pr = 2 * dc + tq2 - 4
                        b.wait(t, ('dnv', th, pr // 2, pr % 2))
                    elif th == 1 and 2 * dc + tq2 < 4:
                        pr = 2 * NCH + 2 * dc + tq2 - 4
                        b.wait(t, ('dnv', 0, pr // 2, pr % 2))
                    for hc in range(NHC):
                        buf = BH if hc < 32 else BQ
                        if dc % 2 == 0:
                            wsl = SCRb[:, hc * 128:(hc + 1) * 128]
                        elif hc < 32:
                            wsl = cosTb[:, hc * 128:(hc + 1) * 128]
                        else:
                            wsl = sinTb[:, (hc - 32) * 128:(hc - 31) * 128]
                        ins = t.matmul(psA[:, slot * 512:(slot + 1) * 512], wsl,
                                       buf[:, (hc % 32) * 1024 + tq2 * 512:(hc % 32) * 1024 + (tq2 + 1) * 512],
                                       start=(hc == 0), stop=(hc == NHC - 1))
                    b.inc(ins, 'sT', 1, ('dn', th, dc, tq2))
                b.mark('sT', ('dnmm', th, dc))

    def p6_vector(v):
        for th in range(2):
            for dc in range(NCH):
                b.wait(v, ('x26', th, dc))
                for tq2 in range(2):
                    slot = (2 * dc + tq2) % 4
                    b.wait(v, ('dn', th, dc, tq2))
                    if dc >= 2:
                        b.wait(v, ('ost', th, dc - 2))
                    elif th == 1:
                        b.wait(v, ('ost', 0, NCH - 2 + dc))
                    b.inc(v.tensor_tensor(BVf[:, PBf6[dc % 2] + tq2 * 512:PBf6[dc % 2] + (tq2 + 1) * 512],
                                          psA[:, slot * 512:(slot + 1) * 512],
                                          BKf[:, XS6[dc % 2] + tq2 * 512:XS6[dc % 2] + (tq2 + 1) * 512],
                                          ALU.add), 'sV', 1, ('dnv', th, dc, tq2))

    b.phase('P6', [('sync', p6_sync), ('gpsimd', p6_gpsimd), ('tensor', p6_tensor),
                   ('vector', p6_vector)])
    es.close()
    return nc, {'out': out}



_NCACHE = {}


def _get_program(tap=None):
    if tap not in _NCACHE:
        _NCACHE[tap] = build_program(tap)
    return _NCACHE[tap]


def make_inputs(L0_attn_x, L0_attn_wq, L0_attn_wk, L0_attn_wv, L0_attn_wo,
                L0_ffn_w_gate, L0_ffn_w_up, L0_ffn_w_down):
    pq = _perm_q()
    pk = _perm_k()
    cos, sin = _rope_tables()
    wqT = np.ascontiguousarray(np.asarray(L0_attn_wq).T[:, pq], dtype=np.float32)
    wkT = np.ascontiguousarray(np.asarray(L0_attn_wk).T[:, pk], dtype=np.float32)
    wvT = np.ascontiguousarray(np.asarray(L0_attn_wv).T, dtype=np.float32)
    woT = np.ascontiguousarray(np.asarray(L0_attn_wo).T, dtype=np.float32)
    wgT = np.ascontiguousarray(np.asarray(L0_ffn_w_gate).T, dtype=np.float32)
    wuT = np.ascontiguousarray(np.asarray(L0_ffn_w_up).T, dtype=np.float32)
    wdT = np.ascontiguousarray(np.asarray(L0_ffn_w_down).T, dtype=np.float32)
    diag = np.where(np.arange(128)[None, :] > np.arange(128)[:, None],
                    np.float32(NEG), np.float32(0.0)).astype(np.float32)
    shared = dict(wq=wqT, wk=wkT, wv=wvT, wo=woT, wg=wgT, wu=wuT, wd=wdT,
                  cos_in=cos, sin_in=sin,
                  ident_in=np.eye(128, dtype=np.float32),
                  diag_in=diag,
                  onesc_in=np.ones((128, 1), np.float32),
                  onesf_in=np.ones((128, 128), np.float32))
    x = np.asarray(L0_attn_x)
    in_maps = []
    for core in range(8):
        im = dict(shared)
        im['xf'] = np.ascontiguousarray(x[core].T, dtype=np.float32)
        in_maps.append(im)
    return in_maps


def kernel(**inputs):
    nc, outs = _get_program(None)
    in_maps = make_inputs(**inputs)
    res = run_bass_kernel_spmd(nc, in_maps, core_ids=list(range(8)))
    return np.stack([r['outf'].T for r in res.results], axis=0).astype(np.float32)


# version 13
# speedup vs baseline: 1.3019x; 1.0182x over previous
import sys
import numpy as np

sys.path.insert(0, '/opt/trn_rl_repo')

import concourse.bass as bass
import concourse.mybir as mybir
from concourse.ap import AP
from concourse.bass_utils import run_bass_kernel_spmd
from contextlib import ExitStack

F32 = mybir.dt.float32
BF16 = mybir.dt.bfloat16
AF = mybir.ActivationFunctionType
ALU = mybir.AluOpType
AX = mybir.AxisListType

D = 2048
T = 2048
H = 32
HD = 64
DH = 8192
NCH = 16
NHC = DH // 128
NTC = T // 128
NTQ = T // 512
NEG = -1.0e9


def _perm_q():
    perm = np.zeros(2048, np.int64)
    for c in range(8):
        for j in range(4):
            h = 4 * (j + 4 * (1 if c >= 4 else 0)) + (c % 4)
            for i in range(32):
                perm[c * 128 + 32 * j + i] = h * 64 + 2 * i
                perm[(c + 8) * 128 + 32 * j + i] = h * 64 + 2 * i + 1
    return perm


def _perm_k():
    perm = np.zeros(512, np.int64)
    for c in range(2):
        for j in range(4):
            g = 4 * c + j
            for i in range(32):
                perm[c * 128 + 32 * j + i] = g * 64 + 2 * i
                perm[(c + 2) * 128 + 32 * j + i] = g * 64 + 2 * i + 1
    return perm


def _rope_tables():
    theta = 1.0 / (10000.0 ** (np.arange(0, HD, 2, dtype=np.float64) / HD))
    t = np.arange(T, dtype=np.float64)
    fr = theta[:, None] * t[None, :]
    cos = np.cos(fr).astype(np.float32)
    sin = np.sin(fr).astype(np.float32)
    return np.tile(cos, (4, 1)), np.tile(sin, (4, 1))



class _Dry:
    def then_inc(self, *a, **k):
        return self

    def __getattr__(self, name):
        return lambda *a, **k: self

_DRY = _Dry()


class B:

    def __init__(self, nc, es):
        self.nc = nc
        self.es = es
        self.sems = {}
        self.cnt = {}
        self.marks = {}
        self.want = []
        self.emitting = False

    def sem(self, name):
        self.sems[name] = self.es.enter_context(self.nc.semaphore(name))
        self.cnt[name] = 0

    def inc(self, instr, s, k=1, label=None):
        self.cnt[s] += k
        if self.emitting and instr is not _DRY and k > 0:
            instr.then_inc(self.sems[s], k)
        if label is not None:
            if self.emitting:
                assert self.marks[label] == (s, self.cnt[s]), (label, self.marks.get(label), (s, self.cnt[s]))
            else:
                assert label not in self.marks, label
                self.marks[label] = (s, self.cnt[s])
        return instr

    def mark(self, s, label):
        return self.inc(_DRY, s, 0, label)

    def wait(self, eng, label):
        if not self.emitting:
            self.want.append(label)
            return
        s, v = self.marks[label]
        eng.wait_ge(self.sems[s], v)

    def wait_maybe(self, eng, label):
        if not self.emitting:
            return
        if label in self.marks:
            s, v = self.marks[label]
            eng.wait_ge(self.sems[s], v)

    def phase(self, name, emitters):
        snap = dict(self.cnt)
        self.emitting = False
        self.want = []
        for en, fn in emitters:
            fn(_DRY)
        missing = [w for w in self.want if w not in self.marks]
        assert not missing, ('unresolved waits', missing[:5])
        self.cnt = snap
        self.emitting = True
        with self.nc.Block(name) as blk:
            for en, fn in emitters:
                getattr(blk, en)(fn)



def build_program(tap=None):
    nc = bass.Bass('TRN2', target_bir_lowering=False, detect_race_conditions=False)

    xf = nc.dram_tensor('xf', [D, T], F32, kind='ExternalInput')
    wq = nc.dram_tensor('wq', [D, D], F32, kind='ExternalInput')
    wk = nc.dram_tensor('wk', [D, 512], F32, kind='ExternalInput')
    wv = nc.dram_tensor('wv', [D, 512], F32, kind='ExternalInput')
    wo = nc.dram_tensor('wo', [D, D], F32, kind='ExternalInput')
    wg = nc.dram_tensor('wg', [D, DH], F32, kind='ExternalInput')
    wu = nc.dram_tensor('wu', [D, DH], F32, kind='ExternalInput')
    wd = nc.dram_tensor('wd', [DH, D], F32, kind='ExternalInput')
    cos_in = nc.dram_tensor('cos_in', [128, T], F32, kind='ExternalInput')
    sin_in = nc.dram_tensor('sin_in', [128, T], F32, kind='ExternalInput')
    ident_in = nc.dram_tensor('ident_in', [128, 128], F32, kind='ExternalInput')
    diag_in = nc.dram_tensor('diag_in', [128, 128], F32, kind='ExternalInput')
    onesc_in = nc.dram_tensor('onesc_in', [128, 1], F32, kind='ExternalInput')
    onesf_in = nc.dram_tensor('onesf_in', [128, 128], F32, kind='ExternalInput')

    x2h = nc.dram_tensor('x2h', [D, T], F32, kind='Internal')
    gth = nc.dram_tensor('gth', [DH, T], BF16, kind='Internal')

    def fm_pair(dram, sbuf_ap, nch, w):
        return (AP(dram, 0, [[w * 1, 128] if False else [dram.shape[1], 128],
                             [128 * dram.shape[1], nch], [1, w]]),
                sbuf_ap.rearrange('p (c t) -> p c t', c=nch))

    taps = {}
    out = None
    if tap is None:
        out = nc.dram_tensor('outf', [D, T], F32, kind='ExternalOutput')

    es = ExitStack()
    sb = lambda n, s, d: es.enter_context(nc.sbuf_tensor(n, s, d))

    BH = sb('BH', [128, NCH * T], BF16)
    BQ = sb('BQ', [128, NCH * T], BF16)
    BK = sb('BK', [128, 4 * T], BF16)
    BV = sb('BV', [128, NTC * 512], BF16)
    cosT = sb('cosT', [128, T], F32)
    sinT = sb('sinT', [128, T], F32)
    identb = sb('identb', [128, 128], BF16)
    diagm = sb('diagm', [128, 128], F32)
    onesc = sb('onesc', [128, 1], BF16)
    onesf = sb('onesf', [128, 128], F32)
    identf = sb('identf', [128, 128], F32)
    STT = sb('STT', [128, 160], F32)
    SCR = sb('SCR', [128, 5120], F32)
    SCRb = SCR.bitcast(BF16)
    BKf = BK.bitcast(F32)
    BVf = BV.bitcast(F32)
    cosTb = cosT.bitcast(BF16)
    sinTb = sinT.bitcast(BF16)
    mt = [sb('m%d' % i, [128, 1], F32) for i in range(2)]
    nmt = [sb('nm%d' % i, [128, 1], F32) for i in range(2)]
    lt = [sb('l%d' % i, [128, 1], F32) for i in range(2)]
    rt = [sb('r%d' % i, [128, 1], F32) for i in range(2)]

    RP = [4096, 4608]
    WSLOT = [0, 2048, 4096, 6144]
    PHB = [0, 2048]
    PTB = [0, 2048]
    GSL = [0, 2048]
    SIL = [4096, 4608]
    SQ1 = [0, 512]
    DGP = [8192, 8320]
    XS = [0, 2048]
    PBf = [0, 2048]

    psA = es.enter_context(nc.psum_tensor('psA', [128, 2048], F32))
    psB = es.enter_context(nc.psum_tensor('psB', [128, 2048], F32))
    psAb = psA.bitcast(BF16)
    psBb = psB.bitcast(BF16)
    diagmb = diagm.bitcast(BF16)

    b = B(nc, es)
    for s in ('sG', 'sS', 'sT', 'sV', 'sC'):
        b.sem(s)

    def R(i, lo=0, hi=16):
        return STT[:, i * 16 + lo:i * 16 + hi]

    def step(eng, sem, instr):
        b.inc(instr, sem, 1)
        if b.emitting:
            eng.wait_ge(b.sems[sem], b.cnt[sem])
        return instr

    def ln_finish_vec(v, ph):
        step(v, 'sV', v.tensor_scalar(R(2), R(0), 1.0 / D, None, ALU.mult))
        step(v, 'sV', v.tensor_scalar(R(3), R(1), 1.0 / D, None, ALU.mult))
        step(v, 'sV', v.tensor_tensor(R(4), R(2), R(2), ALU.mult))
        step(v, 'sV', v.tensor_tensor(R(4), R(3), R(4), ALU.subtract))
        step(v, 'sV', v.tensor_scalar(R(4), R(4), 1e-5, None, ALU.add))
        b.inc(v.reciprocal(R(5), R(4)), 'sV', 1, ph + '_rec')
        b.wait(v, ph + '_sqrt')
        step(v, 'sV', v.tensor_tensor(R(5), R(6), R(6), ALU.mult))
        step(v, 'sV', v.tensor_tensor(R(5), R(5), R(4), ALU.mult))
        step(v, 'sV', v.tensor_scalar(R(5), R(5), -1.0, 3.0, ALU.mult, ALU.add))
        step(v, 'sV', v.tensor_tensor(R(5), R(6), R(5), ALU.mult))
        step(v, 'sV', v.tensor_scalar(R(7), R(5), 0.5, None, ALU.mult))
        step(v, 'sV', v.tensor_tensor(R(8), R(2), R(7), ALU.mult))
        b.inc(v.tensor_scalar(R(8), R(8), -1.0, None, ALU.mult), 'sV', 1, ph + '_ab')
        if b.emitting:
            v.wait_ge(b.sems['sV'], b.cnt['sV'])

    def ln_finish_sca(s, ph):
        b.wait(s, ph + '_rec')
        b.inc(s.activation(R(6), R(5), AF.Sqrt), 'sC', 1, ph + '_sqrt')

    DGB = [RP[0], RP[0] + 128]

    def bcast_vec(v, ph):
        b.wait(v, 'c2_done')
        b.wait(v, ph + '_ab')
        for idx in range(32):
            tc, isb = idx // 2, idx % 2
            if idx >= 2:
                b.wait(v, (ph + '_bcm', idx - 2))
            col = R(8 if isb else 7, tc, tc + 1)
            b.inc(v.tensor_scalar(SCR[:, DGB[idx % 2]:DGB[idx % 2] + 128],
                                  identf[:, :], col, None, ALU.mult),
                  'sV', 1, (ph + '_dg', idx))

    def bcast_tensor(t, ph):
        for idx in range(32):
            tc, isb = idx // 2, idx % 2
            b.wait(t, (ph + '_dg', idx))
            ps = psB if isb else psA
            ins = t.matmul(ps[:, tc * 128:(tc + 1) * 128], onesf[:, :],
                           SCR[:, DGB[idx % 2]:DGB[idx % 2] + 128],
                           start=True, stop=True)
            b.inc(ins, 'sT', 1, (ph + '_bcm', idx))
        b.mark('sT', ph + '_bc')

    def stats_mms(t, ph, src_of, sq_of, sqwait):
        for wave in range(4):
            if wave >= 1:
                b.wait(t, (ph + '_evw', wave - 1))
            for c in range(NCH):
                b.wait(t, (ph + '_st_x', c) if ph == 'ln1' else ('x2bf', c))
                b.wait(t, sqwait(wave, c))
                for tbl in range(4):
                    tb = wave * 4 + tbl
                    t.matmul(psA[:, tbl * 512 + wave:tbl * 512 + wave + 1],
                             src_of(c)[:, tb * 128:(tb + 1) * 128],
                             onesc[:, :], start=(c == 0), stop=(c == NCH - 1))
                    ins = t.matmul(psB[:, tbl * 512 + wave:tbl * 512 + wave + 1],
                                   sq_of(c)[:, tbl * 128:(tbl + 1) * 128],
                                   onesc[:, :], start=(c == 0), stop=(c == NCH - 1))
                b.inc(ins, 'sT', 1, (ph + '_sw', wave, c))
            b.inc(_DRY, 'sT', 0, (ph + '_wave', wave))

    def stats_evac(v, ph):
        for wave in range(4):
            b.wait(v, (ph + '_wave', wave))
            b.inc(v.tensor_copy(R(0, wave * 4, wave * 4 + 4),
                                AP(psA, wave, [[2048, 128], [512, 4]])), 'sV', 1)
            b.inc(v.tensor_copy(R(1, wave * 4, wave * 4 + 4),
                                AP(psB, wave, [[2048, 128], [512, 4]])),
                  'sV', 1, (ph + '_evw', wave))
        if b.emitting:
            v.wait_ge(b.sems['sV'], b.cnt['sV'])
        b.mark('sV', ph + '_ev')

    def normalize(v, ph, src_off, dst, c):
        for tq in range(NTQ):
            sl = slice(tq * 512, (tq + 1) * 512)
            step(v, 'sV', v.tensor_tensor(SCR[:, RP[0]:RP[0] + 512],
                                          BKf[:, src_off + tq * 512:src_off + (tq + 1) * 512],
                                          psA[:, sl], ALU.mult))
            ins = v.tensor_tensor(dst[:, c * T + tq * 512:c * T + (tq + 1) * 512],
                                  SCR[:, RP[0]:RP[0] + 512], psB[:, sl], ALU.add)
            if tq < NTQ - 1:
                step(v, 'sV', ins)
            else:
                b.inc(ins, 'sV', 1)
                if b.emitting:
                    v.wait_ge(b.sems['sV'], b.cnt['sV'])
        return ins

    def p1_gpsimd(g):
        b.inc(g.dma_start(out=identb[:, :], in_=ident_in[:, :]), 'sG', 16)
        b.inc(g.dma_start(out=diagmb[:, 0:128], in_=diag_in[:, :]), 'sG', 16)
        b.inc(g.dma_start(out=onesc[:, :], in_=onesc_in[:, :]), 'sG', 16, 'c_done')
        for c in range(NCH):
            b.inc(g.dma_start(out=BQ[:, c * T:(c + 1) * T],
                              in_=xf[c * 128:(c + 1) * 128, :]), 'sG', 16, ('xbf', c))

    def p1_sync(s):
        b.inc(s.dma_start(out=cosT[:, :], in_=cos_in[:, :]), 'sS', 16)
        b.inc(s.dma_start(out=sinT[:, :], in_=sin_in[:, :]), 'sS', 16)
        b.inc(s.dma_start(out=onesf[:, :], in_=onesf_in[:, :]), 'sS', 16)
        b.inc(s.dma_start(out=identf[:, :], in_=ident_in[:, :]), 'sS', 16, 'c2_done')
        for c in range(NCH):
            if c >= 2:
                b.wait(s, ('norm1', c - 2))
            b.inc(s.dma_start(out=BKf[:, XS[c % 2]:XS[c % 2] + 2048],
                              in_=xf[c * 128:(c + 1) * 128, :]), 'sS', 16, ('xin', c))

    def p1_scalar(s):
        b.wait(s, 'c_done')
        for wave in range(4):
            for c in range(NCH):
                b.wait(s, ('xbf', c))
                idx = wave * NCH + c
                if idx >= 2:
                    pidx = idx - 2
                    b.wait(s, ('ln1_sw', pidx // NCH, pidx % NCH))
                b.inc(s.activation(SCRb[:, SQ1[c % 2]:SQ1[c % 2] + 512],
                                   BQ[:, c * T + wave * 512:c * T + wave * 512 + 512],
                                   AF.Square), 'sC', 1, ('sq1', wave, c))
        ln_finish_sca(s, 'ln1')

    def p1_tensor(t):
        for c in range(NCH):
            b.wait(t, ('xbf', c))
            b.mark('sT', ('ln1_st_x', c))
        stats_mms(t, 'ln1', lambda c: BQ[:, c * T:(c + 1) * T],
                  lambda c: SCRb[:, SQ1[c % 2]:SQ1[c % 2] + 512],
                  lambda wave, c: ('sq1', wave, c))
        bcast_tensor(t, 'ln1')

    def p1_vector(v):
        stats_evac(v, 'ln1')
        ln_finish_vec(v, 'ln1')
        bcast_vec(v, 'ln1')
        b.wait(v, 'ln1_bc')
        for c in range(NCH):
            b.wait(v, ('xin', c))
            normalize(v, 'ln1', XS[c % 2], BH, c)
            b.mark('sV', ('norm1', c))

    b.phase('P1', [('gpsimd', p1_gpsimd), ('sync', p1_sync), ('scalar', p1_scalar),
                   ('tensor', p1_tensor), ('vector', p1_vector)])

    if tap == 'h':
        taps['h'] = nc.dram_tensor('tap_h', [D, T], BF16, kind='ExternalOutput')
        taps['stt'] = nc.dram_tensor('tap_stt', [128, 160], F32, kind='ExternalOutput')

        def ts_(s):
            o_ap, i_ap = fm_pair(taps['h'], BH[:, :], NCH, T)
            b.inc(s.dma_start(out=o_ap, in_=i_ap), 'sS', 16, 'tap_st')
            b.inc(s.dma_start(out=taps['stt'][:, :], in_=STT[:, :]), 'sS', 16, 'tap_st2')
            b.wait(s, 'tap_st2')
        b.phase('TAP', [('sync', ts_)])
        es.close()
        return nc, taps

    def wblk_ap(w, ncols, cblk, blkw=128):
        rows = w.shape[0]
        return AP(w, cblk * blkw, [[ncols, 128], [128 * ncols, rows // 128], [1, blkw]])

    def wslot(i, n=2048):
        return SCRb[:, WSLOT[i]:WSLOT[i] + n]

    def wslot_r(i, k=16, n=2048):
        return SCRb[:, WSLOT[i]:WSLOT[i] + n].rearrange('p (k n) -> p k n', k=k)

    def p2_gpsimd(g):
        for p in range(8):
            if p >= 2:
                b.wait(g, ('qmm', p - 2, NTQ - 1))
            b.inc(g.dma_start(out=wslot_r(2 * (p % 2)), in_=wblk_ap(wq, D, p)),
                  'sG', 16, ('wqe', p))
            b.inc(g.dma_start(out=wslot_r(2 * (p % 2) + 1), in_=wblk_ap(wq, D, p + 8)),
                  'sG', 16, ('wqo', p))
        for c in range(4):
            b.inc(g.dma_start(out=BV[:, c * 2048:(c + 1) * 2048]
                              .rearrange('p (k n) -> p k n', k=16),
                              in_=wblk_ap(wk, 512, c)), 'sG', 16, ('wk', c))
        b.wait(g, ('qmm', 6, NTQ - 1))
        b.inc(g.dma_start(out=SCRb[:, WSLOT[0]:WSLOT[0] + 4096]
                          .rearrange('p (k n) -> p k n', k=16),
                          in_=wblk_ap(wv, 512, 0, 256)), 'sG', 16, ('wv', 0))
        b.wait(g, ('qmm', 7, NTQ - 1))
        b.inc(g.dma_start(out=SCRb[:, WSLOT[2]:WSLOT[2] + 4096]
                          .rearrange('p (k n) -> p k n', k=16),
                          in_=wblk_ap(wv, 512, 1, 256)), 'sG', 16, ('wv', 1))

    def p2_tensor(t):
        for p in range(8):
            b.wait(t, ('wqe', p))
            b.wait(t, ('wqo', p))
            for tq in range(NTQ):
                if p >= 1:
                    b.wait(t, ('rope_q', p - 1, tq))
                se, so = 2 * (p % 2), 2 * (p % 2) + 1
                for K in range(NCH):
                    t.matmul(psA[:, tq * 512:(tq + 1) * 512],
                             wslot(se)[:, K * 128:(K + 1) * 128],
                             BH[:, K * T + tq * 512:K * T + (tq + 1) * 512],
                             start=(K == 0), stop=(K == NCH - 1))
                for K in range(NCH):
                    ins = t.matmul(psB[:, tq * 512:(tq + 1) * 512],
                                   wslot(so)[:, K * 128:(K + 1) * 128],
                                   BH[:, K * T + tq * 512:K * T + (tq + 1) * 512],
                                   start=(K == 0), stop=(K == NCH - 1))
                b.inc(ins, 'sT', 1, ('qmm', p, tq))
        for p in range(2):
            b.wait(t, ('wk', p))
            b.wait(t, ('wk', p + 2))
            for tq in range(NTQ):
                b.wait(t, ('rope_q', 7, tq))
                if p == 1:
                    b.wait(t, ('rope_k', 0, tq))
                for K in range(NCH):
                    t.matmul(psA[:, tq * 512:(tq + 1) * 512],
                             BV[:, p * 2048 + K * 128:p * 2048 + (K + 1) * 128],
                             BH[:, K * T + tq * 512:K * T + (tq + 1) * 512],
                             start=(K == 0), stop=(K == NCH - 1))
                for K in range(NCH):
                    ins = t.matmul(psB[:, tq * 512:(tq + 1) * 512],
                                   BV[:, (p + 2) * 2048 + K * 128:(p + 2) * 2048 + (K + 1) * 128],
                                   BH[:, K * T + tq * 512:K * T + (tq + 1) * 512],
                                   start=(K == 0), stop=(K == NCH - 1))
                b.inc(ins, 'sT', 1, ('kmm', p, tq))
        for tc in range(NTC):
            for oh in range(2):
                b.wait(t, ('wv', oh))
                b.wait(t, ('rope_k', 1, NTQ - 1))
                if tc >= 2:
                    b.wait(t, ('vev', tc - 2, oh))
                slot = (2 * tc + oh) % 4
                woff = WSLOT[0] if oh == 0 else WSLOT[2]
                for K in range(NCH):
                    ins = t.matmul(psA[:, slot * 512:slot * 512 + 256],
                                   BH[:, K * T + tc * 128:K * T + (tc + 1) * 128],
                                   SCRb[:, woff + K * 256:woff + (K + 1) * 256],
                                   start=(K == 0), stop=(K == NCH - 1))
                b.inc(ins, 'sT', 1, ('vmm', tc, oh))

    def _rope(v, p, tq, is_q):
        sl = slice(tq * 512, (tq + 1) * 512)
        cs = cosT[:, tq * 512:(tq + 1) * 512]
        sn = sinT[:, tq * 512:(tq + 1) * 512]
        t1 = SCR[:, RP[0]:RP[0] + 512]
        t2 = SCR[:, RP[1]:RP[1] + 512]
        dst, off = (BQ, 8) if is_q else (BK, 2)
        de = dst[:, p * T + tq * 512:p * T + (tq + 1) * 512]
        do = dst[:, (p + off) * T + tq * 512:(p + off) * T + (tq + 1) * 512]
        b.inc(v.tensor_tensor(t1, psA[:, sl], cs, ALU.mult), 'sV', 1)
        step(v, 'sV', v.tensor_tensor(t2, psB[:, sl], sn, ALU.mult))
        step(v, 'sV', v.tensor_tensor(de, t1, t2, ALU.subtract))
        b.inc(v.tensor_tensor(t1, psA[:, sl], sn, ALU.mult), 'sV', 1)
        step(v, 'sV', v.tensor_tensor(t2, psB[:, sl], cs, ALU.mult))
        ins = v.tensor_tensor(do, t1, t2, ALU.add)
        b.inc(ins, 'sV', 1)
        if b.emitting:
            v.wait_ge(b.sems['sV'], b.cnt['sV'])
        return ins

    def p2_vector(v):
        for p in range(8):
            for tq in range(NTQ):
                b.wait(v, ('qmm', p, tq))
                _rope(v, p, tq, True)
                b.mark('sV', ('rope_q', p, tq))
        for p in range(2):
            for tq in range(NTQ):
                b.wait(v, ('kmm', p, tq))
                _rope(v, p, tq, False)
                b.mark('sV', ('rope_k', p, tq))
        for tc in range(NTC):
            for oh in range(2):
                b.wait(v, ('vmm', tc, oh))
                slot = (2 * tc + oh) % 4
                b.inc(v.tensor_copy(BV[:, tc * 512 + oh * 256:tc * 512 + oh * 256 + 256],
                                    psA[:, slot * 512:slot * 512 + 256]),
                      'sV', 1, ('vev', tc, oh))

    b.phase('P2', [('gpsimd', p2_gpsimd), ('tensor', p2_tensor), ('vector', p2_vector)])

    if tap == 'qkv':
        taps['q'] = nc.dram_tensor('tap_q', [D, T], BF16, kind='ExternalOutput')
        taps['k'] = nc.dram_tensor('tap_k', [512, T], BF16, kind='ExternalOutput')
        taps['v'] = nc.dram_tensor('tap_v', [T, 512], BF16, kind='ExternalOutput')

        def ts_(s):
            o_ap, i_ap = fm_pair(taps['q'], BQ[:, :], NCH, T)
            b.inc(s.dma_start(out=o_ap, in_=i_ap), 'sS', 16, 'tap_qst')
            o_ap, i_ap = fm_pair(taps['k'], BK[:, :], 4, T)
            b.inc(s.dma_start(out=o_ap, in_=i_ap), 'sS', 16, 'tap_kst')
            o_ap, i_ap = fm_pair(taps['v'], BV[:, :], NTC, 512)
            b.inc(s.dma_start(out=o_ap, in_=i_ap), 'sS', 16, 'tap_vst')
            b.wait(s, 'tap_vst')
        b.phase('TAP', [('sync', ts_)])
        es.close()
        return nc, taps

    ITERS = [(h, qt) for h in range(H) for qt in range(NTC)]
    n3 = len(ITERS)

    _SLOT_COL = [0, 512, 1024, 1536, 1024, 1536]
    alloc3 = []
    claim3 = []
    _last = [None] * 6
    _p = 0
    for _i, (_h3, _qt3) in enumerate(ITERS):
        _bi = (_qt3 + 4) // 4
        if _p < 4 and _p + _bi > 4:
            _p = 4
        if _p >= 4 and _p + _bi > 6:
            _p = 0
        claim3.append(sorted({_last[s] for s in range(_p, _p + _bi)
                              if _last[s] is not None}))
        for s in range(_p, _p + _bi):
            _last[s] = _i
        alloc3.append((0 if _p < 4 else 1, _SLOT_COL[_p]))
        _p += _bi
        if _p >= 6:
            _p = 0

    PT3 = [0, 2048, 4096]
    YSB = [6144, 6208]
    YACC = [0, 64]
    YTP = [256, 384]
    LAC = [512, 513]

    def pt3(i, lo, hi):
        return SCRb[:, PT3[i % 3] + lo:PT3[i % 3] + hi]

    def sc_ps(i):
        reg, base = alloc3[i]
        return (psA if reg == 0 else psB), base

    def emit_sc(t, i):
        h, qt = ITERS[i]
        g = h // 4
        j = g % 4
        c_ev = h % 4 + 4 * (1 if g >= 4 else 0)
        ps, base = sc_ps(i)
        for it_ in claim3[i]:
            b.wait(t, ('exp', it_))
        rows = slice(32 * j, 32 * j + 32)
        for sj in range(qt + 1):
            out = ps[:, base + sj * 128:base + (sj + 1) * 128]
            t.matmul(out,
                     BK[rows, (g // 4) * T + sj * 128:(g // 4) * T + (sj + 1) * 128],
                     BQ[rows, c_ev * T + qt * 128:c_ev * T + (qt + 1) * 128],
                     start=True, stop=False, tile_position=(32 * j, 0))
            ins = t.matmul(out,
                           BK[rows, (g // 4 + 2) * T + sj * 128:(g // 4 + 2) * T + (sj + 1) * 128],
                           BQ[rows, (c_ev + 8) * T + qt * 128:(c_ev + 8) * T + (qt + 1) * 128],
                           start=False, stop=True, tile_position=(32 * j, 0))
        b.inc(ins, 'sT', 1, ('sc', i))

    def emit_pv(t, i):
        h, qt = ITERS[i]
        g = h // 4
        par = i % 2
        if i >= 1:
            b.wait(t, ('yev', i - 1))
        b.wait(t, ('exp', i))
        for sj in range(qt + 1):
            if sj == qt:
                b.wait(t, ('dz', i))
            t.matmul(psB[:, YACC[par]:YACC[par] + 64],
                     pt3(i, sj * 128, (sj + 1) * 128),
                     BV[:, sj * 512 + g * 64:sj * 512 + g * 64 + 64],
                     start=(sj == 0), stop=(sj == qt))
            ins = t.matmul(psB[:, LAC[par]:LAC[par] + 1],
                           pt3(i, sj * 128, (sj + 1) * 128),
                           onesc[:, :],
                           start=(sj == 0), stop=(sj == qt))
        b.inc(ins, 'sT', 1, ('pv', i))

    def emit_ytr(t, i):
        h, qt = ITERS[i]
        par = i % 2
        b.wait(t, ('ysc', i))
        rws = slice(0, 64) if h % 2 == 0 else slice(64, 128)
        tpos = (0, 0) if h % 2 == 0 else (0, 64)
        ins = t.matmul(psB[rws, YTP[par]:YTP[par] + 128],
                       SCRb[:, YSB[par]:YSB[par] + 64],
                       identb[:, :], start=True, stop=True, tile_position=tpos)
        b.inc(ins, 'sT', 1, ('ytr', i))

    def p3_tensor(t):
        for x in range(n3 + 2):
            if x < n3:
                emit_sc(t, x)
            if 0 <= x - 2 < n3:
                emit_pv(t, x - 2)
                emit_ytr(t, x - 2)

    def p3_vector(v):
        for x in range(n3 + 1):
            if x < n3:
                i = x
                h, qt = ITERS[i]
                S = (qt + 1) * 128
                b.wait(v, ('exp', i))
                b.inc(v.tensor_tensor(pt3(i, qt * 128, S), pt3(i, qt * 128, S),
                                      diagmb[:, 0:128], ALU.mult), 'sV', 1, ('dz', i))
            if 0 <= x - 1 < n3:
                iy = x - 1
                hp, qp = ITERS[iy]
                b.wait(v, ('ytr', iy))
                rws = slice(0, 64) if hp % 2 == 0 else slice(64, 128)
                b.inc(v.tensor_copy(BH[rws, (hp // 2) * T + qp * 128:(hp // 2) * T + (qp + 1) * 128],
                                    psB[rws, YTP[iy % 2]:YTP[iy % 2] + 128]),
                      'sV', 1, ('yev', iy))
            if x < n3:
                i = x
                par = i % 2
                b.wait(v, ('pv', i))
                b.inc(v.reciprocal(rt[par][:, :], psB[:, LAC[par]:LAC[par] + 1]),
                      'sV', 1, ('rc', i))
                if i >= 2:
                    b.wait(v, ('ytr', i - 2))
                step(v, 'sV', v.tensor_copy(SCR[:, RP[0] + par * 64:RP[0] + par * 64 + 64],
                                            psB[:, YACC[par]:YACC[par] + 64]))
                b.wait(v, ('rc', i))
                b.inc(v.tensor_scalar(SCRb[:, YSB[par]:YSB[par] + 64],
                                      SCR[:, RP[0] + par * 64:RP[0] + par * 64 + 64],
                                      rt[par][:, :], None, ALU.mult), 'sV', 1, ('ysc', i))

    def p3_scalar(s):
        for i, (h, qt) in enumerate(ITERS):
            S = (qt + 1) * 128
            ps, base = sc_ps(i)
            b.wait(s, ('sc', i))
            if i >= 3:
                b.wait(s, ('pv', i - 3))
            b.inc(s.activation(pt3(i, 0, S), ps[:, base:base + S], AF.Exp,
                               scale=0.125), 'sC', 1, ('exp', i))

    b.phase('P3', [('tensor', p3_tensor), ('vector', p3_vector), ('scalar', p3_scalar)])

    if tap == 'y':
        taps['y'] = nc.dram_tensor('tap_y', [D, T], BF16, kind='ExternalOutput')

        def ts_(s):
            o_ap, i_ap = fm_pair(taps['y'], BH[:, :], NCH, T)
            b.inc(s.dma_start(out=o_ap, in_=i_ap), 'sS', 16, 'tap_st')
            b.wait(s, 'tap_st')
        b.phase('TAP', [('sync', ts_)])
        es.close()
        return nc, taps

    def p4_gpsimd(g):
        for ec in range(NCH):
            if ec >= 2:
                b.wait(g, ('wo_mm', ec - 2, NTQ - 1))
            b.inc(g.dma_start(out=wslot_r(ec % 2), in_=wblk_ap(wo, D, ec)),
                  'sG', 16, ('wo', ec))

    def p4_sync(s):
        for ec in range(NCH):
            if ec >= 2:
                b.wait(s, ('x2v', ec - 2, NTQ - 1))
            b.inc(s.dma_start(out=BKf[:, XS[ec % 2]:XS[ec % 2] + 2048],
                              in_=xf[ec * 128:(ec + 1) * 128, :]), 'sS', 16, ('x4', ec))
            b.wait(s, ('x2bf', ec))
            b.inc(s.dma_start(out=x2h[ec * 128:(ec + 1) * 128, :],
                              in_=BVf[:, PBf[ec % 2]:PBf[ec % 2] + 2048]), 'sS', 16, ('x2st', ec))
        if tap in ('x2', 'gated'):
            b.wait(s, ('x2bf', NCH - 1))

    def p4_tensor(t):
        for ec in range(NCH):
            b.wait(t, ('wo', ec))
            for tq in range(NTQ):
                if ec >= 1:
                    b.wait(t, ('x2v', ec - 1, tq))
                for K in range(NCH):
                    ins = t.matmul(psA[:, tq * 512:(tq + 1) * 512],
                                   wslot(ec % 2)[:, K * 128:(K + 1) * 128],
                                   BH[:, K * T + tq * 512:K * T + (tq + 1) * 512],
                                   start=(K == 0), stop=(K == NCH - 1))
                b.inc(ins, 'sT', 1, ('wo_mm', ec, tq))
        for tq in range(NTQ):
            b.wait(t, ('x2v', NCH - 1, tq))
        stats_mms(t, 'ln2', lambda c: BQ[:, c * T:(c + 1) * T],
                  lambda c: SCRb[:, SQ1[c % 2]:SQ1[c % 2] + 512],
                  lambda wave, c: ('sq2', wave, c))
        bcast_tensor(t, 'ln2')

    def p4_vector(v):
        for ec in range(NCH):
            for tq in range(NTQ):
                b.wait(v, ('wo_mm', ec, tq))
                if tq == 0:
                    b.wait(v, ('x4', ec))
                    if ec >= 2:
                        b.wait(v, ('x2st', ec - 2))
                        b.wait(v, ('x2bf', ec - 2))
                b.inc(v.tensor_tensor(BVf[:, PBf[ec % 2] + tq * 512:PBf[ec % 2] + (tq + 1) * 512],
                                      psA[:, tq * 512:(tq + 1) * 512],
                                      BKf[:, XS[ec % 2] + tq * 512:XS[ec % 2] + (tq + 1) * 512],
                                      ALU.add), 'sV', 1, ('x2v', ec, tq))
        stats_evac(v, 'ln2')
        ln_finish_vec(v, 'ln2')
        bcast_vec(v, 'ln2')

    def p4_scalar(s):
        for ec in range(NCH):
            b.wait(s, ('x2v', ec, NTQ - 1))
            b.inc(s.copy(BQ[:, ec * T:(ec + 1) * T],
                         BVf[:, PBf[ec % 2]:PBf[ec % 2] + 2048]), 'sC', 1, ('x2bf', ec))
        for wave in range(4):
            for ec in range(NCH):
                idx = wave * NCH + ec
                if idx >= 2:
                    pidx = idx - 2
                    b.wait(s, ('ln2_sw', pidx // NCH, pidx % NCH))
                b.wait(s, ('x2bf', ec))
                b.inc(s.activation(SCRb[:, SQ1[ec % 2]:SQ1[ec % 2] + 512],
                                   BQ[:, ec * T + wave * 512:ec * T + wave * 512 + 512],
                                   AF.Square), 'sC', 1, ('sq2', wave, ec))
        ln_finish_sca(s, 'ln2')

    b.phase('P4', [('gpsimd', p4_gpsimd), ('sync', p4_sync), ('tensor', p4_tensor),
                   ('vector', p4_vector), ('scalar', p4_scalar)])

    if tap == 'x2':
        taps['x2o'] = nc.dram_tensor('tap_x2o', [D, T], F32, kind='ExternalOutput')

        def ts_(s):
            for ec in range(NCH):
                b.inc(s.dma_start(out=taps['x2o'][ec * 128:(ec + 1) * 128, :],
                                  in_=x2h[ec * 128:(ec + 1) * 128, :]), 'sS', 16, ('x2o', ec))
            b.wait(s, ('x2o', NCH - 1))
        b.phase('TAP', [('sync', ts_)])
        es.close()
        return nc, taps

    def p5_sync(s):
        for c in range(NCH):
            if c >= 2:
                b.wait(s, ('h2', c - 2))
            b.inc(s.dma_start(out=BKf[:, XS[c % 2]:XS[c % 2] + 2048],
                              in_=x2h[c * 128:(c + 1) * 128, :]), 'sS', 16, ('x25', c))
        for hc in range(NHC):
            b.wait(s, ('gsl', hc))
            b.inc(s.dma_start(out=gth[hc * 128:(hc + 1) * 128, :],
                              in_=BV[:, GSL[hc % 2]:GSL[hc % 2] + 2048]),
                  'sS', 16, ('gst', hc))

    def p5_gpsimd(g):
        for hc in range(NHC):
            if hc >= 2:
                b.wait(g, ('u_mm', hc - 2, NTQ - 1))
            sl = (hc % 2) * 2
            b.inc(g.dma_start(out=wslot_r(sl), in_=wblk_ap(wg, DH, hc)), 'sG', 16, ('wg', hc))
            b.inc(g.dma_start(out=wslot_r(sl + 1), in_=wblk_ap(wu, DH, hc)), 'sG', 16, ('wu', hc))

    def p5_tensor(t):
        for hc in range(NHC):
            b.wait(t, ('wg', hc))
            b.wait(t, ('wu', hc))
            if hc == 0:
                b.wait(t, ('h2', NCH - 1))
            sl = (hc % 2) * 2
            for tq in range(NTQ):
                if hc >= 1:
                    b.wait(t, ('gu_ev', hc - 1, tq))
                for K in range(NCH):
                    ins = t.matmul(psA[:, tq * 512:(tq + 1) * 512],
                                   wslot(sl)[:, K * 128:(K + 1) * 128],
                                   BH[:, K * T + tq * 512:K * T + (tq + 1) * 512],
                                   start=(K == 0), stop=(K == NCH - 1))
                b.inc(ins, 'sT', 1, ('g_mm', hc, tq))
                for K in range(NCH):
                    ins = t.matmul(psB[:, tq * 512:(tq + 1) * 512],
                                   wslot(sl + 1)[:, K * 128:(K + 1) * 128],
                                   BH[:, K * T + tq * 512:K * T + (tq + 1) * 512],
                                   start=(K == 0), stop=(K == NCH - 1))
                b.inc(ins, 'sT', 1, ('u_mm', hc, tq))

    def p5_vector(v):
        b.wait(v, 'ln2_bc')
        for c in range(NCH):
            b.wait(v, ('x25', c))
            normalize(v, 'ln2', XS[c % 2], BH, c)
            b.mark('sV', ('h2', c))
        for hc in range(NHC):
            if hc >= 2:
                b.wait(v, ('gst', hc - 2))
            for tq in range(NTQ):
                b.wait(v, ('sil', hc, tq))
                b.wait(v, ('u_mm', hc, tq))
                b.inc(v.tensor_tensor(BV[:, GSL[hc % 2] + tq * 512:GSL[hc % 2] + (tq + 1) * 512],
                                      BV[:, SIL[tq % 2]:SIL[tq % 2] + 512],
                                      psB[:, tq * 512:(tq + 1) * 512], ALU.mult),
                      'sV', 1, ('gu_ev', hc, tq))
            b.mark('sV', ('gsl', hc))

    def p5_scalar(s):
        for hc in range(NHC):
            for tq in range(NTQ):
                b.wait(s, ('g_mm', hc, tq))
                prev = (hc, tq - 2) if tq >= 2 else ((hc - 1, tq + 2) if hc >= 1 else None)
                if prev is not None:
                    b.wait(s, ('gu_ev', prev[0], prev[1]))
                b.inc(s.activation(BV[:, SIL[tq % 2]:SIL[tq % 2] + 512],
                                   psA[:, tq * 512:(tq + 1) * 512], AF.Silu),
                      'sC', 1, ('sil', hc, tq))

    b.phase('P5', [('sync', p5_sync), ('gpsimd', p5_gpsimd), ('tensor', p5_tensor),
                   ('vector', p5_vector), ('scalar', p5_scalar)])

    if tap == 'gated':
        taps['g'] = nc.dram_tensor('tap_g', [DH, T], BF16, kind='ExternalOutput')
        taps['x2o'] = nc.dram_tensor('tap_x2o', [D, T], F32, kind='ExternalOutput')

        def ts2(s):
            for ec in range(NCH):
                b.inc(s.dma_start(out=taps['x2o'][ec * 128:(ec + 1) * 128, :],
                                  in_=x2h[ec * 128:(ec + 1) * 128, :]), 'sS', 16, ('x2o', ec))
            for hc in range(NHC):
                b.inc(s.dma_start(out=taps['g'][hc * 128:(hc + 1) * 128, :],
                                  in_=gth[hc * 128:(hc + 1) * 128, :]), 'sS', 16, ('gto', hc))
            b.wait(s, ('gto', NHC - 1))
        b.phase('TAP', [('sync', ts2)])
        es.close()
        return nc, taps

    XS6 = [0, 1024]
    PBf6 = [0, 1024]

    def p6_sync(s):
        for th in range(2):
            for half, buf in ((0, BH), (1, BQ)):
                in_ap = AP(gth, (half * 32 * 128) * T + th * 1024,
                           [[T, 128], [128 * T, 32], [1, 1024]])
                b.inc(s.dma_start(out=buf[:, 0:32 * 1024].rearrange('p (c n) -> p c n', c=32),
                                  in_=in_ap), 'sS', 16, ('gh', th, half))
            for dc in range(NCH):
                if dc >= 2:
                    b.wait(s, ('dnv', th, dc - 2, 1))
                b.inc(s.dma_start(out=BKf[:, XS6[dc % 2]:XS6[dc % 2] + 1024],
                                  in_=x2h[dc * 128:(dc + 1) * 128, th * 1024:(th + 1) * 1024]),
                      'sS', 16, ('x26', th, dc))
                b.wait(s, ('dnv', th, dc, 1))
                b.inc(s.dma_start(out=out[dc * 128:(dc + 1) * 128, th * 1024:(th + 1) * 1024],
                                  in_=BVf[:, PBf6[dc % 2]:PBf6[dc % 2] + 1024]),
                      'sS', 16, ('ost', th, dc))
        b.wait(s, ('ost', 1, NCH - 1))

    def p6_gpsimd(g):
        for th in range(2):
            for dc in range(NCH):
                idx = th * NCH + dc
                if idx >= 2:
                    pidx = idx - 2
                    b.wait(g, ('dnmm', pidx // NCH, pidx % NCH))
                for piece in range(4):
                    in_ap = AP(wd, dc * 128 + piece * 16 * 128 * D,
                               [[D, 128], [128 * D, 16], [1, 128]])
                    if dc % 2 == 0:
                        dst = SCRb[:, piece * 2048:(piece + 1) * 2048]
                    elif piece < 2:
                        dst = cosTb[:, piece * 2048:(piece + 1) * 2048]
                    else:
                        dst = sinTb[:, (piece - 2) * 2048:(piece - 1) * 2048]
                    b.inc(g.dma_start(out=dst.rearrange('p (c n) -> p c n', c=16),
                                      in_=in_ap), 'sG', 16,
                          ('wd', th, dc) if piece == 3 else None)

    def p6_tensor(t):
        for th in range(2):
            b.wait(t, ('gh', th, 0))
            b.wait(t, ('gh', th, 1))
            for dc in range(NCH):
                b.wait(t, ('wd', th, dc))
                for tq2 in range(2):
                    slot = (2 * dc + tq2) % 4
                    if 2 * dc + tq2 >= 4:
                        pr = 2 * dc + tq2 - 4
                        b.wait(t, ('dnv', th, pr // 2, pr % 2))
                    elif th == 1 and 2 * dc + tq2 < 4:
                        pr = 2 * NCH + 2 * dc + tq2 - 4
                        b.wait(t, ('dnv', 0, pr // 2, pr % 2))
                    for hc in range(NHC):
                        buf = BH if hc < 32 else BQ
                        if dc % 2 == 0:
                            wsl = SCRb[:, hc * 128:(hc + 1) * 128]
                        elif hc < 32:
                            wsl = cosTb[:, hc * 128:(hc + 1) * 128]
                        else:
                            wsl = sinTb[:, (hc - 32) * 128:(hc - 31) * 128]
                        ins = t.matmul(psA[:, slot * 512:(slot + 1) * 512], wsl,
                                       buf[:, (hc % 32) * 1024 + tq2 * 512:(hc % 32) * 1024 + (tq2 + 1) * 512],
                                       start=(hc == 0), stop=(hc == NHC - 1))
                    b.inc(ins, 'sT', 1, ('dn', th, dc, tq2))
                b.mark('sT', ('dnmm', th, dc))

    def p6_vector(v):
        for th in range(2):
            for dc in range(NCH):
                b.wait(v, ('x26', th, dc))
                for tq2 in range(2):
                    slot = (2 * dc + tq2) % 4
                    b.wait(v, ('dn', th, dc, tq2))
                    if dc >= 2:
                        b.wait(v, ('ost', th, dc - 2))
                    elif th == 1:
                        b.wait(v, ('ost', 0, NCH - 2 + dc))
                    b.inc(v.tensor_tensor(BVf[:, PBf6[dc % 2] + tq2 * 512:PBf6[dc % 2] + (tq2 + 1) * 512],
                                          psA[:, slot * 512:(slot + 1) * 512],
                                          BKf[:, XS6[dc % 2] + tq2 * 512:XS6[dc % 2] + (tq2 + 1) * 512],
                                          ALU.add), 'sV', 1, ('dnv', th, dc, tq2))

    b.phase('P6', [('sync', p6_sync), ('gpsimd', p6_gpsimd), ('tensor', p6_tensor),
                   ('vector', p6_vector)])
    es.close()
    return nc, {'out': out}



_NCACHE = {}


def _get_program(tap=None):
    if tap not in _NCACHE:
        _NCACHE[tap] = build_program(tap)
    return _NCACHE[tap]


def make_inputs(L0_attn_x, L0_attn_wq, L0_attn_wk, L0_attn_wv, L0_attn_wo,
                L0_ffn_w_gate, L0_ffn_w_up, L0_ffn_w_down):
    pq = _perm_q()
    pk = _perm_k()
    cos, sin = _rope_tables()
    wqT = np.ascontiguousarray(np.asarray(L0_attn_wq).T[:, pq], dtype=np.float32)
    wkT = np.ascontiguousarray(np.asarray(L0_attn_wk).T[:, pk], dtype=np.float32)
    wvT = np.ascontiguousarray(np.asarray(L0_attn_wv).T, dtype=np.float32)
    woT = np.ascontiguousarray(np.asarray(L0_attn_wo).T, dtype=np.float32)
    wgT = np.ascontiguousarray(np.asarray(L0_ffn_w_gate).T, dtype=np.float32)
    wuT = np.ascontiguousarray(np.asarray(L0_ffn_w_up).T, dtype=np.float32)
    wdT = np.ascontiguousarray(np.asarray(L0_ffn_w_down).T, dtype=np.float32)
    diag = np.where(np.arange(128)[None, :] >= np.arange(128)[:, None],
                    np.float32(1.0), np.float32(0.0)).astype(np.float32)
    shared = dict(wq=wqT, wk=wkT, wv=wvT, wo=woT, wg=wgT, wu=wuT, wd=wdT,
                  cos_in=cos, sin_in=sin,
                  ident_in=np.eye(128, dtype=np.float32),
                  diag_in=diag,
                  onesc_in=np.ones((128, 1), np.float32),
                  onesf_in=np.ones((128, 128), np.float32))
    x = np.asarray(L0_attn_x)
    in_maps = []
    for core in range(8):
        im = dict(shared)
        im['xf'] = np.ascontiguousarray(x[core].T, dtype=np.float32)
        in_maps.append(im)
    return in_maps


def kernel(**inputs):
    nc, outs = _get_program(None)
    in_maps = make_inputs(**inputs)
    res = run_bass_kernel_spmd(nc, in_maps, core_ids=list(range(8)))
    return np.stack([r['outf'].T for r in res.results], axis=0).astype(np.float32)


# version 14
# speedup vs baseline: 1.6477x; 1.2656x over previous
import sys
import numpy as np

sys.path.insert(0, '/opt/trn_rl_repo')

import concourse.bass as bass
import concourse.mybir as mybir
from concourse.ap import AP
from concourse.bass_utils import run_bass_kernel_spmd
from contextlib import ExitStack

F32 = mybir.dt.float32
BF16 = mybir.dt.bfloat16
AF = mybir.ActivationFunctionType
ALU = mybir.AluOpType
AX = mybir.AxisListType

D = 2048
T = 2048
H = 32
HD = 64
DH = 8192
NCH = 16
NHC = DH // 128
NTC = T // 128
NTQ = T // 512
NEG = -1.0e9


def _perm_q():
    perm = np.zeros(2048, np.int64)
    for c in range(8):
        for j in range(4):
            h = 4 * (j + 4 * (1 if c >= 4 else 0)) + (c % 4)
            for i in range(32):
                perm[c * 128 + 32 * j + i] = h * 64 + 2 * i
                perm[(c + 8) * 128 + 32 * j + i] = h * 64 + 2 * i + 1
    return perm


def _perm_k():
    perm = np.zeros(512, np.int64)
    for c in range(2):
        for j in range(4):
            g = 4 * c + j
            for i in range(32):
                perm[c * 128 + 32 * j + i] = g * 64 + 2 * i
                perm[(c + 2) * 128 + 32 * j + i] = g * 64 + 2 * i + 1
    return perm


def _rope_tables():
    theta = 1.0 / (10000.0 ** (np.arange(0, HD, 2, dtype=np.float64) / HD))
    t = np.arange(T, dtype=np.float64)
    fr = theta[:, None] * t[None, :]
    cos = np.cos(fr).astype(np.float32)
    sin = np.sin(fr).astype(np.float32)
    return np.tile(cos, (4, 1)), np.tile(sin, (4, 1))



class _Dry:
    def then_inc(self, *a, **k):
        return self

    def __getattr__(self, name):
        return lambda *a, **k: self

_DRY = _Dry()


class B:

    def __init__(self, nc, es):
        self.nc = nc
        self.es = es
        self.sems = {}
        self.cnt = {}
        self.marks = {}
        self.want = []
        self.emitting = False

    def sem(self, name):
        self.sems[name] = self.es.enter_context(self.nc.semaphore(name))
        self.cnt[name] = 0

    def inc(self, instr, s, k=1, label=None):
        self.cnt[s] += k
        if self.emitting and instr is not _DRY and k > 0:
            instr.then_inc(self.sems[s], k)
        if label is not None:
            if self.emitting:
                assert self.marks[label] == (s, self.cnt[s]), (label, self.marks.get(label), (s, self.cnt[s]))
            else:
                assert label not in self.marks, label
                self.marks[label] = (s, self.cnt[s])
        return instr

    def mark(self, s, label):
        return self.inc(_DRY, s, 0, label)

    def wait(self, eng, label):
        if not self.emitting:
            self.want.append(label)
            return
        s, v = self.marks[label]
        eng.wait_ge(self.sems[s], v)

    def wait_maybe(self, eng, label):
        if not self.emitting:
            return
        if label in self.marks:
            s, v = self.marks[label]
            eng.wait_ge(self.sems[s], v)

    def phase(self, name, emitters):
        snap = dict(self.cnt)
        self.emitting = False
        self.want = []
        for en, fn in emitters:
            fn(_DRY)
        missing = [w for w in self.want if w not in self.marks]
        assert not missing, ('unresolved waits', missing[:5])
        self.cnt = snap
        self.emitting = True
        with self.nc.Block(name) as blk:
            for en, fn in emitters:
                getattr(blk, en)(fn)



def build_program(tap=None):
    nc = bass.Bass('TRN2', target_bir_lowering=False, detect_race_conditions=False)

    xf = nc.dram_tensor('xf', [D, T], F32, kind='ExternalInput')
    wq = nc.dram_tensor('wq', [D, D], F32, kind='ExternalInput')
    wk = nc.dram_tensor('wk', [D, 512], F32, kind='ExternalInput')
    wv = nc.dram_tensor('wv', [D, 512], F32, kind='ExternalInput')
    wo = nc.dram_tensor('wo', [D, D], F32, kind='ExternalInput')
    wg = nc.dram_tensor('wg', [D, DH], F32, kind='ExternalInput')
    wu = nc.dram_tensor('wu', [D, DH], F32, kind='ExternalInput')
    wd = nc.dram_tensor('wd', [DH, D], F32, kind='ExternalInput')
    cos_in = nc.dram_tensor('cos_in', [128, T], F32, kind='ExternalInput')
    sin_in = nc.dram_tensor('sin_in', [128, T], F32, kind='ExternalInput')
    ident_in = nc.dram_tensor('ident_in', [128, 128], F32, kind='ExternalInput')
    diag_in = nc.dram_tensor('diag_in', [128, 128], F32, kind='ExternalInput')
    onesc_in = nc.dram_tensor('onesc_in', [128, 1], F32, kind='ExternalInput')
    onesf_in = nc.dram_tensor('onesf_in', [128, 128], F32, kind='ExternalInput')

    x2h = nc.dram_tensor('x2h', [D, T], F32, kind='Internal')
    gth = nc.dram_tensor('gth', [DH, T], BF16, kind='Internal')

    def fm_pair(dram, sbuf_ap, nch, w):
        return (AP(dram, 0, [[w * 1, 128] if False else [dram.shape[1], 128],
                             [128 * dram.shape[1], nch], [1, w]]),
                sbuf_ap.rearrange('p (c t) -> p c t', c=nch))

    taps = {}
    out = None
    if tap is None:
        out = nc.dram_tensor('outf', [D, T], F32, kind='ExternalOutput')

    es = ExitStack()
    sb = lambda n, s, d: es.enter_context(nc.sbuf_tensor(n, s, d))

    BH = sb('BH', [128, NCH * T], BF16)
    BQ = sb('BQ', [128, NCH * T], BF16)
    BK = sb('BK', [128, 4 * T], BF16)
    BV = sb('BV', [128, NTC * 520], BF16)
    cosT = sb('cosT', [128, T], F32)
    sinT = sb('sinT', [128, T], F32)
    identb = sb('identb', [128, 128], BF16)
    diagm = sb('diagm', [128, 128], F32)
    onesc = sb('onesc', [128, 1], BF16)
    onesf = sb('onesf', [128, 128], F32)
    identf = sb('identf', [128, 128], F32)
    STT = sb('STT', [128, 160], F32)
    SCR = sb('SCR', [128, 5120], F32)
    SCRb = SCR.bitcast(BF16)
    BKf = BK.bitcast(F32)
    BVf = BV.bitcast(F32)
    cosTb = cosT.bitcast(BF16)
    sinTb = sinT.bitcast(BF16)
    mt = [sb('m%d' % i, [128, 1], F32) for i in range(2)]
    nmt = [sb('nm%d' % i, [128, 1], F32) for i in range(2)]
    lt = [sb('l%d' % i, [128, 1], F32) for i in range(2)]
    rt = [sb('r%d' % i, [128, 1], F32) for i in range(2)]

    RP = [4096, 4608]
    WSLOT = [0, 2048, 4096, 6144]
    PHB = [0, 2048]
    PTB = [0, 2048]
    GSL = [0, 2048]
    SIL = [4096, 4608]
    SQ1 = [0, 512]
    DGP = [8192, 8320]
    XS = [0, 2048]
    PBf = [0, 2048]

    psA = es.enter_context(nc.psum_tensor('psA', [128, 2048], F32))
    psB = es.enter_context(nc.psum_tensor('psB', [128, 2048], F32))
    psAb = psA.bitcast(BF16)
    psBb = psB.bitcast(BF16)
    diagmb = diagm.bitcast(BF16)

    b = B(nc, es)
    for s in ('sG', 'sS', 'sT', 'sV', 'sC'):
        b.sem(s)

    def R(i, lo=0, hi=16):
        return STT[:, i * 16 + lo:i * 16 + hi]

    def step(eng, sem, instr):
        b.inc(instr, sem, 1)
        if b.emitting:
            eng.wait_ge(b.sems[sem], b.cnt[sem])
        return instr

    def ln_finish_vec(v, ph):
        step(v, 'sV', v.tensor_scalar(R(2), R(0), 1.0 / D, None, ALU.mult))
        step(v, 'sV', v.tensor_scalar(R(3), R(1), 1.0 / D, None, ALU.mult))
        step(v, 'sV', v.tensor_tensor(R(4), R(2), R(2), ALU.mult))
        step(v, 'sV', v.tensor_tensor(R(4), R(3), R(4), ALU.subtract))
        step(v, 'sV', v.tensor_scalar(R(4), R(4), 1e-5, None, ALU.add))
        b.inc(v.reciprocal(R(5), R(4)), 'sV', 1, ph + '_rec')
        b.wait(v, ph + '_sqrt')
        step(v, 'sV', v.tensor_tensor(R(5), R(6), R(6), ALU.mult))
        step(v, 'sV', v.tensor_tensor(R(5), R(5), R(4), ALU.mult))
        step(v, 'sV', v.tensor_scalar(R(5), R(5), -1.0, 3.0, ALU.mult, ALU.add))
        step(v, 'sV', v.tensor_tensor(R(5), R(6), R(5), ALU.mult))
        step(v, 'sV', v.tensor_scalar(R(7), R(5), 0.5, None, ALU.mult))
        step(v, 'sV', v.tensor_tensor(R(8), R(2), R(7), ALU.mult))
        b.inc(v.tensor_scalar(R(8), R(8), -1.0, None, ALU.mult), 'sV', 1, ph + '_ab')
        if b.emitting:
            v.wait_ge(b.sems['sV'], b.cnt['sV'])

    def ln_finish_sca(s, ph):
        b.wait(s, ph + '_rec')
        b.inc(s.activation(R(6), R(5), AF.Sqrt), 'sC', 1, ph + '_sqrt')

    DGB = [RP[0], RP[0] + 128]

    def bcast_vec(v, ph):
        b.wait(v, 'c2_done')
        b.wait(v, ph + '_ab')
        for idx in range(32):
            tc, isb = idx // 2, idx % 2
            if idx >= 2:
                b.wait(v, (ph + '_bcm', idx - 2))
            col = R(8 if isb else 7, tc, tc + 1)
            b.inc(v.tensor_scalar(SCR[:, DGB[idx % 2]:DGB[idx % 2] + 128],
                                  identf[:, :], col, None, ALU.mult),
                  'sV', 1, (ph + '_dg', idx))

    def bcast_tensor(t, ph):
        for idx in range(32):
            tc, isb = idx // 2, idx % 2
            b.wait(t, (ph + '_dg', idx))
            ps = psB if isb else psA
            ins = t.matmul(ps[:, tc * 128:(tc + 1) * 128], onesf[:, :],
                           SCR[:, DGB[idx % 2]:DGB[idx % 2] + 128],
                           start=True, stop=True)
            b.inc(ins, 'sT', 1, (ph + '_bcm', idx))
        b.mark('sT', ph + '_bc')

    def stats_mms(t, ph, src_of, sq_of, sqwait):
        for wave in range(4):
            if wave >= 1:
                b.wait(t, (ph + '_evw', wave - 1))
            for c in range(NCH):
                b.wait(t, (ph + '_st_x', c) if ph == 'ln1' else ('x2bf', c))
                b.wait(t, sqwait(wave, c))
                for tbl in range(4):
                    tb = wave * 4 + tbl
                    t.matmul(psA[:, tbl * 512 + wave:tbl * 512 + wave + 1],
                             src_of(c)[:, tb * 128:(tb + 1) * 128],
                             onesc[:, :], start=(c == 0), stop=(c == NCH - 1))
                    ins = t.matmul(psB[:, tbl * 512 + wave:tbl * 512 + wave + 1],
                                   sq_of(c)[:, tbl * 128:(tbl + 1) * 128],
                                   onesc[:, :], start=(c == 0), stop=(c == NCH - 1))
                b.inc(ins, 'sT', 1, (ph + '_sw', wave, c))
            b.inc(_DRY, 'sT', 0, (ph + '_wave', wave))

    def stats_evac(v, ph):
        for wave in range(4):
            b.wait(v, (ph + '_wave', wave))
            b.inc(v.tensor_copy(R(0, wave * 4, wave * 4 + 4),
                                AP(psA, wave, [[2048, 128], [512, 4]])), 'sV', 1)
            b.inc(v.tensor_copy(R(1, wave * 4, wave * 4 + 4),
                                AP(psB, wave, [[2048, 128], [512, 4]])),
                  'sV', 1, (ph + '_evw', wave))
        if b.emitting:
            v.wait_ge(b.sems['sV'], b.cnt['sV'])
        b.mark('sV', ph + '_ev')

    def normalize(v, ph, src_off, dst, c):
        for tq in range(NTQ):
            sl = slice(tq * 512, (tq + 1) * 512)
            step(v, 'sV', v.tensor_tensor(SCR[:, RP[0]:RP[0] + 512],
                                          BKf[:, src_off + tq * 512:src_off + (tq + 1) * 512],
                                          psA[:, sl], ALU.mult))
            ins = v.tensor_tensor(dst[:, c * T + tq * 512:c * T + (tq + 1) * 512],
                                  SCR[:, RP[0]:RP[0] + 512], psB[:, sl], ALU.add)
            if tq < NTQ - 1:
                step(v, 'sV', ins)
            else:
                b.inc(ins, 'sV', 1)
                if b.emitting:
                    v.wait_ge(b.sems['sV'], b.cnt['sV'])
        return ins

    def p1_gpsimd(g):
        b.inc(g.dma_start(out=identb[:, :], in_=ident_in[:, :]), 'sG', 16)
        b.inc(g.dma_start(out=diagmb[:, 0:128], in_=diag_in[:, :]), 'sG', 16)
        b.inc(g.dma_start(out=onesc[:, :], in_=onesc_in[:, :]), 'sG', 16, 'c_done')
        for c in range(NCH):
            b.inc(g.dma_start(out=BQ[:, c * T:(c + 1) * T],
                              in_=xf[c * 128:(c + 1) * 128, :]), 'sG', 16, ('xbf', c))

    def p1_sync(s):
        b.inc(s.dma_start(out=cosT[:, :], in_=cos_in[:, :]), 'sS', 16)
        b.inc(s.dma_start(out=sinT[:, :], in_=sin_in[:, :]), 'sS', 16)
        b.inc(s.dma_start(out=onesf[:, :], in_=onesf_in[:, :]), 'sS', 16)
        b.inc(s.dma_start(out=identf[:, :], in_=ident_in[:, :]), 'sS', 16, 'c2_done')
        for c in range(NCH):
            if c >= 2:
                b.wait(s, ('norm1', c - 2))
            b.inc(s.dma_start(out=BKf[:, XS[c % 2]:XS[c % 2] + 2048],
                              in_=xf[c * 128:(c + 1) * 128, :]), 'sS', 16, ('xin', c))

    def p1_scalar(s):
        b.wait(s, 'c_done')
        for wave in range(4):
            for c in range(NCH):
                b.wait(s, ('xbf', c))
                idx = wave * NCH + c
                if idx >= 2:
                    pidx = idx - 2
                    b.wait(s, ('ln1_sw', pidx // NCH, pidx % NCH))
                b.inc(s.activation(SCRb[:, SQ1[c % 2]:SQ1[c % 2] + 512],
                                   BQ[:, c * T + wave * 512:c * T + wave * 512 + 512],
                                   AF.Square), 'sC', 1, ('sq1', wave, c))
        ln_finish_sca(s, 'ln1')

    def p1_tensor(t):
        for c in range(NCH):
            b.wait(t, ('xbf', c))
            b.mark('sT', ('ln1_st_x', c))
        stats_mms(t, 'ln1', lambda c: BQ[:, c * T:(c + 1) * T],
                  lambda c: SCRb[:, SQ1[c % 2]:SQ1[c % 2] + 512],
                  lambda wave, c: ('sq1', wave, c))
        bcast_tensor(t, 'ln1')

    def p1_vector(v):
        stats_evac(v, 'ln1')
        ln_finish_vec(v, 'ln1')
        bcast_vec(v, 'ln1')
        b.wait(v, 'ln1_bc')
        for c in range(NCH):
            b.wait(v, ('xin', c))
            normalize(v, 'ln1', XS[c % 2], BH, c)
            b.mark('sV', ('norm1', c))

    b.phase('P1', [('gpsimd', p1_gpsimd), ('sync', p1_sync), ('scalar', p1_scalar),
                   ('tensor', p1_tensor), ('vector', p1_vector)])

    if tap == 'h':
        taps['h'] = nc.dram_tensor('tap_h', [D, T], BF16, kind='ExternalOutput')
        taps['stt'] = nc.dram_tensor('tap_stt', [128, 160], F32, kind='ExternalOutput')

        def ts_(s):
            o_ap, i_ap = fm_pair(taps['h'], BH[:, :], NCH, T)
            b.inc(s.dma_start(out=o_ap, in_=i_ap), 'sS', 16, 'tap_st')
            b.inc(s.dma_start(out=taps['stt'][:, :], in_=STT[:, :]), 'sS', 16, 'tap_st2')
            b.wait(s, 'tap_st2')
        b.phase('TAP', [('sync', ts_)])
        es.close()
        return nc, taps

    def wblk_ap(w, ncols, cblk, blkw=128):
        rows = w.shape[0]
        return AP(w, cblk * blkw, [[ncols, 128], [128 * ncols, rows // 128], [1, blkw]])

    def wslot(i, n=2048):
        return SCRb[:, WSLOT[i]:WSLOT[i] + n]

    def wslot_r(i, k=16, n=2048):
        return SCRb[:, WSLOT[i]:WSLOT[i] + n].rearrange('p (k n) -> p k n', k=k)

    def p2_gpsimd(g):
        for p in range(8):
            if p >= 2:
                b.wait(g, ('qmm', p - 2, NTQ - 1))
            b.inc(g.dma_start(out=wslot_r(2 * (p % 2)), in_=wblk_ap(wq, D, p)),
                  'sG', 16, ('wqe', p))
            b.inc(g.dma_start(out=wslot_r(2 * (p % 2) + 1), in_=wblk_ap(wq, D, p + 8)),
                  'sG', 16, ('wqo', p))
        for c in range(4):
            b.inc(g.dma_start(out=BV[:, c * 2048:(c + 1) * 2048]
                              .rearrange('p (k n) -> p k n', k=16),
                              in_=wblk_ap(wk, 512, c)), 'sG', 16, ('wk', c))
        b.wait(g, ('qmm', 6, NTQ - 1))
        b.inc(g.dma_start(out=SCRb[:, WSLOT[0]:WSLOT[0] + 4096]
                          .rearrange('p (k n) -> p k n', k=16),
                          in_=wblk_ap(wv, 512, 0, 256)), 'sG', 16, ('wv', 0))
        b.wait(g, ('qmm', 7, NTQ - 1))
        b.inc(g.dma_start(out=SCRb[:, WSLOT[2]:WSLOT[2] + 4096]
                          .rearrange('p (k n) -> p k n', k=16),
                          in_=wblk_ap(wv, 512, 1, 256)), 'sG', 16, ('wv', 1))

    def p2_tensor(t):
        for p in range(8):
            b.wait(t, ('wqe', p))
            b.wait(t, ('wqo', p))
            for tq in range(NTQ):
                if p >= 1:
                    b.wait(t, ('rope_q', p - 1, tq))
                se, so = 2 * (p % 2), 2 * (p % 2) + 1
                for K in range(NCH):
                    t.matmul(psA[:, tq * 512:(tq + 1) * 512],
                             wslot(se)[:, K * 128:(K + 1) * 128],
                             BH[:, K * T + tq * 512:K * T + (tq + 1) * 512],
                             start=(K == 0), stop=(K == NCH - 1))
                for K in range(NCH):
                    ins = t.matmul(psB[:, tq * 512:(tq + 1) * 512],
                                   wslot(so)[:, K * 128:(K + 1) * 128],
                                   BH[:, K * T + tq * 512:K * T + (tq + 1) * 512],
                                   start=(K == 0), stop=(K == NCH - 1))
                b.inc(ins, 'sT', 1, ('qmm', p, tq))
        for p in range(2):
            b.wait(t, ('wk', p))
            b.wait(t, ('wk', p + 2))
            for tq in range(NTQ):
                b.wait(t, ('rope_q', 7, tq))
                if p == 1:
                    b.wait(t, ('rope_k', 0, tq))
                for K in range(NCH):
                    t.matmul(psA[:, tq * 512:(tq + 1) * 512],
                             BV[:, p * 2048 + K * 128:p * 2048 + (K + 1) * 128],
                             BH[:, K * T + tq * 512:K * T + (tq + 1) * 512],
                             start=(K == 0), stop=(K == NCH - 1))
                for K in range(NCH):
                    ins = t.matmul(psB[:, tq * 512:(tq + 1) * 512],
                                   BV[:, (p + 2) * 2048 + K * 128:(p + 2) * 2048 + (K + 1) * 128],
                                   BH[:, K * T + tq * 512:K * T + (tq + 1) * 512],
                                   start=(K == 0), stop=(K == NCH - 1))
                b.inc(ins, 'sT', 1, ('kmm', p, tq))
        for tc in range(NTC):
            for oh in range(2):
                b.wait(t, ('wv', oh))
                b.wait(t, ('rope_k', 1, NTQ - 1))
                if tc >= 2:
                    b.wait(t, ('vev', tc - 2, oh))
                slot = (2 * tc + oh) % 4
                woff = WSLOT[0] if oh == 0 else WSLOT[2]
                for K in range(NCH):
                    ins = t.matmul(psA[:, slot * 512:slot * 512 + 256],
                                   BH[:, K * T + tc * 128:K * T + (tc + 1) * 128],
                                   SCRb[:, woff + K * 256:woff + (K + 1) * 256],
                                   start=(K == 0), stop=(K == NCH - 1))
                b.inc(ins, 'sT', 1, ('vmm', tc, oh))

    def _rope(v, p, tq, is_q):
        sl = slice(tq * 512, (tq + 1) * 512)
        cs = cosT[:, tq * 512:(tq + 1) * 512]
        sn = sinT[:, tq * 512:(tq + 1) * 512]
        t1 = SCR[:, RP[0]:RP[0] + 512]
        t2 = SCR[:, RP[1]:RP[1] + 512]
        dst, off = (BQ, 8) if is_q else (BK, 2)
        de = dst[:, p * T + tq * 512:p * T + (tq + 1) * 512]
        do = dst[:, (p + off) * T + tq * 512:(p + off) * T + (tq + 1) * 512]
        b.inc(v.tensor_tensor(t1, psA[:, sl], cs, ALU.mult), 'sV', 1)
        step(v, 'sV', v.tensor_tensor(t2, psB[:, sl], sn, ALU.mult))
        step(v, 'sV', v.tensor_tensor(de, t1, t2, ALU.subtract))
        b.inc(v.tensor_tensor(t1, psA[:, sl], sn, ALU.mult), 'sV', 1)
        step(v, 'sV', v.tensor_tensor(t2, psB[:, sl], cs, ALU.mult))
        ins = v.tensor_tensor(do, t1, t2, ALU.add)
        b.inc(ins, 'sV', 1)
        if b.emitting:
            v.wait_ge(b.sems['sV'], b.cnt['sV'])
        return ins

    def p2_vector(v):
        for p in range(8):
            for tq in range(NTQ):
                b.wait(v, ('qmm', p, tq))
                _rope(v, p, tq, True)
                b.mark('sV', ('rope_q', p, tq))
        for p in range(2):
            for tq in range(NTQ):
                b.wait(v, ('kmm', p, tq))
                _rope(v, p, tq, False)
                b.mark('sV', ('rope_k', p, tq))
        for tc in range(NTC):
            for oh in range(2):
                b.wait(v, ('vmm', tc, oh))
                slot = (2 * tc + oh) % 4
                b.inc(v.tensor_copy(AP(BV, tc * 520 + oh * 260,
                                       [[NTC * 520, 128], [65, 4], [1, 64]]),
                                    psA[:, slot * 512:slot * 512 + 256]
                                    .rearrange('p (s c) -> p s c', c=64)),
                      'sV', 1, ('vev', tc, oh))
        b.inc(v.tensor_copy(AP(BV, 64, [[NTC * 520, 128], [65, 128]]),
                            onesf[:, :]), 'sV', 1, 'vones')

    b.phase('P2', [('gpsimd', p2_gpsimd), ('tensor', p2_tensor), ('vector', p2_vector)])

    if tap == 'qkv':
        taps['q'] = nc.dram_tensor('tap_q', [D, T], BF16, kind='ExternalOutput')
        taps['k'] = nc.dram_tensor('tap_k', [512, T], BF16, kind='ExternalOutput')
        taps['v'] = nc.dram_tensor('tap_v', [T, 512], BF16, kind='ExternalOutput')

        def ts_(s):
            o_ap, i_ap = fm_pair(taps['q'], BQ[:, :], NCH, T)
            b.inc(s.dma_start(out=o_ap, in_=i_ap), 'sS', 16, 'tap_qst')
            o_ap, i_ap = fm_pair(taps['k'], BK[:, :], 4, T)
            b.inc(s.dma_start(out=o_ap, in_=i_ap), 'sS', 16, 'tap_kst')
            o_ap, i_ap = fm_pair(taps['v'], BV[:, :], NTC, 512)
            b.inc(s.dma_start(out=o_ap, in_=i_ap), 'sS', 16, 'tap_vst')
            b.wait(s, 'tap_vst')
        b.phase('TAP', [('sync', ts_)])
        es.close()
        return nc, taps

    ITERS = [(h, qt) for h in range(H) for qt in range(NTC)]
    n3 = len(ITERS)

    _SLOT_COL = [0, 512, 1024, 1536, 1024, 1536]
    alloc3 = []
    claim3 = []
    _last = [None] * 6
    _p = 0
    for _i, (_h3, _qt3) in enumerate(ITERS):
        _bi = (_qt3 + 4) // 4
        if _p < 4 and _p + _bi > 4:
            _p = 4
        if _p >= 4 and _p + _bi > 6:
            _p = 0
        claim3.append(sorted({_last[s] for s in range(_p, _p + _bi)
                              if _last[s] is not None}))
        for s in range(_p, _p + _bi):
            _last[s] = _i
        alloc3.append((0 if _p < 4 else 1, _SLOT_COL[_p]))
        _p += _bi
        if _p >= 6:
            _p = 0

    PT3 = [0, 2048, 4096]
    YSB = [6144, 6208]
    YACC = [0, 512]
    LAC = [64, 576]
    YTP = [256, 768]

    def pt3(i, lo, hi):
        return SCRb[:, PT3[i % 3] + lo:PT3[i % 3] + hi]

    def sc_ps(i):
        reg, base = alloc3[i]
        return (psA if reg == 0 else psB), base

    def emit_sc(t, i):
        h, qt = ITERS[i]
        g = h // 4
        j = g % 4
        c_ev = h % 4 + 4 * (1 if g >= 4 else 0)
        ps, base = sc_ps(i)
        for it_ in claim3[i]:
            b.wait(t, ('exp', it_))
        rows = slice(32 * j, 32 * j + 32)
        for sj in range(qt + 1):
            out = ps[:, base + sj * 128:base + (sj + 1) * 128]
            t.matmul(out,
                     BK[rows, (g // 4) * T + sj * 128:(g // 4) * T + (sj + 1) * 128],
                     BQ[rows, c_ev * T + qt * 128:c_ev * T + (qt + 1) * 128],
                     start=True, stop=False, tile_position=(32 * j, 0))
            ins = t.matmul(out,
                           BK[rows, (g // 4 + 2) * T + sj * 128:(g // 4 + 2) * T + (sj + 1) * 128],
                           BQ[rows, (c_ev + 8) * T + qt * 128:(c_ev + 8) * T + (qt + 1) * 128],
                           start=False, stop=True, tile_position=(32 * j, 0))
        b.inc(ins, 'sT', 1, ('sc', i))

    def emit_pv(t, i):
        h, qt = ITERS[i]
        g = h // 4
        par = i % 2
        if i >= 2:
            b.wait(t, ('yev', i - 2))
        b.wait(t, ('exp', i))
        for sj in range(qt + 1):
            if sj == qt:
                b.wait(t, ('dz', i))
            ins = t.matmul(psB[:, YACC[par]:YACC[par] + 65],
                           pt3(i, sj * 128, (sj + 1) * 128),
                           BV[:, sj * 520 + g * 65:sj * 520 + g * 65 + 65],
                           start=(sj == 0), stop=(sj == qt))
        b.inc(ins, 'sT', 1, ('pv', i))

    def emit_ytr(t, i):
        h, qt = ITERS[i]
        par = i % 2
        b.wait(t, ('ysc', i))
        if i >= 2:
            b.wait(t, ('yev', i - 2))
        rws = slice(0, 64) if h % 2 == 0 else slice(64, 128)
        tpos = (0, 0) if h % 2 == 0 else (0, 64)
        ins = t.matmul(psB[rws, YTP[par]:YTP[par] + 128],
                       SCRb[:, YSB[par]:YSB[par] + 64],
                       identb[:, :], start=True, stop=True, tile_position=tpos)
        b.inc(ins, 'sT', 1, ('ytr', i))

    def p3_tensor(t):
        for x in range(n3 + 3):
            if x < n3:
                emit_sc(t, x)
            if 0 <= x - 2 < n3:
                emit_pv(t, x - 2)
            if 0 <= x - 3 < n3:
                emit_ytr(t, x - 3)

    def p3_vector(v):
        for x in range(n3 + 3):
            if x < n3:
                i = x
                h, qt = ITERS[i]
                S = (qt + 1) * 128
                b.wait(v, ('exp', i))
                b.inc(v.tensor_tensor(pt3(i, qt * 128, S), pt3(i, qt * 128, S),
                                      diagmb[:, 0:128], ALU.mult), 'sV', 1, ('dz', i))
            if 0 <= x - 2 < n3:
                iy = x - 2
                hp, qp = ITERS[iy]
                b.wait(v, ('ytr', iy))
                rws = slice(0, 64) if hp % 2 == 0 else slice(64, 128)
                b.inc(v.tensor_copy(BH[rws, (hp // 2) * T + qp * 128:(hp // 2) * T + (qp + 1) * 128],
                                    psB[rws, YTP[iy % 2]:YTP[iy % 2] + 128]),
                      'sV', 1, ('yev', iy))
            if x < n3:
                i = x
                par = i % 2
                b.wait(v, ('pv', i))
                b.inc(v.reciprocal(rt[par][:, :], psB[:, LAC[par]:LAC[par] + 1]),
                      'sV', 1, ('rc', i))
                if i >= 2:
                    b.wait(v, ('ytr', i - 2))
                step(v, 'sV', v.tensor_copy(SCR[:, RP[0] + par * 64:RP[0] + par * 64 + 64],
                                            psB[:, YACC[par]:YACC[par] + 64]))
                b.wait(v, ('rc', i))
                b.inc(v.tensor_scalar(SCRb[:, YSB[par]:YSB[par] + 64],
                                      SCR[:, RP[0] + par * 64:RP[0] + par * 64 + 64],
                                      rt[par][:, :], None, ALU.mult), 'sV', 1, ('ysc', i))

    def p3_scalar(s):
        for i, (h, qt) in enumerate(ITERS):
            S = (qt + 1) * 128
            ps, base = sc_ps(i)
            b.wait(s, ('sc', i))
            if i >= 3:
                b.wait(s, ('pv', i - 3))
            b.inc(s.activation(pt3(i, 0, S), ps[:, base:base + S], AF.Exp,
                               scale=0.125), 'sC', 1, ('exp', i))

    b.phase('P3', [('tensor', p3_tensor), ('vector', p3_vector), ('scalar', p3_scalar)])

    if tap == 'y':
        taps['y'] = nc.dram_tensor('tap_y', [D, T], BF16, kind='ExternalOutput')

        def ts_(s):
            o_ap, i_ap = fm_pair(taps['y'], BH[:, :], NCH, T)
            b.inc(s.dma_start(out=o_ap, in_=i_ap), 'sS', 16, 'tap_st')
            b.wait(s, 'tap_st')
        b.phase('TAP', [('sync', ts_)])
        es.close()
        return nc, taps

    def p4_gpsimd(g):
        for ec in range(NCH):
            if ec >= 2:
                b.wait(g, ('wo_mm', ec - 2, NTQ - 1))
            b.inc(g.dma_start(out=wslot_r(ec % 2), in_=wblk_ap(wo, D, ec)),
                  'sG', 16, ('wo', ec))

    def p4_sync(s):
        for ec in range(NCH):
            if ec >= 2:
                b.wait(s, ('x2v', ec - 2, NTQ - 1))
            b.inc(s.dma_start(out=BKf[:, XS[ec % 2]:XS[ec % 2] + 2048],
                              in_=xf[ec * 128:(ec + 1) * 128, :]), 'sS', 16, ('x4', ec))
            b.wait(s, ('x2bf', ec))
            b.inc(s.dma_start(out=x2h[ec * 128:(ec + 1) * 128, :],
                              in_=BVf[:, PBf[ec % 2]:PBf[ec % 2] + 2048]), 'sS', 16, ('x2st', ec))
        if tap in ('x2', 'gated'):
            b.wait(s, ('x2bf', NCH - 1))

    def p4_tensor(t):
        for ec in range(NCH):
            b.wait(t, ('wo', ec))
            for tq in range(NTQ):
                if ec >= 1:
                    b.wait(t, ('x2v', ec - 1, tq))
                for K in range(NCH):
                    ins = t.matmul(psA[:, tq * 512:(tq + 1) * 512],
                                   wslot(ec % 2)[:, K * 128:(K + 1) * 128],
                                   BH[:, K * T + tq * 512:K * T + (tq + 1) * 512],
                                   start=(K == 0), stop=(K == NCH - 1))
                b.inc(ins, 'sT', 1, ('wo_mm', ec, tq))
        for tq in range(NTQ):
            b.wait(t, ('x2v', NCH - 1, tq))
        stats_mms(t, 'ln2', lambda c: BQ[:, c * T:(c + 1) * T],
                  lambda c: SCRb[:, SQ1[c % 2]:SQ1[c % 2] + 512],
                  lambda wave, c: ('sq2', wave, c))
        bcast_tensor(t, 'ln2')

    def p4_vector(v):
        for ec in range(NCH):
            for tq in range(NTQ):
                b.wait(v, ('wo_mm', ec, tq))
                if tq == 0:
                    b.wait(v, ('x4', ec))
                    if ec >= 2:
                        b.wait(v, ('x2st', ec - 2))
                        b.wait(v, ('x2bf', ec - 2))
                b.inc(v.tensor_tensor(BVf[:, PBf[ec % 2] + tq * 512:PBf[ec % 2] + (tq + 1) * 512],
                                      psA[:, tq * 512:(tq + 1) * 512],
                                      BKf[:, XS[ec % 2] + tq * 512:XS[ec % 2] + (tq + 1) * 512],
                                      ALU.add), 'sV', 1, ('x2v', ec, tq))
        stats_evac(v, 'ln2')
        ln_finish_vec(v, 'ln2')
        bcast_vec(v, 'ln2')

    def p4_scalar(s):
        for ec in range(NCH):
            b.wait(s, ('x2v', ec, NTQ - 1))
            b.inc(s.copy(BQ[:, ec * T:(ec + 1) * T],
                         BVf[:, PBf[ec % 2]:PBf[ec % 2] + 2048]), 'sC', 1, ('x2bf', ec))
        for wave in range(4):
            for ec in range(NCH):
                idx = wave * NCH + ec
                if idx >= 2:
                    pidx = idx - 2
                    b.wait(s, ('ln2_sw', pidx // NCH, pidx % NCH))
                b.wait(s, ('x2bf', ec))
                b.inc(s.activation(SCRb[:, SQ1[ec % 2]:SQ1[ec % 2] + 512],
                                   BQ[:, ec * T + wave * 512:ec * T + wave * 512 + 512],
                                   AF.Square), 'sC', 1, ('sq2', wave, ec))
        ln_finish_sca(s, 'ln2')

    b.phase('P4', [('gpsimd', p4_gpsimd), ('sync', p4_sync), ('tensor', p4_tensor),
                   ('vector', p4_vector), ('scalar', p4_scalar)])

    if tap == 'x2':
        taps['x2o'] = nc.dram_tensor('tap_x2o', [D, T], F32, kind='ExternalOutput')

        def ts_(s):
            for ec in range(NCH):
                b.inc(s.dma_start(out=taps['x2o'][ec * 128:(ec + 1) * 128, :],
                                  in_=x2h[ec * 128:(ec + 1) * 128, :]), 'sS', 16, ('x2o', ec))
            b.wait(s, ('x2o', NCH - 1))
        b.phase('TAP', [('sync', ts_)])
        es.close()
        return nc, taps

    def p5_sync(s):
        for c in range(NCH):
            if c >= 2:
                b.wait(s, ('h2', c - 2))
            b.inc(s.dma_start(out=BKf[:, XS[c % 2]:XS[c % 2] + 2048],
                              in_=x2h[c * 128:(c + 1) * 128, :]), 'sS', 16, ('x25', c))
        for hc in range(NHC):
            b.wait(s, ('gsl', hc))
            b.inc(s.dma_start(out=gth[hc * 128:(hc + 1) * 128, :],
                              in_=BV[:, GSL[hc % 2]:GSL[hc % 2] + 2048]),
                  'sS', 16, ('gst', hc))

    def p5_gpsimd(g):
        for hc in range(NHC):
            if hc >= 2:
                b.wait(g, ('u_mm', hc - 2, NTQ - 1))
            sl = (hc % 2) * 2
            b.inc(g.dma_start(out=wslot_r(sl), in_=wblk_ap(wg, DH, hc)), 'sG', 16, ('wg', hc))
            b.inc(g.dma_start(out=wslot_r(sl + 1), in_=wblk_ap(wu, DH, hc)), 'sG', 16, ('wu', hc))

    def p5_tensor(t):
        for hc in range(NHC):
            b.wait(t, ('wg', hc))
            b.wait(t, ('wu', hc))
            if hc == 0:
                b.wait(t, ('h2', NCH - 1))
            sl = (hc % 2) * 2
            for tq in range(NTQ):
                if hc >= 1:
                    b.wait(t, ('gu_ev', hc - 1, tq))
                for K in range(NCH):
                    ins = t.matmul(psA[:, tq * 512:(tq + 1) * 512],
                                   wslot(sl)[:, K * 128:(K + 1) * 128],
                                   BH[:, K * T + tq * 512:K * T + (tq + 1) * 512],
                                   start=(K == 0), stop=(K == NCH - 1))
                b.inc(ins, 'sT', 1, ('g_mm', hc, tq))
                for K in range(NCH):
                    ins = t.matmul(psB[:, tq * 512:(tq + 1) * 512],
                                   wslot(sl + 1)[:, K * 128:(K + 1) * 128],
                                   BH[:, K * T + tq * 512:K * T + (tq + 1) * 512],
                                   start=(K == 0), stop=(K == NCH - 1))
                b.inc(ins, 'sT', 1, ('u_mm', hc, tq))

    def p5_vector(v):
        b.wait(v, 'ln2_bc')
        for c in range(NCH):
            b.wait(v, ('x25', c))
            normalize(v, 'ln2', XS[c % 2], BH, c)
            b.mark('sV', ('h2', c))
        for hc in range(NHC):
            if hc >= 2:
                b.wait(v, ('gst', hc - 2))
            for tq in range(NTQ):
                b.wait(v, ('sil', hc, tq))
                b.wait(v, ('u_mm', hc, tq))
                b.inc(v.tensor_tensor(BV[:, GSL[hc % 2] + tq * 512:GSL[hc % 2] + (tq + 1) * 512],
                                      BV[:, SIL[tq % 2]:SIL[tq % 2] + 512],
                                      psB[:, tq * 512:(tq + 1) * 512], ALU.mult),
                      'sV', 1, ('gu_ev', hc, tq))
            b.mark('sV', ('gsl', hc))

    def p5_scalar(s):
        for hc in range(NHC):
            for tq in range(NTQ):
                b.wait(s, ('g_mm', hc, tq))
                prev = (hc, tq - 2) if tq >= 2 else ((hc - 1, tq + 2) if hc >= 1 else None)
                if prev is not None:
                    b.wait(s, ('gu_ev', prev[0], prev[1]))
                b.inc(s.activation(BV[:, SIL[tq % 2]:SIL[tq % 2] + 512],
                                   psA[:, tq * 512:(tq + 1) * 512], AF.Silu),
                      'sC', 1, ('sil', hc, tq))

    b.phase('P5', [('sync', p5_sync), ('gpsimd', p5_gpsimd), ('tensor', p5_tensor),
                   ('vector', p5_vector), ('scalar', p5_scalar)])

    if tap == 'gated':
        taps['g'] = nc.dram_tensor('tap_g', [DH, T], BF16, kind='ExternalOutput')
        taps['x2o'] = nc.dram_tensor('tap_x2o', [D, T], F32, kind='ExternalOutput')

        def ts2(s):
            for ec in range(NCH):
                b.inc(s.dma_start(out=taps['x2o'][ec * 128:(ec + 1) * 128, :],
                                  in_=x2h[ec * 128:(ec + 1) * 128, :]), 'sS', 16, ('x2o', ec))
            for hc in range(NHC):
                b.inc(s.dma_start(out=taps['g'][hc * 128:(hc + 1) * 128, :],
                                  in_=gth[hc * 128:(hc + 1) * 128, :]), 'sS', 16, ('gto', hc))
            b.wait(s, ('gto', NHC - 1))
        b.phase('TAP', [('sync', ts2)])
        es.close()
        return nc, taps

    XS6 = [0, 1024]
    PBf6 = [0, 1024]

    def p6_sync(s):
        for th in range(2):
            for half, buf in ((0, BH), (1, BQ)):
                in_ap = AP(gth, (half * 32 * 128) * T + th * 1024,
                           [[T, 128], [128 * T, 32], [1, 1024]])
                b.inc(s.dma_start(out=buf[:, 0:32 * 1024].rearrange('p (c n) -> p c n', c=32),
                                  in_=in_ap), 'sS', 16, ('gh', th, half))
            for dc in range(NCH):
                if dc >= 2:
                    b.wait(s, ('dnv', th, dc - 2, 1))
                b.inc(s.dma_start(out=BKf[:, XS6[dc % 2]:XS6[dc % 2] + 1024],
                                  in_=x2h[dc * 128:(dc + 1) * 128, th * 1024:(th + 1) * 1024]),
                      'sS', 16, ('x26', th, dc))
                b.wait(s, ('dnv', th, dc, 1))
                b.inc(s.dma_start(out=out[dc * 128:(dc + 1) * 128, th * 1024:(th + 1) * 1024],
                                  in_=BVf[:, PBf6[dc % 2]:PBf6[dc % 2] + 1024]),
                      'sS', 16, ('ost', th, dc))
        b.wait(s, ('ost', 1, NCH - 1))

    def p6_gpsimd(g):
        for th in range(2):
            for dc in range(NCH):
                idx = th * NCH + dc
                if idx >= 2:
                    pidx = idx - 2
                    b.wait(g, ('dnmm', pidx // NCH, pidx % NCH))
                for piece in range(4):
                    in_ap = AP(wd, dc * 128 + piece * 16 * 128 * D,
                               [[D, 128], [128 * D, 16], [1, 128]])
                    if dc % 2 == 0:
                        dst = SCRb[:, piece * 2048:(piece + 1) * 2048]
                    elif piece < 2:
                        dst = cosTb[:, piece * 2048:(piece + 1) * 2048]
                    else:
                        dst = sinTb[:, (piece - 2) * 2048:(piece - 1) * 2048]
                    b.inc(g.dma_start(out=dst.rearrange('p (c n) -> p c n', c=16),
                                      in_=in_ap), 'sG', 16,
                          ('wd', th, dc) if piece == 3 else None)

    def p6_tensor(t):
        for th in range(2):
            b.wait(t, ('gh', th, 0))
            b.wait(t, ('gh', th, 1))
            for dc in range(NCH):
                b.wait(t, ('wd', th, dc))
                for tq2 in range(2):
                    slot = (2 * dc + tq2) % 4
                    if 2 * dc + tq2 >= 4:
                        pr = 2 * dc + tq2 - 4
                        b.wait(t, ('dnv', th, pr // 2, pr % 2))
                    elif th == 1 and 2 * dc + tq2 < 4:
                        pr = 2 * NCH + 2 * dc + tq2 - 4
                        b.wait(t, ('dnv', 0, pr // 2, pr % 2))
                    for hc in range(NHC):
                        buf = BH if hc < 32 else BQ
                        if dc % 2 == 0:
                            wsl = SCRb[:, hc * 128:(hc + 1) * 128]
                        elif hc < 32:
                            wsl = cosTb[:, hc * 128:(hc + 1) * 128]
                        else:
                            wsl = sinTb[:, (hc - 32) * 128:(hc - 31) * 128]
                        ins = t.matmul(psA[:, slot * 512:(slot + 1) * 512], wsl,
                                       buf[:, (hc % 32) * 1024 + tq2 * 512:(hc % 32) * 1024 + (tq2 + 1) * 512],
                                       start=(hc == 0), stop=(hc == NHC - 1))
                    b.inc(ins, 'sT', 1, ('dn', th, dc, tq2))
                b.mark('sT', ('dnmm', th, dc))

    def p6_vector(v):
        for th in range(2):
            for dc in range(NCH):
                b.wait(v, ('x26', th, dc))
                for tq2 in range(2):
                    slot = (2 * dc + tq2) % 4
                    b.wait(v, ('dn', th, dc, tq2))
                    if dc >= 2:
                        b.wait(v, ('ost', th, dc - 2))
                    elif th == 1:
                        b.wait(v, ('ost', 0, NCH - 2 + dc))
                    b.inc(v.tensor_tensor(BVf[:, PBf6[dc % 2] + tq2 * 512:PBf6[dc % 2] + (tq2 + 1) * 512],
                                          psA[:, slot * 512:(slot + 1) * 512],
                                          BKf[:, XS6[dc % 2] + tq2 * 512:XS6[dc % 2] + (tq2 + 1) * 512],
                                          ALU.add), 'sV', 1, ('dnv', th, dc, tq2))

    b.phase('P6', [('sync', p6_sync), ('gpsimd', p6_gpsimd), ('tensor', p6_tensor),
                   ('vector', p6_vector)])
    es.close()
    return nc, {'out': out}



_NCACHE = {}


def _get_program(tap=None):
    if tap not in _NCACHE:
        _NCACHE[tap] = build_program(tap)
    return _NCACHE[tap]


def make_inputs(L0_attn_x, L0_attn_wq, L0_attn_wk, L0_attn_wv, L0_attn_wo,
                L0_ffn_w_gate, L0_ffn_w_up, L0_ffn_w_down):
    pq = _perm_q()
    pk = _perm_k()
    cos, sin = _rope_tables()
    wqT = np.ascontiguousarray(np.asarray(L0_attn_wq).T[:, pq], dtype=np.float32)
    wkT = np.ascontiguousarray(np.asarray(L0_attn_wk).T[:, pk], dtype=np.float32)
    wvT = np.ascontiguousarray(np.asarray(L0_attn_wv).T, dtype=np.float32)
    woT = np.ascontiguousarray(np.asarray(L0_attn_wo).T, dtype=np.float32)
    wgT = np.ascontiguousarray(np.asarray(L0_ffn_w_gate).T, dtype=np.float32)
    wuT = np.ascontiguousarray(np.asarray(L0_ffn_w_up).T, dtype=np.float32)
    wdT = np.ascontiguousarray(np.asarray(L0_ffn_w_down).T, dtype=np.float32)
    diag = np.where(np.arange(128)[None, :] >= np.arange(128)[:, None],
                    np.float32(1.0), np.float32(0.0)).astype(np.float32)
    shared = dict(wq=wqT, wk=wkT, wv=wvT, wo=woT, wg=wgT, wu=wuT, wd=wdT,
                  cos_in=cos, sin_in=sin,
                  ident_in=np.eye(128, dtype=np.float32),
                  diag_in=diag,
                  onesc_in=np.ones((128, 1), np.float32),
                  onesf_in=np.ones((128, 128), np.float32))
    x = np.asarray(L0_attn_x)
    in_maps = []
    for core in range(8):
        im = dict(shared)
        im['xf'] = np.ascontiguousarray(x[core].T, dtype=np.float32)
        in_maps.append(im)
    return in_maps


def kernel(**inputs):
    nc, outs = _get_program(None)
    in_maps = make_inputs(**inputs)
    res = run_bass_kernel_spmd(nc, in_maps, core_ids=list(range(8)))
    return np.stack([r['outf'].T for r in res.results], axis=0).astype(np.float32)
